# version 04
# baseline (speedup 1.0000x reference)
import numpy as np
import concourse.bass as bass
import concourse.mybir as mybir
from concourse.bass_utils import run_bass_kernel_spmd

F32 = mybir.dt.float32
BF16 = mybir.dt.bfloat16
AF = mybir.ActivationFunctionType
ALU = mybir.AluOpType
AX = mybir.AxisListType

D = 1024
S = 2048
NC = 8
NCORES = 8
HPAIRS = 8
FUSE = True

SCR_Q = 0
SCR_K = 2048
SCR_V = 4096
SCR_PROBS = 6144
SCR_MH2 = 0
SCR_SG = 6144
SCR_COLS = 6144 + 2 * 2 * 512

WPAN_SLOT = 2048
WPAN_NSLOT = 2


class Prog:
    ENGS = "TSVGQ"
    GD = ("G0", "G1", "G2", "G3")
    QD = ("Q0", "Q1")
    SEMS = ("T", "S", "V", "Gc", "Qc") + GD + QD

    def __init__(self):
        self.ops = {e: [] for e in self.ENGS}
        self.cnt = {s: 0 for s in self.SEMS}
        self.marks = {e: dict.fromkeys(self.SEMS, 0) for e in self.ENGS}
        self.pending = {e: [] for e in self.ENGS}
        self.dmai = {"G": 0, "Q": 0}

    def emit(self, eng, fn, deps=(), inc=False, dma=False, self_wait=False):
        deps = [d for d in deps if d is not None]
        if dma:
            assert inc and eng in ("G", "Q")
            ring = self.GD if eng == "G" else self.QD
            sem = ring[self.dmai[eng] % len(ring)]
            self.dmai[eng] += 1
            deps = deps + [(sem, self.cnt[sem])]
        else:
            sem = eng + "c" if eng in ("G", "Q") else eng
        want = {}
        for d in list(self.pending[eng]) + deps:
            if d is None:
                continue
            de, dc = d
            if dc <= 0:
                continue
            if dc > self.marks[eng][de]:
                self.marks[eng][de] = dc
                want[de] = max(want.get(de, 0), dc)
        self.pending[eng] = []
        waits = sorted(want.items())
        amt = 0
        if inc:
            amt = 16 if dma else 1
            self.cnt[sem] += amt
        self.ops[eng].append([fn, waits, amt, sem])
        if inc:
            return (sem, self.cnt[sem])
        return None

    def tok(self, sem):
        if sem == "G":
            sem = "Gc"
        if sem == "Q":
            ring = self.QD
            i = (self.dmai["Q"] - 1) % len(ring) if self.dmai["Q"] else 0
            return (ring[i], self.cnt[ring[i]])
        return (sem, self.cnt[sem])

    def barrier(self):
        for e in self.ENGS:
            if self.ops[e]:
                last = self.ops[e][-1]
                if last[2] == 0:
                    sem = last[3]
                    assert sem not in self.GD + self.QD
                    last[2] = 1
                    self.cnt[sem] += last[2]
        toks = [(s, self.cnt[s]) for s in self.SEMS]
        for e in self.ENGS:
            self.pending[e] = list(toks)


class Pool:

    def __init__(self, mk, nslots=WPAN_NSLOT):
        self.mk = mk
        self.n = nslots
        self.freetok = [None] * nslots
        self.i = 0

    def load(self, in_ap, nch, ncols):
        mk = self.mk
        s = self.i % self.n
        self.i += 1
        base = s * WPAN_SLOT
        assert nch * ncols <= WPAN_SLOT
        out = mk.sb["wpan"][:, base:base + nch * ncols].rearrange(
            "p (c n) -> p c n", c=nch)
        tok = mk.p.emit("G", lambda g, o=out, i=in_ap: g.dma_start(out=o, in_=i),
                        deps=[self.freetok[s]], inc=True, dma=True)
        return s, tok

    def tile(self, s, kc, ncols):
        base = s * WPAN_SLOT
        return self.mk.sb["wpan"][:, base + kc * ncols: base + (kc + 1) * ncols]

    def done(self, s, tok):
        self.freetok[s] = tok


class MK:
    def __init__(self, nc, prog, sb, pb, w, dump):
        self.nc = nc
        self.p = prog
        self.sb = sb
        self.pb = pb
        self.w = w
        self.dump = dump

    def make_sel(self, e):
        tprev = self.p.tok("T")
        g1 = self.G(lambda g: g.memset(self.sb["sel"][0:8, 0:128], 0.0),
                    deps=[tprev], inc=True)
        return self.p.emit("G", lambda g, o=self.sb["sel"][e:e + 1, 0:128],
                           i=self.w["onesrc"]: g.dma_start(out=o, in_=i),
                           deps=[g1], inc=True, dma=True, self_wait=True)

    def fm(self, slot, c, t0, n, h0=0, hp=128):
        return self.sb[slot][h0:h0 + hp, c * S + t0: c * S + t0 + n]

    def pprow(self, col, p0=0, np_=128):
        return self.sb["pp"][p0:p0 + np_, col:col + 1]

    def srow(self, p, c0, n):
        return self.sb["stats"][p:p + 1, c0:c0 + n]

    def T(self, fn, deps=(), inc=False):
        return self.p.emit("T", fn, deps, inc)

    def Sc(self, fn, deps=(), inc=False, sw=False):
        return self.p.emit("S", fn, deps, inc, False, sw)

    def Ve(self, fn, deps=(), inc=False, sw=False):
        return self.p.emit("V", fn, deps, inc, False, sw)

    def G(self, fn, deps=(), inc=False, dma=False):
        return self.p.emit("G", fn, deps, inc, dma)

    def Q(self, fn, deps=(), inc=False, dma=False, self_wait=False):
        return self.p.emit("Q", fn, deps, inc, dma, self_wait)

    def dump_slot(self, name, slot, dtype_cols=16384):
        if name not in self.dump:
            return
        out = self.dump[name]
        src = self.sb[slot][:, 0:dtype_cols]
        self.p.barrier()
        self.Q(lambda q, o=out, s=src: q.dma_start(out=o, in_=s), inc=True, dma=True)
        self.p.barrier()


    def layernorm(self, src_slot, dst_slot, kind, w_col=None, b_col=None,
                  eps_col=81, src_f32=None):
        mk = self
        pb = self.pb
        if src_f32 is not None:
            for c in range(NC):
                self.Ve(lambda v, o=self.fm(dst_slot, c, 0, S),
                        i=self.fm(src_f32, c, 0, S): v.tensor_copy(o, i))
            src_b = dst_slot
        else:
            src_b = src_slot
        self.p.barrier()
        need_mean = kind != "rms"
        sq_mm_tok = [None, None]
        for qc in range(4):
            vdep = self.p.tok("V")
            for c in range(NC):
                buf = c % 2
                sqv = self.sb["scr"][:, SCR_PROBS + buf * 512:SCR_PROBS + buf * 512 + 512]
                t = self.Sc(lambda s, o=sqv, i=self.fm(src_b, c, qc * 512, 512):
                            s.activation(o, i, AF.Square),
                            deps=[sq_mm_tok[buf]], inc=True)
                if need_mean:
                    self.T(lambda tt, o=pb[6][0:1, :], l=self.sb["onesb"][:, 0:1],
                           r=self.fm(src_b, c, qc * 512, 512), st=(c == 0), sp=(c == NC - 1):
                           tt.matmul(o, l, r, start=st, stop=sp),
                           deps=[vdep])
                self.T(lambda tt, o=pb[5][0:1, :], l=self.sb["onesb"][:, 0:1],
                       r=sqv, st=(c == 0), sp=(c == NC - 1):
                       tt.matmul(o, l, r, start=st, stop=sp),
                       deps=[t, vdep], inc=True)
                sq_mm_tok[buf] = self.p.tok("T")
            tk = self.p.tok("T")
            if need_mean:
                self.Ve(lambda v, o=self.srow(0, qc * 512, 512), i=pb[6][0:1, 0:512]:
                        v.tensor_copy(o, i), deps=[tk])
            self.Ve(lambda v, o=self.srow(32, qc * 512, 512), i=pb[5][0:1, 0:512]:
                    v.tensor_copy(o, i), deps=[tk], inc=True)
        self.p.barrier()
        a_row = lambda c0, n: self.srow(32, S + c0, n)
        b_row = lambda c0, n: self.srow(0, c0, n)
        if need_mean:
            k1 = self.Ve(lambda v, o=self.srow(0, 0, S), i=self.srow(0, 0, S):
                         v.tensor_scalar_mul(o, i, 1.0 / D), inc=True)
            k2 = self.Ve(lambda v, o=self.srow(0, S, S), a=self.srow(0, 0, S),
                         b=self.srow(0, 0, S): v.tensor_mul(o, a, b),
                         deps=[k1], inc=True, sw=True)
            k3 = self.Ve(lambda v, o=self.srow(32, S, S), i=self.srow(0, S, S):
                         v.tensor_copy(o, i), deps=[k2], inc=True, sw=True)
            self.Ve(lambda v, o=self.srow(32, 0, S), a=self.srow(32, 0, S),
                    b=self.srow(32, S, S):
                    v.scalar_tensor_tensor(o, a, 1.0 / D, b, ALU.mult, ALU.subtract),
                    deps=[k3], inc=True, sw=True)
        else:
            self.Ve(lambda v, o=self.srow(32, 0, S), i=self.srow(32, 0, S):
                    v.tensor_scalar_mul(o, i, 1.0 / D), inc=True)
        sv = self.p.tok("V")
        l1 = self.Sc(lambda s, o=self.srow(32, S, S), i=self.srow(32, 0, S),
                     e=self.sb["pp"][32:33, eps_col:eps_col + 1]:
                     s.activation(o, i, AF.Ln, bias=e), deps=[sv], inc=True)
        self.Sc(lambda s, o=self.srow(32, S, S), i=self.srow(32, S, S):
                s.activation(o, i, AF.Exp, scale=-0.5), deps=[l1], inc=True, sw=True)
        ss = self.p.tok("S")
        if need_mean:
            k4 = self.Ve(lambda v, o=self.srow(0, S, S), i=self.srow(32, S, S):
                         v.tensor_copy(o, i), deps=[ss], inc=True)
            self.Ve(lambda v, o=self.srow(0, 0, S), a=self.srow(0, 0, S),
                    b=self.srow(0, S, S):
                    v.scalar_tensor_tensor(o, a, -1.0, b, ALU.mult, ALU.mult),
                    deps=[k4], inc=True, sw=True)
        self.p.barrier()
        uprev = None
        for qc in range(4):
            vdep = self.p.tok("V")
            ta = self.T(lambda t, o=pb[7][:, 0:512], l=self.sb["onesf"][32:33, :],
                        r=a_row(qc * 512, 512): t.matmul(o, l, r, start=True, stop=True),
                        deps=[vdep], inc=True)
            tb = None
            if need_mean:
                tb = self.T(lambda t, o=pb[4][:, 0:512], l=self.sb["onesf"][0:1, :],
                            r=b_row(qc * 512, 512): t.matmul(o, l, r, start=True, stop=True),
                            deps=[vdep], inc=True)
            for c in range(NC):
                src = (self.fm(src_f32, c, qc * 512, 512) if src_f32 is not None
                       else self.fm(src_b, c, qc * 512, 512))
                if kind == "plain":
                    u1 = self.Ve(lambda v, o=self.sb["tmpf"][:, 0:512], a=src, b=pb[7][:, 0:512]:
                                 v.tensor_mul(o, a, b), deps=[ta, uprev], inc=True)
                    uprev = self.Ve(lambda v, o=self.fm(dst_slot, c, qc * 512, 512),
                                    a=self.sb["tmpf"][:, 0:512], b=pb[4][:, 0:512]:
                                    v.tensor_add(o, a, b), deps=[tb, u1], inc=True)
                elif kind == "affine":
                    u1 = self.Ve(lambda v, o=self.sb["tmpf"][:, 0:512], a=src, b=pb[7][:, 0:512]:
                                 v.tensor_mul(o, a, b), deps=[ta, uprev], inc=True)
                    u2 = self.Ve(lambda v, o=self.sb["tmpf"][:, 0:512],
                                 a=self.sb["tmpf"][:, 0:512], b=pb[4][:, 0:512]:
                                 v.tensor_add(o, a, b), deps=[tb, u1], inc=True)
                    uprev = self.Ve(lambda v, o=self.fm(dst_slot, c, qc * 512, 512),
                                    a=self.sb["tmpf"][:, 0:512], wc=self.pprow(w_col + c),
                                    bc=self.pprow(b_col + c):
                                    v.tensor_scalar(o, a, wc, bc, ALU.mult, ALU.add),
                                    deps=[u2], inc=True)
                else:
                    u1 = self.Ve(lambda v, o=self.sb["tmpf"][:, 0:512], a=src, b=pb[7][:, 0:512]:
                                 v.tensor_mul(o, a, b), deps=[ta, uprev], inc=True)
                    uprev = self.Ve(lambda v, o=self.fm(dst_slot, c, qc * 512, 512),
                                    a=self.sb["tmpf"][:, 0:512], wc=self.pprow(w_col + c):
                                    v.tensor_scalar_mul(o, a, wc), deps=[u1], inc=True)
        self.p.barrier()

    def attention(self, inp_slot, mkq, mkk, mkv, mkwo, scale, causal=False,
                  pos=False, bq_col=None, bk_col=None, vbias_row=None,
                  bo_col=None, out_mode="acc", out_slot=None, res_slot=None):
        mk = self
        pb = self.pb
        pool = Pool(self)
        self.sc_free = [None, None, None]
        self.probs_free = [None, None]
        self.pv_free = [None, None]
        self.sum_free = [None, None]
        self._pend_epi = None
        for pr in range(HPAIRS):
            sq, tq = pool.load(mkq(pr), NC, 128)
            sk, tk_ = pool.load(mkk(pr), NC, 128)
            sdep = self.p.tok("S")
            for which, slot_off, wsl, wtok, bcol in (
                    (0, SCR_Q, sq, tq, bq_col), (1, SCR_K, sk, tk_, bk_col)):
                for qc in range(4):
                    bank = pb[which * 2 + (qc % 2)]
                    dep_prev = self.p.tok("V")
                    for kc in range(NC):
                        self.T(lambda t, o=bank[:, 0:512],
                               l=pool.tile(wsl, kc, 128),
                               r=self.fm(inp_slot, kc, qc * 512, 512), st=(kc == 0), sp=(kc == NC - 1):
                               t.matmul(o, l, r, start=st, stop=sp),
                               deps=[wtok, dep_prev, sdep], inc=(kc == NC - 1))
                    ttok = self.p.tok("T")
                    dst = self.sb["scr"][:, slot_off + qc * 512: slot_off + qc * 512 + 512]
                    if pos:
                        self.Ve(lambda v, o=dst, a=bank[:, 0:512],
                                b=self.pos2v[:, qc * 512:qc * 512 + 512]:
                                v.tensor_add(o, a, b), deps=[ttok], inc=True)
                    elif bcol is not None:
                        self.Ve(lambda v, o=dst, a=bank[:, 0:512],
                                bc=self.pprow(bcol + pr):
                                v.tensor_scalar_add(o, a, bc), deps=[ttok], inc=True)
                    else:
                        self.Ve(lambda v, o=dst, a=bank[:, 0:512]:
                                v.tensor_copy(o, a), deps=[ttok], inc=True)
            pool.done(sq, self.p.tok("T"))
            pool.done(sk, self.p.tok("T"))
            sv, tv = pool.load(mkv(pr), NC, 128)
            for tt in range(16):
                bank = pb[4 + (tt % 2)]
                vdep = self.p.tok("V")
                last_plain = (vbias_row is None)
                for kc in range(NC):
                    self.T(lambda t, o=bank[:, 0:128],
                           l=self.fm(inp_slot, kc, tt * 128, 128),
                           r=pool.tile(sv, kc, 128), st=(kc == 0),
                           sp=(last_plain and kc == NC - 1):
                           t.matmul(o, l, r, start=st, stop=sp),
                           deps=[tv, vdep, self.pv_free[tt % 2]],
                           inc=(last_plain and kc == NC - 1))
                if vbias_row is not None:
                    self.T(lambda t, o=bank[:, 0:128], l=self.sb["onesf"][0:1, :],
                           r=vbias_row(pr): t.matmul(o, l, r, start=False, stop=True),
                           inc=True)
                ttok = self.p.tok("T")
                self.Ve(lambda v, o=self.sb["scr"][:, SCR_V + tt * 128: SCR_V + tt * 128 + 128],
                        a=bank[:, 0:128]: v.tensor_copy(o, a), deps=[ttok], inc=True)
                self.pv_free[tt % 2] = self.p.tok("V")
            pool.done(sv, self.p.tok("T"))
            if pr == 0 and "qkv0" in self.dump:
                self.p.barrier()
                self.Q(lambda q, o=self.dump["qkv0"], s=self.sb["scr"][:, 0:8192]:
                       q.dma_start(out=o, in_=s), inc=True, dma=True)
                self.p.barrier()
            sqreg = self.sb["scr"][:, SCR_PROBS:SCR_PROBS + 2048]
            vprev = self.p.tok("V")
            cells = self.sb["cells"]
            for which, slot_off, ccol in ((0, SCR_Q, 0), (1, SCR_K, 1)):
                tprev = self.p.tok("T")
                sqt = self.Sc(lambda s, o=sqreg,
                              i=self.sb["scr"][:, slot_off:slot_off + S]:
                              s.activation(o, i, AF.Square),
                              deps=[vprev, tprev, self.probs_free[0], self.probs_free[1], sdep],
                              inc=True)
                vdone = self.p.tok("V")
                for qc in range(4):
                    self.T(lambda t, o=pb[qc][0:1, 0:512], l=self.sb["onesb"][:, 0:1],
                           r=sqreg[:, qc * 512:qc * 512 + 512]:
                           t.matmul(o, l, r, start=True, stop=True),
                           deps=[sqt, vdone, self.sc_free[0], self.sc_free[1], self.sc_free[2],
                                 self.sum_free[1]], inc=True)
                tmm = self.p.tok("T")
                mtk = None
                for qc in range(4):
                    rt = self.Ve(lambda v, o=cells[0:1, 8 + qc:9 + qc], i=pb[qc][0:1, 0:512]:
                                 v.reduce_max(o, i, AX.X), deps=[tmm, mtk],
                                 inc=True, sw=True)
                    mtk = rt
                m1t = self.Ve(lambda v, o=cells[0:1, ccol:ccol + 1], a=cells[0:1, 8:9],
                              b=cells[0:1, 9:10]: v.tensor_max(o, a, b),
                              deps=[mtk], inc=True, sw=True)
                m2t = self.Ve(lambda v, o=cells[0:1, 12:13], a=cells[0:1, 10:11],
                              b=cells[0:1, 11:12]: v.tensor_max(o, a, b),
                              deps=[m1t], inc=True, sw=True)
                self.Ve(lambda v, o=cells[0:1, ccol:ccol + 1], a=cells[0:1, ccol:ccol + 1],
                        b=cells[0:1, 12:13]: v.tensor_max(o, a, b),
                        deps=[m2t], inc=True, sw=True)
            s1 = self.Ve(lambda v, o=cells[0:1, 4:5], a=cells[0:1, 0:1],
                         b=cells[0:1, 1:2]: v.tensor_add(o, a, b),
                         deps=[self.p.tok("V")], inc=True, sw=True)
            self.Ve(lambda v, o=self.sb["lrow"][0:1, 0:1], i=cells[0:1, 4:5]:
                    v.tensor_scalar_mul(o, i, -0.5 * scale), deps=[s1], inc=True, sw=True)
            vb = self.p.tok("V")
            tb = self.T(lambda t, o=pb[6][:, 256:257], l=self.sb["onesb"][0:1, :],
                        r=self.sb["lrow"][0:1, 0:1]:
                        t.matmul(o, l, r, start=True, stop=True),
                        deps=[vb, self.sum_free[0]], inc=True)
            self.Ve(lambda v, o=self.sb["negB"][:, 0:1], i=pb[6][:, 256:257]:
                    v.tensor_copy(o, i), deps=[tb, sdep], inc=True)
            self.probs_free[0] = self.p.tok("T")
            self.probs_free[1] = self.p.tok("T")
            negB_tok = self.p.tok("V")
            if "bnd" in self.dump:
                self.p.barrier()
                self.Q(lambda q, o=self.dump["bnd"][:, pr * 4:pr * 4 + 1],
                       s=self.sb["negB"][:, 0:1]:
                       q.dma_start(out=o, in_=s, allow_slow_non_contiguous=True),
                       inc=True, dma=True)
                self.Q(lambda q, o=self.dump["bnd"][0:1, 32 + pr * 8:32 + pr * 8 + 5],
                       s=self.sb["cells"][0:1, 0:5]:
                       q.dma_start(out=o, in_=s, allow_slow_non_contiguous=True),
                       inc=True, dma=True)
                self.p.barrier()
            for hh, h0 in ((0, 0), (1, 64)):
                for qb in range(4):
                    pvb_i = qb % 2
                    pvbank = pb[4 + pvb_i]
                    par = (4 * hh + qb) % 2
                    sumbank = pb[6] if par == 0 else pb[3]
                    kmax = (4 * qb + 4) if causal else 16
                    kcs = list(range(kmax))
                    strips = [kcs[i:i + 2] for i in range(0, kmax, 2)]
                    nst = len(strips)
                    exp_toks = {}

                    def emit_se(st_i, h0=h0, qb=qb):
                        strip = strips[st_i]
                        buf = st_i % 2
                        pbase = SCR_PROBS + buf * 1024
                        toks = []
                        for j, kc in enumerate(strip):
                            scb = pb[kc % 3]
                            self.T(lambda t, o=scb[:, 0:512],
                                   l=self.sb["scr"][h0:h0 + 64, SCR_K + kc * 128:SCR_K + kc * 128 + 128],
                                   r=self.sb["scr"][h0:h0 + 64, SCR_Q + qb * 512:SCR_Q + qb * 512 + 512]:
                                   t.matmul(o, l, r, start=True, stop=True),
                                   deps=[negB_tok, self.sc_free[kc % 3]], inc=True)
                            sctok = self.p.tok("T")
                            ptile = self.sb["scr"][:, pbase + j * 512: pbase + j * 512 + 512]
                            diag = causal and (kc >= 4 * qb)
                            et = self.Sc(lambda s, o=ptile, i=scb[:, 0:512],
                                         b=self.sb["negB"][:, 0:1], sc=scale:
                                         s.activation(o, i, AF.Exp, bias=b, scale=sc),
                                         deps=[sctok, self.probs_free[buf]],
                                         inc=True)
                            self.sc_free[kc % 3] = self.p.tok("S")
                            if diag:
                                gt = self.G(lambda g, o=ptile, base=512 * qb - 128 * kc:
                                            g.affine_select(o, o, [[1, 512]], ALU.is_ge,
                                                            0.0, base=base,
                                                            channel_multiplier=-1),
                                            deps=[et], inc=True)
                                toks.append(gt)
                            else:
                                toks.append(et)
                        exp_toks[st_i] = toks

                    def emit_cspv(st_i, h0=h0, qb=qb, pvbank=pvbank, sumbank=sumbank,
                                  par=par, pvb_i=pvb_i, nst_=None):
                        strip = strips[st_i]
                        buf = st_i % 2
                        pbase = SCR_PROBS + buf * 1024
                        dep_all = list(exp_toks[st_i])
                        for j, kc in enumerate(strip):
                            ptile = self.sb["scr"][:, pbase + j * 512: pbase + j * 512 + 512]
                            first = (st_i == 0 and j == 0)
                            last = (st_i == nst - 1 and j == len(strip) - 1)
                            self.T(lambda t, o=sumbank[0:1, 0:512], l=self.sb["onesb"][:, 0:1],
                                   r=ptile, st=first, sp=last:
                                   t.matmul(o, l, r, start=st, stop=sp),
                                   deps=dep_all + [self.sum_free[par]])
                            self.T(lambda t, o=pvbank[h0:h0 + 64, 0:512],
                                   l=self.sb["scr"][:, SCR_V + kc * 128 + h0: SCR_V + kc * 128 + h0 + 64],
                                   r=ptile, st=first, sp=last, tp=(0, h0):
                                   t.matmul(o, l, r, start=st, stop=sp, tile_position=tp),
                                   deps=[self.pv_free[pvb_i]], inc=True)
                            dep_all = []
                        self.probs_free[buf] = self.p.tok("T")

                    emit_se(0)
                    if self._pend_epi is not None:
                        self._pend_epi()
                        self._pend_epi = None
                    for st_i in range(nst):
                        if st_i + 1 < nst:
                            emit_se(st_i + 1)
                        emit_cspv(st_i)
                    tk = self.p.tok("T")

                    def epilogue(tk=tk, sumbank=sumbank, h0=h0, qb=qb, pr=pr,
                                 par=par, pvbank=pvbank, pvb_i=pvb_i):
                        rc = self.Ve(lambda v, o=self.srow(0, 0, 512), i=sumbank[0:1, 0:512]:
                                     v.reciprocal(o, i), deps=[tk], inc=True)
                        self.Ve(lambda v, o=self.sb["lrow"][0:1, 64:576], i=self.srow(0, 0, 512):
                                v.tensor_copy(o, i), deps=[rc], inc=True)
                        rv = self.p.tok("V")
                        tbc = self.T(lambda t, o=sumbank[h0:h0 + 64, 0:512],
                                     l=self.sb["onesb"][0:1, 0:64],
                                     r=self.sb["lrow"][0:1, 64:576], tp=(0, h0):
                                     t.matmul(o, l, r, start=True, stop=True,
                                              tile_position=tp), deps=[rv], inc=True)
                        rb = self.Ve(lambda v, o=self.sb["tmpf"][h0:h0 + 64, 0:512],
                                     i=sumbank[h0:h0 + 64, 0:512]:
                                     v.tensor_copy(o, i), deps=[tbc], inc=True)
                        self.sum_free[par] = self.p.tok("V")
                        self.Ve(lambda v, o=self.fm("ctx", pr, qb * 512, 512, h0, 64),
                                a=pvbank[h0:h0 + 64, 0:512],
                                b=self.sb["tmpf"][h0:h0 + 64, 0:512]:
                                v.tensor_mul(o, a, b), deps=[rb], inc=True)
                        self.pv_free[pvb_i] = self.p.tok("V")

                    self._pend_epi = epilogue
            if self._pend_epi is not None:
                self._pend_epi()
                self._pend_epi = None
        self.dump_slot("ctx", "ctx")
        self.dump_slot("accpre", "x_acc")
        self.p.barrier()
        pool = Pool(self)
        for mc in range(NC):
            sw, tw = pool.load(mkwo(mc), NC, 128)
            for qc in range(4):
                bank = pb[(qc % 2)]
                vdep = self.p.tok("V")
                for kc in range(NC):
                    self.T(lambda t, o=bank[:, 0:512], l=pool.tile(sw, kc, 128),
                           r=self.fm("ctx", kc, qc * 512, 512), st=(kc == 0), sp=(kc == NC - 1):
                           t.matmul(o, l, r, start=st, stop=sp),
                           deps=[tw, vdep], inc=(kc == NC - 1))
                ttok = self.p.tok("T")
                xa = self.fm("x_acc", mc, qc * 512, 512)
                if out_mode == "acc":
                    if bo_col is not None:
                        self.Ve(lambda v, o=xa, a=bank[:, 0:512], bc=self.pprow(bo_col + mc), x=xa:
                                v.scalar_tensor_tensor(o, a, bc, x, ALU.add, ALU.add),
                                deps=[ttok], inc=True)
                    else:
                        self.Ve(lambda v, o=xa, a=bank[:, 0:512], x=xa:
                                v.tensor_add(o, a, x), deps=[ttok], inc=True)
                else:
                    oslot = self.fm(out_slot, mc, qc * 512, 512)
                    r1 = self.Ve(lambda v, o=oslot, a=bank[:, 0:512],
                                 b=self.fm(res_slot, mc, qc * 512, 512):
                                 v.tensor_add(o, a, b), deps=[ttok], inc=True)
                    self.Ve(lambda v, o=xa, a=oslot, x=xa:
                            v.tensor_add(o, a, x), deps=[r1], inc=True)
            pool.done(sw, self.p.tok("T"))
        self.p.barrier()

    def moe_l0(self, h2_slot, sub=99):
        mk = self
        pb = self.pb
        pool = Pool(self)
        sgw, tgw = pool.load(self.w["L0_ffn_gate_w"].rearrange("(c p) n -> p c n", p=128),
                             NC, 8)
        for qc in range(4):
            vdep = self.p.tok("V")
            for kc in range(NC):
                self.T(lambda t, o=pb[6][0:8, 0:512], l=pool.tile(sgw, kc, 8),
                       r=self.fm(h2_slot, kc, qc * 512, 512), st=(kc == 0), sp=(kc == NC - 1):
                       t.matmul(o, l, r, start=st, stop=sp),
                       deps=[tgw, vdep], inc=(kc == NC - 1))
            tk = self.p.tok("T")
            self.Ve(lambda v, o=self.sb["lrow"][0:8, qc * 512:qc * 512 + 512],
                    a=pb[6][0:8, 0:512], bc=self.pprow(80, 0, 8):
                    v.tensor_scalar_add(o, a, bc), deps=[tk], inc=True)
        self.p.barrier()
        if sub < 1:
            return
        for tt in range(16):
            t1 = self.T(lambda t, o=self.pb_bf7[:, 0:8],
                        i=self.sb["lrow"][0:8, tt * 128:(tt + 1) * 128],
                        d=self.sb["idbf"][0:8, 0:8]: t.transpose(o, i, d),
                        deps=[self.p.tok("V")], inc=True)
            a1 = self.Ve(lambda v, o=self.sb["tmpf"][:, 0:8], a=self.pb_bf7[:, 0:8],
                         b=self.sb["pp"][:, 83:91]: v.tensor_add(o, a, b), deps=[t1],
                         inc=True)
            a2 = self.Ve(lambda v, o=self.sb["cells"][:, 5:6], i=self.sb["tmpf"][:, 0:8]:
                         v.reduce_max(o, i, AX.X), deps=[a1], inc=True, sw=True)
            self.Ve(lambda v, o=self.sb["mtok"][:, 0:8], i=self.sb["tmpf"][:, 0:8],
                    sc=self.sb["cells"][:, 5:6]:
                    v.tensor_single_scalar(o, i, sc, ALU.is_equal), deps=[a2],
                    inc=True, sw=True)
            v1 = self.p.tok("V")
            t2 = self.T(lambda t, o=self.pb_bf7[0:8, 128:256], i=self.sb["mtok"][:, 0:8],
                        d=self.sb["idbf"][:, :]: t.transpose(o, i, d), deps=[v1], inc=True)
            self.Ve(lambda v, o=self.sb["lrow"][0:8, tt * 128:(tt + 1) * 128],
                    i=self.pb_bf7[0:8, 128:256]: v.tensor_copy(o, i), deps=[t2], inc=True)
        self.p.barrier()
        if sub < 2:
            return
        for e in range(min(sub - 1, 8)):
            sel_tok = self.make_sel(e)
            w1r = self.w["L0_ffn_w1"].rearrange("e (c p) n -> e p c n", p=128)
            w2r = self.w["L0_ffn_w2"].rearrange("e (c p) n -> e p c n", p=128)
            for tb in range(2):
                heb_free = self.p.tok("T")
                for colh in range(2):
                    col = tb * 1024 + colh * 512
                    tbc = self.T(lambda t, o=pb[7][:, 0:512],
                                 l=self.sb["sel"][0:8, 0:128],
                                 r=self.sb["lrow"][0:8, col:col + 512]:
                                 t.matmul(o, l, r, start=True, stop=True),
                                 deps=[self.p.tok("V"), sel_tok], inc=True)
                    for kc in range(NC):
                        self.Ve(lambda v,
                                o=self.sb["scr"][:, SCR_MH2 + kc * 1024 + colh * 512:
                                                 SCR_MH2 + kc * 1024 + colh * 512 + 512],
                                a=self.fm(h2_slot, kc, col, 512), b=pb[7][:, 0:512]:
                                v.tensor_mul(o, a, b), deps=[tbc],
                                inc=(kc == NC - 1))
                mh2_tok = self.p.tok("V")
                if sub == 2:
                    continue
                for hp in range(16):
                    sw, tw = pool.load(w1r[e, :, :, hp * 256:(hp + 1) * 256], NC, 256)
                    for hsub in range(2):
                        hid = hp * 2 + hsub
                        for colh in range(2):
                            bank = pb[(2 * hsub + colh) % 3]
                            gel_dep = self.p.tok("S")
                            for kc in range(NC):
                                self.T(lambda t, o=bank[:, 0:512],
                                       l=pool.tile(sw, kc, 256)[:, hsub * 128:(hsub + 1) * 128],
                                       r=self.sb["scr"][:, SCR_MH2 + kc * 1024 + colh * 512:
                                                        SCR_MH2 + kc * 1024 + colh * 512 + 512],
                                       st=(kc == 0), sp=(kc == NC - 1):
                                       t.matmul(o, l, r, start=st, stop=sp),
                                       deps=[tw, mh2_tok, gel_dep], inc=(kc == NC - 1))
                            tk = self.p.tok("T")
                            hb = self.heb_view(hid, colh)
                            self.Sc(lambda s, o=hb, i=bank[:, 0:512]:
                                    s.activation(o, i, AF.Gelu_apprx_tanh),
                                    deps=[tk, heb_free], inc=True)
                    pool.done(sw, self.p.tok("T"))
                heb_tok = self.p.tok("S")
                if sub == 3:
                    continue
                for dout in range(NC):
                    for hh in range(2):
                        sw, tw = pool.load(
                            w2r[e, :, hh * 16:(hh + 1) * 16, dout * 128:(dout + 1) * 128],
                            16, 128)
                        for colh in range(2):
                            bank = pb[4 + colh]
                            vdep = self.p.tok("V")
                            for hc in range(16):
                                hid = hh * 16 + hc
                                self.T(lambda t, o=bank[:, 0:512],
                                       l=pool.tile(sw, hc, 128),
                                       r=self.heb_view(hid, colh), st=(hc == 0 and hh == 0), sp=(hc == 15 and hh == 1):
                                       t.matmul(o, l, r, start=st, stop=sp),
                                       deps=[tw, heb_tok, vdep],
                                       inc=(hc == 15))
                        pool.done(sw, self.p.tok("T"))
                    for colh in range(2):
                        bank = pb[4 + colh]
                        tk = self.p.tok("T")
                        xa = self.fm("x_acc", dout, tb * 1024 + colh * 512, 512)
                        self.Ve(lambda v, o=xa, a=bank[:, 0:512], x=xa:
                                v.tensor_add(o, a, x), deps=[tk], inc=True)
        self.p.barrier()

    def heb_view(self, hid, colh):
        slot = "ctx" if hid < 16 else "xn"
        base = (hid % 16) * 1024 + colh * 512
        return self.sb[slot][:, base:base + 512]

    def moe_l1(self, h2_slot):
        mk = self
        pb = self.pb
        pool = Pool(self)
        srw, trw = pool.load(self.w["L1_ffn_router_w"].rearrange("(c p) n -> p c n", p=128),
                             NC, 8)
        for qc in range(4):
            vdep = self.p.tok("V")
            for kc in range(NC):
                self.T(lambda t, o=pb[6][0:8, 0:512], l=pool.tile(srw, kc, 8),
                       r=self.fm(h2_slot, kc, qc * 512, 512), st=(kc == 0), sp=(kc == NC - 1):
                       t.matmul(o, l, r, start=st, stop=sp),
                       deps=[trw, vdep], inc=(kc == NC - 1))
            tk = self.p.tok("T")
            self.Ve(lambda v, o=self.sb["lrow"][0:8, qc * 512:qc * 512 + 512],
                    a=pb[6][0:8, 0:512]: v.tensor_copy(o, a), deps=[tk], inc=True)
        self.p.barrier()
        cells = self.sb["cells"]
        for tt in range(16):
            t1 = self.T(lambda t, o=self.pb_bf7[:, 0:8],
                        i=self.sb["lrow"][0:8, tt * 128:(tt + 1) * 128],
                        d=self.sb["idbf"][0:8, 0:8]: t.transpose(o, i, d),
                        deps=[self.p.tok("V")], inc=True)
            lt = self.sb["tmpf"][:, 0:8]
            c1 = self.Ve(lambda v, o=lt, i=self.pb_bf7[:, 0:8]: v.tensor_copy(o, i),
                         deps=[t1], inc=True)
            c2 = self.Ve(lambda v, o=cells[:, 6:7], i=lt:
                         v.reduce_max(o, i, AX.X, negate=True), deps=[c1], inc=True, sw=True)
            c3 = self.Sc(lambda s, o=self.sb["tmpf"][:, 8:16], i=lt, b=cells[:, 6:7]:
                         s.activation(o, i, AF.Exp, bias=b), deps=[c2], inc=True)
            et = self.sb["tmpf"][:, 8:16]
            c4 = self.Ve(lambda v, o=cells[:, 7:8], i=et: v.reduce_sum(o, i, AX.X),
                         deps=[c3], inc=True)
            c5 = self.Ve(lambda v, o=cells[:, 8:9], i=cells[:, 7:8]: v.reciprocal(o, i),
                         deps=[c4], inc=True, sw=True)
            probs = self.sb["tmpf"][:, 16:24]
            c6 = self.Ve(lambda v, o=probs, i=et, sc=cells[:, 8:9]:
                         v.tensor_single_scalar(o, i, sc, ALU.mult), deps=[c5],
                         inc=True, sw=True)
            ltb = self.sb["tmpf"][:, 24:32]
            c7 = self.Ve(lambda v, o=ltb, a=probs, b=self.sb["pp"][:, 83:91]:
                         v.tensor_add(o, a, b), deps=[c6], inc=True, sw=True)
            c8 = self.Ve(lambda v, o=cells[:, 9:10], i=ltb: v.reduce_max(o, i, AX.X),
                         deps=[c7], inc=True, sw=True)
            m1 = self.sb["tmpf"][:, 32:40]
            c9 = self.Ve(lambda v, o=m1, i=ltb, sc=cells[:, 9:10]:
                         v.tensor_single_scalar(o, i, sc, ALU.is_equal), deps=[c8],
                         inc=True, sw=True)
            ltb2 = self.sb["tmpf"][:, 40:48]
            c10 = self.Ve(lambda v, o=ltb2, a=m1, b=ltb:
                          v.scalar_tensor_tensor(o, a, -4.0, b, ALU.mult, ALU.add),
                          deps=[c9], inc=True, sw=True)
            c11 = self.Ve(lambda v, o=cells[:, 10:11], i=ltb2: v.reduce_max(o, i, AX.X),
                          deps=[c10], inc=True, sw=True)
            m2 = self.sb["tmpf"][:, 48:56]
            c12 = self.Ve(lambda v, o=m2, i=ltb2, sc=cells[:, 10:11]:
                          v.tensor_single_scalar(o, i, sc, ALU.is_equal), deps=[c11],
                          inc=True, sw=True)
            tmp8 = self.sb["tmpf"][:, 56:64]
            c13 = self.Ve(lambda v, o=tmp8, a=probs, b=m1: v.tensor_mul(o, a, b),
                          deps=[c12], inc=True, sw=True)
            c14 = self.Ve(lambda v, o=cells[:, 11:12], i=tmp8: v.reduce_sum(o, i, AX.X),
                          deps=[c13], inc=True, sw=True)
            c15 = self.Ve(lambda v, o=tmp8, a=probs, b=m2: v.tensor_mul(o, a, b),
                          deps=[c14], inc=True, sw=True)
            c16 = self.Ve(lambda v, o=cells[:, 12:13], i=tmp8: v.reduce_sum(o, i, AX.X),
                          deps=[c15], inc=True, sw=True)
            c17 = self.Ve(lambda v, o=cells[:, 13:14], a=cells[:, 11:12], b=cells[:, 12:13]:
                          v.tensor_sub(o, a, b), deps=[c16], inc=True, sw=True)
            c18 = self.Sc(lambda s, o=cells[:, 14:15], i=cells[:, 13:14]:
                          s.activation(o, i, AF.Sigmoid), deps=[c17], inc=True)
            c19 = self.Ve(lambda v, o=cells[:, 15:16], i=cells[:, 14:15]:
                          v.tensor_scalar(o, i, -1.0, 1.0, ALU.mult, ALU.add),
                          deps=[c18], inc=True)
            cm = self.sb["mtok"][:, 0:8]
            c20 = self.Ve(lambda v, o=cm, i=m1, sc=cells[:, 14:15]:
                          v.tensor_single_scalar(o, i, sc, ALU.mult), deps=[c19],
                          inc=True, sw=True)
            self.Ve(lambda v, o=cm, a=m2, sc=cells[:, 15:16], b=cm:
                    v.scalar_tensor_tensor(o, a, sc, b, ALU.mult, ALU.add),
                    deps=[c20], inc=True, sw=True)
            v1 = self.p.tok("V")
            t2 = self.T(lambda t, o=self.pb_bf7[0:8, 128:256], i=cm,
                        d=self.sb["idbf"][:, :]: t.transpose(o, i, d), deps=[v1], inc=True)
            self.Ve(lambda v, o=self.sb["lrow"][0:8, tt * 128:(tt + 1) * 128],
                    i=self.pb_bf7[0:8, 128:256]: v.tensor_copy(o, i), deps=[t2], inc=True)
        self.p.barrier()
        egr = self.w["L1_ffn_e_gate"].rearrange("e (c p) n -> e p c n", p=128)
        eur = self.w["L1_ffn_e_up"].rearrange("e (c p) n -> e p c n", p=128)
        edr = self.w["L1_ffn_e_down"].rearrange("e (c p) n -> e p c n", p=128)
        shg = self.w["L1_ffn_sh_gate"].rearrange("(c p) n -> p c n", p=128)
        shu = self.w["L1_ffn_sh_up"].rearrange("(c p) n -> p c n", p=128)
        shd = self.w["L1_ffn_sh_down"].rearrange("(c p) n -> p c n", p=128)
        for e in range(9):
            sel_tok = self.make_sel(e) if e < 8 else None
            for tbh in range(2):
                if e < 8:
                    for colh in range(2):
                        col = tbh * 1024 + colh * 512
                        tbc = self.T(lambda t, o=pb[7][:, 0:512],
                                     l=self.sb["sel"][0:8, 0:128],
                                     r=self.sb["lrow"][0:8, col:col + 512]:
                                     t.matmul(o, l, r, start=True, stop=True),
                                     deps=[self.p.tok("V"), sel_tok], inc=True)
                        for kc in range(NC):
                            self.Ve(lambda v,
                                    o=self.sb["xn"][:, kc * 1024 + colh * 512:
                                                    kc * 1024 + colh * 512 + 512],
                                    a=self.fm(h2_slot, kc, col, 512), b=pb[7][:, 0:512]:
                                    v.tensor_mul(o, a, b), deps=[tbc],
                                    inc=(kc == NC - 1))
                mup_tok = self.p.tok("V")
                for ffc in range(16):
                    if e < 8:
                        sg_, tg_ = pool.load(egr[e, :, :, ffc * 128:(ffc + 1) * 128], NC, 128)
                        su_, tu_ = pool.load(eur[e, :, :, ffc * 128:(ffc + 1) * 128], NC, 128)
                    else:
                        sg_, tg_ = pool.load(shg[:, :, ffc * 128:(ffc + 1) * 128], NC, 128)
                        su_, tu_ = pool.load(shu[:, :, ffc * 128:(ffc + 1) * 128], NC, 128)
                    for colh in range(2):
                        gbank = pb[colh]
                        ubank = pb[2 + colh]
                        sil_dep = self.p.tok("S")
                        mex_dep = self.p.tok("V")
                        for kc in range(NC):
                            self.T(lambda t, o=gbank[:, 0:512], l=pool.tile(sg_, kc, 128),
                                   r=self.fm(h2_slot, kc, tbh * 1024 + colh * 512, 512),
                                   st=(kc == 0), sp=(kc == NC - 1):
                                   t.matmul(o, l, r, start=st, stop=sp),
                                   deps=[tg_, sil_dep], inc=(kc == NC - 1))
                        gt = self.p.tok("T")
                        for kc in range(NC):
                            if e < 8:
                                r = self.sb["xn"][:, kc * 1024 + colh * 512:
                                                  kc * 1024 + colh * 512 + 512]
                            else:
                                r = self.fm(h2_slot, kc, tbh * 1024 + colh * 512, 512)
                            self.T(lambda t, o=ubank[:, 0:512], l=pool.tile(su_, kc, 128),
                                   rr=r, st=(kc == 0), sp=(kc == NC - 1):
                                   t.matmul(o, l, rr, start=st, stop=sp),
                                   deps=[tu_, mup_tok, mex_dep], inc=(kc == NC - 1))
                        ut = self.p.tok("T")
                        sgt = self.sb["scr"][:, SCR_SG:SCR_SG + 512]
                        self.Sc(lambda s, o=sgt, i=gbank[:, 0:512]:
                                s.activation(o, i, AF.Silu),
                                deps=[gt, self.p.tok("V")], inc=True)
                        st_ = self.p.tok("S")
                        self.Ve(lambda v, o=self.mexp_view(ffc, colh),
                                a=ubank[:, 0:512], b=sgt:
                                v.tensor_mul(o, a, b), deps=[ut, st_], inc=True)
                    pool.done(sg_, self.p.tok("T"))
                    pool.done(su_, self.p.tok("T"))
                mexp_tok = self.p.tok("V")
                for dout in range(NC):
                    if e < 8:
                        sw, tw = pool.load(edr[e, :, :, dout * 128:(dout + 1) * 128], 16, 128)
                    else:
                        sw, tw = pool.load(shd[:, :, dout * 128:(dout + 1) * 128], 16, 128)
                    for colh in range(2):
                        bank = pb[4 + colh]
                        vdep = self.p.tok("V")
                        for ffc in range(16):
                            self.T(lambda t, o=bank[:, 0:512], l=pool.tile(sw, ffc, 128),
                                   r=self.mexp_view(ffc, colh), st=(ffc == 0), sp=(ffc == 15):
                                   t.matmul(o, l, r, start=st, stop=sp),
                                   deps=[tw, mexp_tok, vdep], inc=(ffc == 15))
                        tk = self.p.tok("T")
                        xa = self.fm("x_acc", dout, tbh * 1024 + colh * 512, 512)
                        self.Ve(lambda v, o=xa, a=bank[:, 0:512], x=xa:
                                v.tensor_add(o, a, x), deps=[tk], inc=True)
                    pool.done(sw, self.p.tok("T"))
        self.p.barrier()

    def mexp_view(self, ffc, colh):
        return self.sb["ctx"][:, ffc * 1024 + colh * 512: ffc * 1024 + colh * 512 + 512]


    def layer0(self, upto=99):
        w = self.w
        self.layernorm(None, "xn", "plain", src_f32="x_acc")
        self.dump_slot("xn0", "xn")
        if "stats" in self.dump:
            self.p.barrier()
            self.Q(lambda q, o=self.dump["stats"], s=self.sb["stats"][:, :]:
                   q.dma_start(out=o, in_=s), inc=True, dma=True)
            self.p.barrier()
        if upto < 2:
            return
        wq = lambda p: w["L0_attn_Wq"].rearrange("(c p) n -> p c n", p=128)[:, :, p * 128:(p + 1) * 128]
        wk = lambda p: w["L0_attn_Wk"].rearrange("(c p) n -> p c n", p=128)[:, :, p * 128:(p + 1) * 128]
        wv = lambda p: w["L0_attn_Wv"].rearrange("(c p) n -> p c n", p=128)[:, :, p * 128:(p + 1) * 128]
        wo = lambda m: w["L0_attn_Wo"].rearrange("(c p) n -> p c n", p=128)[:, :, m * 128:(m + 1) * 128]
        self.attention("xn", wq, wk, wv, wo, scale=D ** -0.5, causal=True, pos=True,
                       out_mode="acc")
        self.dump_slot("acc_a0", "x_acc")
        if upto < 3:
            return
        self.layernorm("xn", "hA", "affine", w_col=0, b_col=8, eps_col=81)
        self.dump_slot("h1_l0", "hA")
        if upto < 4:
            return
        qkv = w["L0_ffn_w_qkv"].rearrange("(c p) n -> p c n", p=128)
        mq = lambda p: qkv[:, :, p * 128:(p + 1) * 128]
        mkk = lambda p: qkv[:, :, 1024 + p * 128: 1024 + (p + 1) * 128]
        mv = lambda p: qkv[:, :, 2048 + p * 128: 2048 + (p + 1) * 128]
        mwo = lambda m: w["L0_ffn_w_proj"].rearrange("(c p) n -> p c n", p=128)[:, :, m * 128:(m + 1) * 128]
        self.attention("hA", mq, mkk, mv, mwo, scale=0.125, causal=False, pos=False,
                       out_mode="res", out_slot="xn", res_slot="xn")
        self.dump_slot("y1", "xn")
        if upto < 5:
            return
        self.layernorm("xn", "hA", "affine", w_col=16, b_col=24, eps_col=81)
        self.dump_slot("h2_l0", "hA")
        if upto < 6:
            return
        self.moe_l0("hA", sub=(upto - 5 if upto < 99 else 99))
        self.dump_slot("x1", "x_acc")

    def layer1(self, upto=99):
        w = self.w
        self.layernorm(None, "xn", "plain", src_f32="x_acc")
        self.dump_slot("l1_xn", "xn")
        if upto < 2:
            return
        qkv = w["L1_attn_Wqkv"].rearrange("(c p) n -> p c n", p=128)
        mq = lambda p: qkv[:, :, p * 128:(p + 1) * 128]
        mkk = lambda p: qkv[:, :, 1024 + p * 128: 1024 + (p + 1) * 128]
        mv = lambda p: qkv[:, :, 2048 + p * 128: 2048 + (p + 1) * 128]
        mwo = lambda m: w["L1_attn_Wproj"].rearrange("(c p) n -> p c n", p=128)[:, :, m * 128:(m + 1) * 128]
        vb = lambda p: self.srow(0, 3072 + p * 128, 128)
        self.Q(lambda q, o=self.srow(0, 3072, 1024), i=self.w["L1_attn_bqkv"][None, 2048:3072]:
               q.dma_start(out=o, in_=i), inc=True, dma=True)
        self.p.barrier()
        self.attention("xn", mq, mkk, mv, mwo, scale=0.5, causal=False, pos=False,
                       bq_col=48, bk_col=56, vbias_row=vb, bo_col=72, out_mode="acc")
        self.dump_slot("l1_acc", "x_acc")
        if upto < 3:
            return
        self.layernorm("xn", "hA", "rms", w_col=32, eps_col=82)
        self.dump_slot("l1_h", "hA")
        if upto < 4:
            return
        fq = lambda p: w["L1_ffn_wq"].rearrange("(c p) n -> p c n", p=128)[:, :, p * 128:(p + 1) * 128]
        fk = lambda p: w["L1_ffn_wk"].rearrange("(c p) n -> p c n", p=128)[:, :, p * 128:(p + 1) * 128]
        fv = lambda p: w["L1_ffn_wv"].rearrange("(c p) n -> p c n", p=128)[:, :, p * 128:(p + 1) * 128]
        fwo = lambda m: w["L1_ffn_wo"].rearrange("(c p) n -> p c n", p=128)[:, :, m * 128:(m + 1) * 128]
        self.attention("hA", fq, fk, fv, fwo, scale=0.125, causal=False, pos=False,
                       out_mode="res", out_slot="xn", res_slot="xn")
        self.dump_slot("l1_x2", "xn")
        if upto < 5:
            return
        self.layernorm("xn", "hA", "rms", w_col=40, eps_col=82)
        self.dump_slot("l1_h2", "hA")
        if upto < 6:
            return
        self.moe_l1("hA")


def build_nc(layers=(0, 1), dump=(), upto=99):
    nc = bass.Bass("TRN2", target_bir_lowering=False,
                   dynamic_dma_scratch_size=16384)
    prog = Prog()
    w = {}
    shapes = {
        "L0_attn_Wq": [D, D], "L0_attn_Wk": [D, D], "L0_attn_Wv": [D, D],
        "L0_attn_Wo": [D, D], "L0_ffn_w_qkv": [D, 3 * D], "L0_ffn_w_proj": [D, D],
        "L0_ffn_gate_w": [D, 8], "L0_ffn_w1": [8, D, 4 * D], "L0_ffn_w2": [8, 4 * D, D],
        "L1_attn_Wqkv": [D, 3 * D], "L1_attn_bqkv": [3 * D], "L1_attn_Wproj": [D, D],
        "L1_ffn_wq": [D, D], "L1_ffn_wk": [D, D], "L1_ffn_wv": [D, D],
        "L1_ffn_wo": [D, D], "L1_ffn_router_w": [D, 8],
        "L1_ffn_sh_gate": [D, 2048], "L1_ffn_sh_up": [D, 2048],
        "L1_ffn_sh_down": [2048, D],
        "L1_ffn_e_gate": [8, D, 2048], "L1_ffn_e_up": [8, D, 2048],
        "L1_ffn_e_down": [8, 2048, D],
    }
    need = set()
    if 0 in layers:
        need |= {k for k in shapes if k.startswith("L0")}
    if 1 in layers:
        need |= {k for k in shapes if k.startswith("L1")}
    for k in sorted(need):
        w[k] = nc.dram_tensor(k, shapes[k], F32, kind="ExternalInput").ap()
    xin = nc.dram_tensor("xT", [D, S], F32, kind="ExternalInput").ap()
    pos2_in = nc.dram_tensor("pos2c", [128, S], F32, kind="ExternalInput").ap()
    pp_in = nc.dram_tensor("ppc", [128, 96], F32, kind="ExternalInput").ap()
    w["onesrc"] = nc.dram_tensor("onesrc", [1, 128], F32, kind="ExternalInput").ap()
    out = nc.dram_tensor("out", [D, S], F32, kind="ExternalOutput").ap()
    dump_t = {}
    for name in dump:
        if name == "bnd":
            dump_t[name] = nc.dram_tensor("DUMP_bnd", [128, 96], F32,
                                          kind="ExternalOutput").ap()
            continue
        if name == "qkv0":
            dump_t[name] = nc.dram_tensor("DUMP_qkv0", [128, 8192], BF16,
                                          kind="ExternalOutput").ap()
            continue
        if name == "stats":
            dump_t[name] = nc.dram_tensor("DUMP_stats", [128, 4096], F32,
                                          kind="ExternalOutput").ap()
            continue
        dt = F32 if name in ("acc_a0", "x1", "accpre", "l1_acc") else BF16
        dump_t[name] = nc.dram_tensor("DUMP_" + name, [128, 16384], dt,
                                      kind="ExternalOutput").ap()

    import contextlib
    with contextlib.ExitStack() as st:
        sb = {}
        def sbt(name, shape, dtype):
            sb[name] = st.enter_context(nc.sbuf_tensor(name, shape, dtype))
        sbt("x_acc", [128, 16384], F32)
        sbt("xn", [128, 16384], BF16)
        sbt("hA", [128, 16384], BF16)
        sbt("ctx", [128, 16384], BF16)
        sbt("scr", [128, SCR_COLS], BF16)
        sbt("wpan", [128, WPAN_SLOT * WPAN_NSLOT], BF16)
        sbt("stats", [128, 4096], F32)
        sbt("lrow", [8, 2048], BF16)
        sbt("sel", [8, 128], BF16)
        sbt("idbf", [128, 128], BF16)
        sbt("tmpf", [128, 512], F32)
        sbt("pp", [128, 96], F32)
        sbt("onesb", [128, 128], BF16)
        sbt("onesf", [128, 128], F32)
        sbt("negB", [128, 1], F32)
        sbt("cells", [128, 16], F32)
        sbt("mtok", [128, 8], BF16)
        sbt("cb", [128, 8], BF16)
        pb = [st.enter_context(nc.psum_tensor(f"pb{i}", [128, 512], F32))
              for i in range(8)]
        sems = {e: st.enter_context(nc.semaphore(f"sem{e}")) for e in Prog.SEMS}

        mk = MK(nc, prog, sb, pb, w, dump_t)
        mk.pb_bf7 = pb[7].bitcast(BF16)
        mk.pb_bf6 = pb[6].bitcast(BF16)
        mk.pos2v = sb["hA"].bitcast(F32)[:, 0:S]
        mk.Q(lambda q, o=sb["x_acc"][:, :].rearrange("p (c t) -> p c t", c=NC),
             i=xin.rearrange("(c p) t -> p c t", p=128): q.dma_start(out=o, in_=i),
             inc=True, dma=True)
        mk.Q(lambda q, o=mk.pos2v, i=pos2_in: q.dma_start(out=o, in_=i),
             inc=True, dma=True)
        mk.Q(lambda q, o=sb["pp"][:, :], i=pp_in: q.dma_start(out=o, in_=i),
             inc=True, dma=True)
        mk.G(lambda g: g.memset(sb["onesb"][:, :], 1.0))
        mk.G(lambda g: g.memset(sb["onesf"][:, :], 1.0))
        g1 = mk.G(lambda g: g.memset(sb["idbf"][:, :], 1.0), inc=True)
        mk.p.emit("G", lambda g: g.affine_select(sb["idbf"][:, :], sb["idbf"][:, :],
                                                 [[1, 128]], ALU.is_equal, 0.0,
                                                 base=0, channel_multiplier=-1),
                  deps=[g1], inc=True, self_wait=True)
        prog.barrier()
        if 0 in layers:
            mk.layer0(upto)
        if 1 in layers:
            mk.layer1(upto if layers == (1,) else 99)
        prog.barrier()
        mk.Q(lambda q, o=out.rearrange("(c p) t -> p c t", p=128),
             i=sb["x_acc"][:, :].rearrange("p (c t) -> p c t", c=NC):
             q.dma_start(out=o, in_=i), inc=True, dma=True)
        mk.Q(lambda q: q.nop(), deps=[prog.tok("Q")], self_wait=True)

        with nc.Block() as block:
            def runner(engname):
                def body(eng):
                    ops = prog.ops[engname]
                    sem_map = sems
                    for fn, waits, incn, semname in ops:
                        for (sname, v) in waits:
                            eng.wait_ge(sem_map[sname], v)
                        inst = fn(eng)
                        if incn:
                            inst.then_inc(sem_map[semname], incn)
                return body
            block.tensor(runner("T"))
            block.scalar(runner("S"))
            block.vector(runner("V"))
            block.gpsimd(runner("G"))
            block.sync(runner("Q"))
    return nc



def _host_consts():
    idx = np.arange(S, dtype=np.float64)
    dim = 64
    eps = float(np.finfo(np.float32).eps)
    scale = 1.0 / (10000.0 ** (np.arange(0, dim, 2, dtype=np.float64) /
                               np.float64(dim)) + eps)
    idx32 = np.arange(S, dtype=np.float32)
    sc32 = (1.0 / (10000.0 ** (np.arange(0, dim, 2, dtype=np.float32) / np.float32(dim))
                   + np.float32(eps))).astype(np.float32)
    ang = idx32[:, None] * sc32[None, :]
    pos = np.stack([np.sin(ang), np.cos(ang)], axis=-1).reshape(S, dim).astype(np.float32)
    pos2 = np.concatenate([pos.T, pos.T], axis=0).astype(np.float32)
    return pos2


def _pack_pp(inputs):
    pp = np.zeros((128, 96), np.float32)
    def put(col, vec):
        v = np.asarray(vec, np.float32).reshape(-1)
        n = v.size // 128
        pp[:, col:col + n] = v.reshape(n, 128).T
    put(0, inputs["L0_ffn_ln1_w"]); put(8, inputs["L0_ffn_ln1_b"])
    put(16, inputs["L0_ffn_ln2_w"]); put(24, inputs["L0_ffn_ln2_b"])
    put(32, inputs["L1_ffn_norm1_w"]); put(40, inputs["L1_ffn_norm2_w"])
    put(48, inputs["L1_attn_bqkv"]); put(72, inputs["L1_attn_bproj"])
    pp[0:8, 80] = np.asarray(inputs["L0_ffn_gate_b"], np.float32)
    pp[:, 81] = 1e-5
    pp[:, 82] = 1e-6
    for e in range(8):
        pp[:, 83 + e] = -e * 1e-6
    return pp


_WEIGHT_KEYS = [
    "L0_attn_Wq", "L0_attn_Wk", "L0_attn_Wv", "L0_attn_Wo",
    "L0_ffn_w_qkv", "L0_ffn_w_proj", "L0_ffn_gate_w", "L0_ffn_w1", "L0_ffn_w2",
    "L1_attn_Wqkv", "L1_attn_bqkv", "L1_attn_Wproj",
    "L1_ffn_wq", "L1_ffn_wk", "L1_ffn_wv", "L1_ffn_wo", "L1_ffn_router_w",
    "L1_ffn_sh_gate", "L1_ffn_sh_up", "L1_ffn_sh_down",
    "L1_ffn_e_gate", "L1_ffn_e_up", "L1_ffn_e_down",
]


def kernel(**inputs):
    x = np.asarray(inputs["L0_attn_x"], np.float32)
    pos2 = _host_consts()
    pp = _pack_pp(inputs)
    warr = {k: np.ascontiguousarray(np.asarray(inputs[k], np.float32))
            for k in _WEIGHT_KEYS}
    xTs = [np.ascontiguousarray(x[i].T) for i in range(NCORES)]
    consts = {"pos2c": pos2, "ppc": pp,
              "onesrc": np.ones((1, 128), np.float32)}

    if FUSE:
        nc = build_nc(layers=(0, 1))
        in_maps = []
        for i in range(NCORES):
            m = dict(consts)
            m.update(warr)
            m["xT"] = xTs[i]
            in_maps.append(m)
        res = run_bass_kernel_spmd(nc, in_maps, core_ids=list(range(NCORES)))
        outs = [res.results[i]["out"] for i in range(NCORES)]
    else:
        nc0 = build_nc(layers=(0,))
        in_maps = []
        w0 = {k: v for k, v in warr.items() if k.startswith("L0")}
        for i in range(NCORES):
            m = dict(consts)
            m.update(w0)
            m["xT"] = xTs[i]
            in_maps.append(m)
        res0 = run_bass_kernel_spmd(nc0, in_maps, core_ids=list(range(NCORES)))
        mids = [np.asarray(res0.results[i]["out"]) for i in range(NCORES)]
        nc1 = build_nc(layers=(1,))
        w1 = {k: v for k, v in warr.items() if k.startswith("L1")}
        in_maps = []
        for i in range(NCORES):
            m = dict(consts)
            m.update(w1)
            m["xT"] = mids[i]
            in_maps.append(m)
        res = run_bass_kernel_spmd(nc1, in_maps, core_ids=list(range(NCORES)))
        outs = [res.results[i]["out"] for i in range(NCORES)]

    return np.stack([np.asarray(o).T for o in outs], axis=0).astype(np.float32)


# version 08
# speedup vs baseline: 1.3638x; 1.3638x over previous
import numpy as np
import concourse.bass as bass
import concourse.mybir as mybir
from concourse.bass_utils import run_bass_kernel_spmd

F32 = mybir.dt.float32
BF16 = mybir.dt.bfloat16
AF = mybir.ActivationFunctionType
ALU = mybir.AluOpType
AX = mybir.AxisListType

D = 1024
S = 2048
NC = 8
NCORES = 8
HPAIRS = 8
FUSE = True

SCR_Q = 0
SCR_K = 2048
SCR_V = 4096
SCR_PROBS = 6176
SCR_MH2 = 0
SCR_SG = 6176
SCR_COLS = 6176 + 2 * 2 * 512

WPAN_SLOT = 2048
WPAN_NSLOT = 3


class Prog:
    ENGS = "TSVGQ"
    GD = ("G0", "G1", "G2", "G3")
    QD = ("Q0", "Q1")
    SEMS = ("T", "S", "V", "Gc", "Qc") + GD + QD

    def __init__(self):
        self.ops = {e: [] for e in self.ENGS}
        self.cnt = {s: 0 for s in self.SEMS}
        self.marks = {e: dict.fromkeys(self.SEMS, 0) for e in self.ENGS}
        self.pending = {e: [] for e in self.ENGS}
        self.dmai = {"G": 0, "Q": 0}

    def emit(self, eng, fn, deps=(), inc=False, dma=False, self_wait=False):
        deps = [d for d in deps if d is not None]
        if dma:
            assert inc and eng in ("G", "Q")
            ring = self.GD if eng == "G" else self.QD
            sem = ring[self.dmai[eng] % len(ring)]
            self.dmai[eng] += 1
            deps = deps + [(sem, self.cnt[sem])]
        else:
            sem = eng + "c" if eng in ("G", "Q") else eng
        want = {}
        for d in list(self.pending[eng]) + deps:
            if d is None:
                continue
            de, dc = d
            if dc <= 0:
                continue
            if dc > self.marks[eng][de]:
                self.marks[eng][de] = dc
                want[de] = max(want.get(de, 0), dc)
        self.pending[eng] = []
        waits = sorted(want.items())
        amt = 0
        if inc:
            amt = 16 if dma else 1
            self.cnt[sem] += amt
        self.ops[eng].append([fn, waits, amt, sem])
        if inc:
            return (sem, self.cnt[sem])
        return None

    def tok(self, sem):
        if sem == "G":
            sem = "Gc"
        if sem == "Q":
            ring = self.QD
            i = (self.dmai["Q"] - 1) % len(ring) if self.dmai["Q"] else 0
            return (ring[i], self.cnt[ring[i]])
        return (sem, self.cnt[sem])

    def barrier(self):
        for e in self.ENGS:
            if self.ops[e]:
                last = self.ops[e][-1]
                if last[2] == 0:
                    sem = last[3]
                    assert sem not in self.GD + self.QD
                    last[2] = 1
                    self.cnt[sem] += last[2]
        toks = [(s, self.cnt[s]) for s in self.SEMS]
        for e in self.ENGS:
            self.pending[e] = list(toks)


class Pool:

    def __init__(self, mk, nslots=WPAN_NSLOT):
        self.mk = mk
        self.n = nslots
        self.freetok = [None] * nslots
        self.i = 0

    def load(self, in_ap, nch, ncols):
        mk = self.mk
        s = self.i % self.n
        self.i += 1
        base = s * WPAN_SLOT
        assert nch * ncols <= WPAN_SLOT
        out = mk.sb["wpan"][:, base:base + nch * ncols].rearrange(
            "p (c n) -> p c n", c=nch)
        tok = mk.p.emit("G", lambda g, o=out, i=in_ap: g.dma_start(out=o, in_=i),
                        deps=[self.freetok[s]], inc=True, dma=True)
        return s, tok

    def tile(self, s, kc, ncols):
        base = s * WPAN_SLOT
        return self.mk.sb["wpan"][:, base + kc * ncols: base + (kc + 1) * ncols]

    def done(self, s, tok):
        self.freetok[s] = tok


class MK:
    def __init__(self, nc, prog, sb, pb, w, dump):
        self.nc = nc
        self.p = prog
        self.sb = sb
        self.pb = pb
        self.w = w
        self.dump = dump

    def make_sel(self, e):
        tprev = self.p.tok("T")
        g1 = self.G(lambda g: g.memset(self.sb["stats"].bitcast(BF16)[0:8, 2432:2560][0:8, 0:128], 0.0),
                    deps=[tprev], inc=True)
        return self.p.emit("G", lambda g, o=self.sb["stats"].bitcast(BF16)[0:8, 2432:2560][e:e + 1, 0:128],
                           i=self.w["onesrc"]: g.dma_start(out=o, in_=i),
                           deps=[g1], inc=True, dma=True, self_wait=True)

    def fm(self, slot, c, t0, n, h0=0, hp=128):
        return self.sb[slot][h0:h0 + hp, c * S + t0: c * S + t0 + n]

    def pprow(self, col, p0=0, np_=128):
        return self.sb["pp"][p0:p0 + np_, col:col + 1]

    def srow(self, p, c0, n):
        return self.sb["stats"][p:p + 1, c0:c0 + n]

    def T(self, fn, deps=(), inc=False):
        return self.p.emit("T", fn, deps, inc)

    def Sc(self, fn, deps=(), inc=False, sw=False):
        return self.p.emit("S", fn, deps, inc, False, sw)

    def Ve(self, fn, deps=(), inc=False, sw=False):
        return self.p.emit("V", fn, deps, inc, False, sw)

    def G(self, fn, deps=(), inc=False, dma=False):
        return self.p.emit("G", fn, deps, inc, dma)

    def Q(self, fn, deps=(), inc=False, dma=False, self_wait=False):
        return self.p.emit("Q", fn, deps, inc, dma, self_wait)

    def dump_slot(self, name, slot, dtype_cols=16384):
        if name not in self.dump:
            return
        out = self.dump[name]
        src = self.sb[slot][:, 0:dtype_cols]
        self.p.barrier()
        self.Q(lambda q, o=out, s=src: q.dma_start(out=o, in_=s), inc=True, dma=True)
        self.p.barrier()


    def layernorm(self, src_slot, dst_slot, kind, w_col=None, b_col=None,
                  eps_col=81, src_f32=None):
        mk = self
        pb = self.pb
        if src_f32 is not None:
            for c in range(NC):
                self.Ve(lambda v, o=self.fm(dst_slot, c, 0, S),
                        i=self.fm(src_f32, c, 0, S): v.tensor_copy(o, i))
            src_b = dst_slot
        else:
            src_b = src_slot
        self.p.barrier()
        need_mean = kind != "rms"
        sq_mm_tok = [None, None]
        st_ev_tok = {}
        for qc in range(4):
            sb6 = pb[6] if qc % 2 == 0 else pb[4]
            sb5 = pb[5] if qc % 2 == 0 else pb[7]
            vdep = st_ev_tok.get(qc % 2)
            for c in range(NC):
                buf = c % 2
                sqv = self.sb["scr"][:, SCR_PROBS + buf * 512:SCR_PROBS + buf * 512 + 512]
                t = self.Sc(lambda s, o=sqv, i=self.fm(src_b, c, qc * 512, 512):
                            s.activation(o, i, AF.Square),
                            deps=[sq_mm_tok[buf]], inc=True)
                if need_mean:
                    self.T(lambda tt, o=sb6[0:1, :], l=self.sb["onesb"][:, 0:1],
                           r=self.fm(src_b, c, qc * 512, 512), st=(c == 0), sp=(c == NC - 1):
                           tt.matmul(o, l, r, start=st, stop=sp),
                           deps=[vdep])
                self.T(lambda tt, o=sb5[0:1, :], l=self.sb["onesb"][:, 0:1],
                       r=sqv, st=(c == 0), sp=(c == NC - 1):
                       tt.matmul(o, l, r, start=st, stop=sp),
                       deps=[t, vdep], inc=True)
                sq_mm_tok[buf] = self.p.tok("T")
            tk = self.p.tok("T")
            if need_mean:
                self.Ve(lambda v, o=self.srow(0, qc * 512, 512), i=sb6[0:1, 0:512]:
                        v.tensor_copy(o, i), deps=[tk])
            self.Ve(lambda v, o=self.srow(32, qc * 512, 512), i=sb5[0:1, 0:512]:
                    v.tensor_copy(o, i), deps=[tk], inc=True)
            st_ev_tok[qc % 2] = self.p.tok("V")
        self.p.barrier()
        a_row = lambda c0, n: self.srow(32, S + c0, n)
        b_row = lambda c0, n: self.srow(0, c0, n)
        if need_mean:
            k1 = self.Ve(lambda v, o=self.srow(0, 0, S), i=self.srow(0, 0, S):
                         v.tensor_scalar_mul(o, i, 1.0 / D), inc=True)
            k2 = self.Ve(lambda v, o=self.srow(0, S, S), a=self.srow(0, 0, S),
                         b=self.srow(0, 0, S): v.tensor_mul(o, a, b),
                         deps=[k1], inc=True, sw=True)
            k3 = self.Ve(lambda v, o=self.srow(32, S, S), i=self.srow(0, S, S):
                         v.tensor_copy(o, i), deps=[k2], inc=True, sw=True)
            self.Ve(lambda v, o=self.srow(32, 0, S), a=self.srow(32, 0, S),
                    b=self.srow(32, S, S):
                    v.scalar_tensor_tensor(o, a, 1.0 / D, b, ALU.mult, ALU.subtract),
                    deps=[k3], inc=True, sw=True)
        else:
            self.Ve(lambda v, o=self.srow(32, 0, S), i=self.srow(32, 0, S):
                    v.tensor_scalar_mul(o, i, 1.0 / D), inc=True)
        sv = self.p.tok("V")
        l1 = self.Sc(lambda s, o=self.srow(32, S, S), i=self.srow(32, 0, S),
                     e=self.sb["pp"][32:33, eps_col:eps_col + 1]:
                     s.activation(o, i, AF.Ln, bias=e), deps=[sv], inc=True)
        self.Sc(lambda s, o=self.srow(32, S, S), i=self.srow(32, S, S):
                s.activation(o, i, AF.Exp, scale=-0.5), deps=[l1], inc=True, sw=True)
        ss = self.p.tok("S")
        if need_mean:
            k4 = self.Ve(lambda v, o=self.srow(0, S, S), i=self.srow(32, S, S):
                         v.tensor_copy(o, i), deps=[ss], inc=True)
            self.Ve(lambda v, o=self.srow(0, 0, S), a=self.srow(0, 0, S),
                    b=self.srow(0, S, S):
                    v.scalar_tensor_tensor(o, a, -1.0, b, ALU.mult, ALU.mult),
                    deps=[k4], inc=True, sw=True)
        self.p.barrier()
        uprev = None
        for qc in range(4):
            vdep = self.p.tok("V")
            ta = self.T(lambda t, o=pb[7][:, 0:512], l=self.sb["onesf"][32:33, :],
                        r=a_row(qc * 512, 512): t.matmul(o, l, r, start=True, stop=True),
                        deps=[vdep], inc=True)
            tb = None
            if need_mean:
                tb = self.T(lambda t, o=pb[4][:, 0:512], l=self.sb["onesf"][0:1, :],
                            r=b_row(qc * 512, 512): t.matmul(o, l, r, start=True, stop=True),
                            deps=[vdep], inc=True)
            for c in range(NC):
                src = (self.fm(src_f32, c, qc * 512, 512) if src_f32 is not None
                       else self.fm(src_b, c, qc * 512, 512))
                if kind == "plain":
                    u1 = self.Ve(lambda v, o=self.sb["scr"].bitcast(F32)[:, 3088:3600], a=src, b=pb[7][:, 0:512]:
                                 v.tensor_mul(o, a, b), deps=[ta, uprev], inc=True)
                    uprev = self.Ve(lambda v, o=self.fm(dst_slot, c, qc * 512, 512),
                                    a=self.sb["scr"].bitcast(F32)[:, 3088:3600], b=pb[4][:, 0:512]:
                                    v.tensor_add(o, a, b), deps=[tb, u1], inc=True)
                elif kind == "affine":
                    u1 = self.Ve(lambda v, o=self.sb["scr"].bitcast(F32)[:, 3088:3600], a=src, b=pb[7][:, 0:512]:
                                 v.tensor_mul(o, a, b), deps=[ta, uprev], inc=True)
                    u2 = self.Ve(lambda v, o=self.sb["scr"].bitcast(F32)[:, 3088:3600],
                                 a=self.sb["scr"].bitcast(F32)[:, 3088:3600], b=pb[4][:, 0:512]:
                                 v.tensor_add(o, a, b), deps=[tb, u1], inc=True)
                    uprev = self.Ve(lambda v, o=self.fm(dst_slot, c, qc * 512, 512),
                                    a=self.sb["scr"].bitcast(F32)[:, 3088:3600], wc=self.pprow(w_col + c),
                                    bc=self.pprow(b_col + c):
                                    v.tensor_scalar(o, a, wc, bc, ALU.mult, ALU.add),
                                    deps=[u2], inc=True)
                else:
                    u1 = self.Ve(lambda v, o=self.sb["scr"].bitcast(F32)[:, 3088:3600], a=src, b=pb[7][:, 0:512]:
                                 v.tensor_mul(o, a, b), deps=[ta, uprev], inc=True)
                    uprev = self.Ve(lambda v, o=self.fm(dst_slot, c, qc * 512, 512),
                                    a=self.sb["scr"].bitcast(F32)[:, 3088:3600], wc=self.pprow(w_col + c):
                                    v.tensor_scalar_mul(o, a, wc), deps=[u1], inc=True)
        self.p.barrier()

    def attention(self, inp_slot, mkq, mkk, mkv, mkwo, scale, causal=False,
                  pos=False, bq_col=None, bk_col=None, vbias_row=None,
                  bo_col=None, out_mode="acc", out_slot=None, res_slot=None):
        mk = self
        pb = self.pb
        pool = Pool(self)
        self.sc_free = [None, None, None]
        self.probs_free = [None, None]
        self.pv_free = [None, None]
        self.sum_free = [None, None]
        self.prj_tok = {}
        self._pend_epi = None
        for pr in range(HPAIRS):
            sq, tq = pool.load(mkq(pr), NC, 128)
            sk, tk_ = pool.load(mkk(pr), NC, 128)
            sdep = self.p.tok("S")
            for which, slot_off, wsl, wtok, bcol in (
                    (0, SCR_Q, sq, tq, bq_col), (1, SCR_K, sk, tk_, bk_col)):
                for qc in range(4):
                    bi = which * 2 + (qc % 2)
                    bank = pb[bi]
                    dep_prev = self.prj_tok.get(bi)
                    for kc in range(NC):
                        self.T(lambda t, o=bank[:, 0:512],
                               l=pool.tile(wsl, kc, 128),
                               r=self.fm(inp_slot, kc, qc * 512, 512), st=(kc == 0), sp=(kc == NC - 1):
                               t.matmul(o, l, r, start=st, stop=sp),
                               deps=[wtok, dep_prev, sdep], inc=(kc == NC - 1))
                    ttok = self.p.tok("T")
                    dst = self.sb["scr"][:, slot_off + qc * 512: slot_off + qc * 512 + 512]
                    if pos:
                        self.Ve(lambda v, o=dst, a=bank[:, 0:512],
                                b=self.pos2v[:, qc * 512:qc * 512 + 512]:
                                v.tensor_add(o, a, b), deps=[ttok], inc=True)
                    elif bcol is not None:
                        self.Ve(lambda v, o=dst, a=bank[:, 0:512],
                                bc=self.pprow(bcol + pr):
                                v.tensor_scalar_add(o, a, bc), deps=[ttok], inc=True)
                    else:
                        self.Ve(lambda v, o=dst, a=bank[:, 0:512]:
                                v.tensor_copy(o, a), deps=[ttok], inc=True)
                    self.prj_tok[bi] = self.p.tok("V")
            pool.done(sq, self.p.tok("T"))
            pool.done(sk, self.p.tok("T"))
            sv, tv = pool.load(mkv(pr), NC, 128)
            if pr == 0:
                ov = self.sb["scr"][:, SCR_V:SCR_V + 2080].rearrange(
                    "p (t c) -> p t c", c=65)[:, :, 64:65]
                self.Ve(lambda v, o=ov: v.memset(o, 1.0),
                        deps=[self.p.tok("T")], inc=True)
            for tt in range(16):
                bank = pb[4 + (tt % 2)]
                vdep = self.pv_free[tt % 2]
                last_plain = (vbias_row is None)
                for kc in range(NC):
                    self.T(lambda t, o=bank[:, 0:128],
                           l=self.fm(inp_slot, kc, tt * 128, 128),
                           r=pool.tile(sv, kc, 128), st=(kc == 0),
                           sp=(last_plain and kc == NC - 1):
                           t.matmul(o, l, r, start=st, stop=sp),
                           deps=[tv, vdep],
                           inc=(last_plain and kc == NC - 1))
                if vbias_row is not None:
                    self.T(lambda t, o=bank[:, 0:128], l=self.sb["onesf"][0:1, :],
                           r=vbias_row(pr): t.matmul(o, l, r, start=False, stop=True),
                           inc=True)
                ttok = self.p.tok("T")
                self.Ve(lambda v, o=self.sb["scr"][:, SCR_V + tt * 130: SCR_V + tt * 130 + 64],
                        a=bank[:, 0:64]: v.tensor_copy(o, a), deps=[ttok], inc=True)
                self.Ve(lambda v, o=self.sb["scr"][:, SCR_V + tt * 130 + 65: SCR_V + tt * 130 + 129],
                        a=bank[:, 64:128]: v.tensor_copy(o, a), inc=True)
                self.pv_free[tt % 2] = self.p.tok("V")
            pool.done(sv, self.p.tok("T"))
            if pr == 0 and "qkv0" in self.dump:
                self.p.barrier()
                self.Q(lambda q, o=self.dump["qkv0"], s=self.sb["scr"][:, 0:8192]:
                       q.dma_start(out=o, in_=s), inc=True, dma=True)
                self.p.barrier()
            sqreg = self.sb["scr"][:, SCR_PROBS:SCR_PROBS + 2048]
            vprev = self.p.tok("V")
            cells = self.sb["cells"]
            for which, slot_off, ccol in ((0, SCR_Q, 0), (1, SCR_K, 1)):
                tprev = self.p.tok("T")
                sqt = self.Sc(lambda s, o=sqreg,
                              i=self.sb["scr"][:, slot_off:slot_off + S]:
                              s.activation(o, i, AF.Square),
                              deps=[vprev, tprev, self.probs_free[0], self.probs_free[1], sdep],
                              inc=True)
                vdone = self.p.tok("V")
                for qc in range(4):
                    self.T(lambda t, o=pb[qc][0:1, 0:512], l=self.sb["onesb"][:, 0:1],
                           r=sqreg[:, qc * 512:qc * 512 + 512]:
                           t.matmul(o, l, r, start=True, stop=True),
                           deps=[sqt, vdone, self.sc_free[0], self.sc_free[1], self.sc_free[2],
                                 self.sum_free[1]], inc=True)
                tmm = self.p.tok("T")
                mtk = None
                for qc in range(4):
                    rt = self.Ve(lambda v, o=cells[0:1, 8 + qc:9 + qc], i=pb[qc][0:1, 0:512]:
                                 v.reduce_max(o, i, AX.X), deps=[tmm, mtk],
                                 inc=True, sw=True)
                    mtk = rt
                m1t = self.Ve(lambda v, o=cells[0:1, ccol:ccol + 1], a=cells[0:1, 8:9],
                              b=cells[0:1, 9:10]: v.tensor_max(o, a, b),
                              deps=[mtk], inc=True, sw=True)
                m2t = self.Ve(lambda v, o=cells[0:1, 12:13], a=cells[0:1, 10:11],
                              b=cells[0:1, 11:12]: v.tensor_max(o, a, b),
                              deps=[m1t], inc=True, sw=True)
                self.Ve(lambda v, o=cells[0:1, ccol:ccol + 1], a=cells[0:1, ccol:ccol + 1],
                        b=cells[0:1, 12:13]: v.tensor_max(o, a, b),
                        deps=[m2t], inc=True, sw=True)
            s1 = self.Ve(lambda v, o=cells[0:1, 4:5], a=cells[0:1, 0:1],
                         b=cells[0:1, 1:2]: v.tensor_add(o, a, b),
                         deps=[self.p.tok("V")], inc=True, sw=True)
            self.Ve(lambda v, o=self.sb["stats"].bitcast(BF16)[0:8, 2560:4608][0:1, 0:1], i=cells[0:1, 4:5]:
                    v.tensor_scalar_mul(o, i, -0.5 * scale), deps=[s1], inc=True, sw=True)
            vb = self.p.tok("V")
            tb = self.T(lambda t, o=pb[6][:, 256:257], l=self.sb["onesb"][0:1, :],
                        r=self.sb["stats"].bitcast(BF16)[0:8, 2560:4608][0:1, 0:1]:
                        t.matmul(o, l, r, start=True, stop=True),
                        deps=[vb, self.sum_free[0]], inc=True)
            self.Ve(lambda v, o=self.sb["negB"][:, 0:1], i=pb[6][:, 256:257]:
                    v.tensor_copy(o, i), deps=[tb, sdep], inc=True)
            self.probs_free[0] = self.p.tok("T")
            self.probs_free[1] = self.p.tok("T")
            negB_tok = self.p.tok("V")
            if "bnd" in self.dump:
                self.p.barrier()
                self.Q(lambda q, o=self.dump["bnd"][:, pr * 4:pr * 4 + 1],
                       s=self.sb["negB"][:, 0:1]:
                       q.dma_start(out=o, in_=s, allow_slow_non_contiguous=True),
                       inc=True, dma=True)
                self.Q(lambda q, o=self.dump["bnd"][0:1, 32 + pr * 8:32 + pr * 8 + 5],
                       s=self.sb["cells"][0:1, 0:5]:
                       q.dma_start(out=o, in_=s, allow_slow_non_contiguous=True),
                       inc=True, dma=True)
                self.p.barrier()
            for hh, h0 in ((0, 0), (1, 64)):
                for qb in range(4):
                    pvb_i = qb % 2
                    pvbank = pb[4 + pvb_i]
                    par = (4 * hh + qb) % 2
                    sumbank = pb[6] if par == 0 else pb[3]
                    kmax = (4 * qb + 4) if causal else 16
                    kcs = list(range(kmax))
                    strips = [kcs[i:i + 2] for i in range(0, kmax, 2)]
                    nst = len(strips)
                    exp_toks = {}

                    def emit_se(st_i, h0=h0, qb=qb):
                        strip = strips[st_i]
                        buf = st_i % 2
                        pbase = SCR_PROBS + buf * 1024
                        toks = []
                        for j, kc in enumerate(strip):
                            scb = pb[kc % 3]
                            self.T(lambda t, o=scb[:, 0:512],
                                   l=self.sb["scr"][h0:h0 + 64, SCR_K + kc * 128:SCR_K + kc * 128 + 128],
                                   r=self.sb["scr"][h0:h0 + 64, SCR_Q + qb * 512:SCR_Q + qb * 512 + 512]:
                                   t.matmul(o, l, r, start=True, stop=True),
                                   deps=[negB_tok, self.sc_free[kc % 3]], inc=True)
                            sctok = self.p.tok("T")
                            ptile = self.sb["scr"][:, pbase + j * 512: pbase + j * 512 + 512]
                            diag = causal and (kc >= 4 * qb)
                            et = self.Sc(lambda s, o=ptile, i=scb[:, 0:512],
                                         b=self.sb["negB"][:, 0:1], sc=scale:
                                         s.activation(o, i, AF.Exp, bias=b, scale=sc),
                                         deps=[sctok, self.probs_free[buf]],
                                         inc=True)
                            self.sc_free[kc % 3] = self.p.tok("S")
                            if diag:
                                gt = self.G(lambda g, o=ptile, base=512 * qb - 128 * kc:
                                            g.affine_select(o, o, [[1, 512]], ALU.is_ge,
                                                            0.0, base=base,
                                                            channel_multiplier=-1),
                                            deps=[et], inc=True)
                                toks.append(gt)
                            else:
                                toks.append(et)
                        exp_toks[st_i] = toks

                    def emit_cspv(st_i, h0=h0, qb=qb, pvbank=pvbank, sumbank=sumbank,
                                  par=par, pvb_i=pvb_i, nst_=None):
                        strip = strips[st_i]
                        buf = st_i % 2
                        pbase = SCR_PROBS + buf * 1024
                        dep_all = list(exp_toks[st_i])
                        for j, kc in enumerate(strip):
                            ptile = self.sb["scr"][:, pbase + j * 512: pbase + j * 512 + 512]
                            first = (st_i == 0 and j == 0)
                            last = (st_i == nst - 1 and j == len(strip) - 1)
                            self.T(lambda t, o=pvbank[0:65, 0:512],
                                   l=self.sb["scr"][:, SCR_V + kc * 130 + hh * 65: SCR_V + kc * 130 + hh * 65 + 65],
                                   r=ptile, st=first, sp=last:
                                   t.matmul(o, l, r, start=st, stop=sp),
                                   deps=dep_all + [self.pv_free[pvb_i]], inc=True)
                            dep_all = []
                        self.probs_free[buf] = self.p.tok("T")

                    emit_se(0)
                    if self._pend_epi is not None:
                        self._pend_epi()
                        self._pend_epi = None
                    for st_i in range(nst):
                        if st_i + 1 < nst:
                            emit_se(st_i + 1)
                        emit_cspv(st_i)
                    tk = self.p.tok("T")

                    def epilogue(tk=tk, sumbank=sumbank, h0=h0, qb=qb, pr=pr,
                                 par=par, pvbank=pvbank, pvb_i=pvb_i):
                        e1 = self.Ve(lambda v, o=self.srow(0, 0, 512), i=pvbank[64:65, 0:512]:
                                     v.tensor_copy(o, i), deps=[tk], inc=True)
                        rc = self.Ve(lambda v, o=self.srow(0, 0, 512), i=self.srow(0, 0, 512):
                                     v.reciprocal(o, i), deps=[e1], inc=True)
                        self.Ve(lambda v, o=self.sb["stats"].bitcast(BF16)[0:8, 2560:4608][0:1, 64:576], i=self.srow(0, 0, 512):
                                v.tensor_copy(o, i), deps=[rc], inc=True)
                        rv = self.p.tok("V")
                        tbc = self.T(lambda t, o=sumbank[0:64, 0:512],
                                     l=self.sb["onesb"][0:1, 0:64],
                                     r=self.sb["stats"].bitcast(BF16)[0:8, 2560:4608][0:1, 64:576]:
                                     t.matmul(o, l, r, start=True, stop=True),
                                     deps=[rv], inc=True)
                        rb = self.Ve(lambda v, o=self.sb["stats"][0:64, 512:1024],
                                     i=sumbank[0:64, 0:512]:
                                     v.tensor_copy(o, i), deps=[tbc], inc=True)
                        self.sum_free[par] = self.p.tok("V")
                        if h0 == 0:
                            self.Ve(lambda v, o=self.fm("ctx", pr, qb * 512, 512, 0, 64),
                                    a=pvbank[0:64, 0:512],
                                    b=self.sb["stats"][0:64, 512:1024]:
                                    v.tensor_mul(o, a, b), deps=[rb], inc=True)
                        else:
                            sg = self.Ve(lambda v, o=self.sb["stats"][0:64, 2304:2816],
                                         a=pvbank[0:64, 0:512],
                                         b=self.sb["stats"][0:64, 512:1024]:
                                         v.tensor_mul(o, a, b), deps=[rb], inc=True)
                            self.Ve(lambda v, o=self.fm("ctx", pr, qb * 512, 512, 64, 64),
                                    i=self.sb["stats"][0:64, 2304:2816]:
                                    v.tensor_copy(o, i), deps=[sg], inc=True)
                        self.pv_free[pvb_i] = self.p.tok("V")

                    self._pend_epi = epilogue
            if self._pend_epi is not None:
                self._pend_epi()
                self._pend_epi = None
        self.dump_slot("ctx", "ctx")
        self.dump_slot("accpre", "x_acc")
        self.p.barrier()
        pool = Pool(self)
        wo_tok = {}
        for mc in range(NC):
            sw, tw = pool.load(mkwo(mc), NC, 128)
            for qc in range(4):
                bi = qc % 2
                bank = pb[bi]
                vdep = wo_tok.get(bi)
                for kc in range(NC):
                    self.T(lambda t, o=bank[:, 0:512], l=pool.tile(sw, kc, 128),
                           r=self.fm("ctx", kc, qc * 512, 512), st=(kc == 0), sp=(kc == NC - 1):
                           t.matmul(o, l, r, start=st, stop=sp),
                           deps=[tw, vdep], inc=(kc == NC - 1))
                ttok = self.p.tok("T")
                xa = self.fm("x_acc", mc, qc * 512, 512)
                if out_mode == "acc":
                    if bo_col is not None:
                        self.Ve(lambda v, o=xa, a=bank[:, 0:512], bc=self.pprow(bo_col + mc), x=xa:
                                v.scalar_tensor_tensor(o, a, bc, x, ALU.add, ALU.add),
                                deps=[ttok], inc=True)
                    else:
                        self.Ve(lambda v, o=xa, a=bank[:, 0:512], x=xa:
                                v.tensor_add(o, a, x), deps=[ttok], inc=True)
                else:
                    oslot = self.fm(out_slot, mc, qc * 512, 512)
                    r1 = self.Ve(lambda v, o=oslot, a=bank[:, 0:512],
                                 b=self.fm(res_slot, mc, qc * 512, 512):
                                 v.tensor_add(o, a, b), deps=[ttok], inc=True)
                    self.Ve(lambda v, o=xa, a=oslot, x=xa:
                            v.tensor_add(o, a, x), deps=[r1], inc=True)
                wo_tok[bi] = self.p.tok("V")
            pool.done(sw, self.p.tok("T"))
        self.p.barrier()

    def moe_l0(self, h2_slot, sub=99):
        mk = self
        pb = self.pb
        pool = Pool(self)
        self.gel_tok = {0: None, 1: None, 2: None}
        self.ye_tok = {3: None, 4: None, 5: None, 6: None}
        sgw, tgw = pool.load(self.w["L0_ffn_gate_w"].rearrange("(c p) n -> p c n", p=128),
                             NC, 8)
        for qc in range(4):
            vdep = self.p.tok("V")
            for kc in range(NC):
                self.T(lambda t, o=pb[6][0:8, 0:512], l=pool.tile(sgw, kc, 8),
                       r=self.fm(h2_slot, kc, qc * 512, 512), st=(kc == 0), sp=(kc == NC - 1):
                       t.matmul(o, l, r, start=st, stop=sp),
                       deps=[tgw, vdep], inc=(kc == NC - 1))
            tk = self.p.tok("T")
            self.Ve(lambda v, o=self.sb["stats"].bitcast(BF16)[0:8, 2560:4608][0:8, qc * 512:qc * 512 + 512],
                    a=pb[6][0:8, 0:512], bc=self.pprow(80, 0, 8):
                    v.tensor_scalar_add(o, a, bc), deps=[tk], inc=True)
        self.p.barrier()
        if sub < 1:
            return
        for tt in range(16):
            t1 = self.T(lambda t, o=self.pb_bf7[:, 0:8],
                        i=self.sb["stats"].bitcast(BF16)[0:8, 2560:4608][0:8, tt * 128:(tt + 1) * 128],
                        d=self.sb["idbf"][0:8, 0:8]: t.transpose(o, i, d),
                        deps=[self.p.tok("V")], inc=True)
            a1 = self.Ve(lambda v, o=self.sb["stats"][:, 1024:1032], a=self.pb_bf7[:, 0:8],
                         b=self.sb["pp"][:, 83:91]: v.tensor_add(o, a, b), deps=[t1],
                         inc=True)
            a2 = self.Ve(lambda v, o=self.sb["cells"][:, 5:6], i=self.sb["stats"][:, 1024:1032]:
                         v.reduce_max(o, i, AX.X), deps=[a1], inc=True, sw=True)
            self.Ve(lambda v, o=self.sb["mtok"][:, 0:8], i=self.sb["stats"][:, 1024:1032],
                    sc=self.sb["cells"][:, 5:6]:
                    v.tensor_single_scalar(o, i, sc, ALU.is_equal), deps=[a2],
                    inc=True, sw=True)
            v1 = self.p.tok("V")
            t2 = self.T(lambda t, o=self.pb_bf7[0:8, 128:256], i=self.sb["mtok"][:, 0:8],
                        d=self.sb["idbf"][:, :]: t.transpose(o, i, d), deps=[v1], inc=True)
            self.Ve(lambda v, o=self.sb["stats"].bitcast(BF16)[0:8, 2560:4608][0:8, tt * 128:(tt + 1) * 128],
                    i=self.pb_bf7[0:8, 128:256]: v.tensor_copy(o, i), deps=[t2], inc=True)
        self.p.barrier()
        if sub < 2:
            return
        for e in range(min(sub - 1, 8)):
            sel_tok = self.make_sel(e)
            w1r = self.w["L0_ffn_w1"].rearrange("e (c p) n -> e p c n", p=128)
            w2r = self.w["L0_ffn_w2"].rearrange("e (c p) n -> e p c n", p=128)
            for tb in range(2):
                heb_free = self.p.tok("T")
                for colh in range(2):
                    col = tb * 1024 + colh * 512
                    tbc = self.T(lambda t, o=pb[7][:, 0:512],
                                 l=self.sb["stats"].bitcast(BF16)[0:8, 2432:2560][0:8, 0:128],
                                 r=self.sb["stats"].bitcast(BF16)[0:8, 2560:4608][0:8, col:col + 512]:
                                 t.matmul(o, l, r, start=True, stop=True),
                                 deps=[self.p.tok("V"), sel_tok], inc=True)
                    for kc in range(NC):
                        self.Ve(lambda v,
                                o=self.sb["scr"][:, SCR_MH2 + kc * 1024 + colh * 512:
                                                 SCR_MH2 + kc * 1024 + colh * 512 + 512],
                                a=self.fm(h2_slot, kc, col, 512), b=pb[7][:, 0:512]:
                                v.tensor_mul(o, a, b), deps=[tbc],
                                inc=(kc == NC - 1))
                mh2_tok = self.p.tok("V")
                if sub == 2:
                    continue
                for hp in range(16):
                    sw, tw = pool.load(w1r[e, :, :, hp * 256:(hp + 1) * 256], NC, 256)
                    for hsub in range(2):
                        hid = hp * 2 + hsub
                        for colh in range(2):
                            bi = (2 * hsub + colh) % 3
                            bank = pb[bi]
                            gel_dep = self.gel_tok[bi]
                            for kc in range(NC):
                                self.T(lambda t, o=bank[:, 0:512],
                                       l=pool.tile(sw, kc, 256)[:, hsub * 128:(hsub + 1) * 128],
                                       r=self.sb["scr"][:, SCR_MH2 + kc * 1024 + colh * 512:
                                                        SCR_MH2 + kc * 1024 + colh * 512 + 512],
                                       st=(kc == 0), sp=(kc == NC - 1):
                                       t.matmul(o, l, r, start=st, stop=sp),
                                       deps=[tw, mh2_tok, gel_dep], inc=(kc == NC - 1))
                            tk = self.p.tok("T")
                            hb = self.heb_view(hid, colh)
                            self.Sc(lambda s, o=hb, i=bank[:, 0:512]:
                                    s.activation(o, i, AF.Gelu_apprx_tanh),
                                    deps=[tk, heb_free], inc=True)
                            self.gel_tok[bi] = self.p.tok("S")
                    pool.done(sw, self.p.tok("T"))
                heb_tok = self.p.tok("S")
                if sub == 3:
                    continue
                for dout in range(NC):
                    for hh in range(2):
                        sw, tw = pool.load(
                            w2r[e, :, hh * 16:(hh + 1) * 16, dout * 128:(dout + 1) * 128],
                            16, 128)
                        for colh in range(2):
                            bi = 3 + (2 * dout + colh) % 4
                            bank = pb[bi]
                            for hc in range(16):
                                hid = hh * 16 + hc
                                self.T(lambda t, o=bank[:, 0:512],
                                       l=pool.tile(sw, hc, 128),
                                       r=self.heb_view(hid, colh), st=(hc == 0 and hh == 0), sp=(hc == 15 and hh == 1):
                                       t.matmul(o, l, r, start=st, stop=sp),
                                       deps=[tw, heb_tok, self.ye_tok[bi]],
                                       inc=(hc == 15))
                        pool.done(sw, self.p.tok("T"))
                    for colh in range(2):
                        bi = 3 + (2 * dout + colh) % 4
                        bank = pb[bi]
                        tk = self.p.tok("T")
                        xa = self.fm("x_acc", dout, tb * 1024 + colh * 512, 512)
                        self.Ve(lambda v, o=xa, a=bank[:, 0:512], x=xa:
                                v.tensor_add(o, a, x), deps=[tk], inc=True)
                        self.ye_tok[bi] = self.p.tok("V")
        self.p.barrier()

    def heb_view(self, hid, colh):
        slot = "ctx" if hid < 16 else "xn"
        base = (hid % 16) * 1024 + colh * 512
        return self.sb[slot][:, base:base + 512]

    def moe_l1(self, h2_slot):
        mk = self
        pb = self.pb
        pool = Pool(self)
        self.sil_tok = {0: None, 1: None}
        self.mex_tok = {0: None, 1: None}
        self.ye_tok = {4: None, 5: None, 6: None}
        srw, trw = pool.load(self.w["L1_ffn_router_w"].rearrange("(c p) n -> p c n", p=128),
                             NC, 8)
        for qc in range(4):
            vdep = self.p.tok("V")
            for kc in range(NC):
                self.T(lambda t, o=pb[6][0:8, 0:512], l=pool.tile(srw, kc, 8),
                       r=self.fm(h2_slot, kc, qc * 512, 512), st=(kc == 0), sp=(kc == NC - 1):
                       t.matmul(o, l, r, start=st, stop=sp),
                       deps=[trw, vdep], inc=(kc == NC - 1))
            tk = self.p.tok("T")
            self.Ve(lambda v, o=self.sb["stats"].bitcast(BF16)[0:8, 2560:4608][0:8, qc * 512:qc * 512 + 512],
                    a=pb[6][0:8, 0:512]: v.tensor_copy(o, a), deps=[tk], inc=True)
        self.p.barrier()
        cells = self.sb["cells"]
        for tt in range(16):
            t1 = self.T(lambda t, o=self.pb_bf7[:, 0:8],
                        i=self.sb["stats"].bitcast(BF16)[0:8, 2560:4608][0:8, tt * 128:(tt + 1) * 128],
                        d=self.sb["idbf"][0:8, 0:8]: t.transpose(o, i, d),
                        deps=[self.p.tok("V")], inc=True)
            lt = self.sb["stats"][:, 1024:1032]
            c1 = self.Ve(lambda v, o=lt, i=self.pb_bf7[:, 0:8]: v.tensor_copy(o, i),
                         deps=[t1], inc=True)
            c2 = self.Ve(lambda v, o=cells[:, 6:7], i=lt:
                         v.reduce_max(o, i, AX.X, negate=True), deps=[c1], inc=True, sw=True)
            c3 = self.Sc(lambda s, o=self.sb["stats"][:, 1032:1040], i=lt, b=cells[:, 6:7]:
                         s.activation(o, i, AF.Exp, bias=b), deps=[c2], inc=True)
            et = self.sb["stats"][:, 1032:1040]
            c4 = self.Ve(lambda v, o=cells[:, 7:8], i=et: v.reduce_sum(o, i, AX.X),
                         deps=[c3], inc=True)
            c5 = self.Ve(lambda v, o=cells[:, 8:9], i=cells[:, 7:8]: v.reciprocal(o, i),
                         deps=[c4], inc=True, sw=True)
            probs = self.sb["stats"][:, 1040:1048]
            c6 = self.Ve(lambda v, o=probs, i=et, sc=cells[:, 8:9]:
                         v.tensor_single_scalar(o, i, sc, ALU.mult), deps=[c5],
                         inc=True, sw=True)
            ltb = self.sb["stats"][:, 1048:1056]
            c7 = self.Ve(lambda v, o=ltb, a=probs, b=self.sb["pp"][:, 83:91]:
                         v.tensor_add(o, a, b), deps=[c6], inc=True, sw=True)
            c8 = self.Ve(lambda v, o=cells[:, 9:10], i=ltb: v.reduce_max(o, i, AX.X),
                         deps=[c7], inc=True, sw=True)
            m1 = self.sb["stats"][:, 1056:1064]
            c9 = self.Ve(lambda v, o=m1, i=ltb, sc=cells[:, 9:10]:
                         v.tensor_single_scalar(o, i, sc, ALU.is_equal), deps=[c8],
                         inc=True, sw=True)
            ltb2 = self.sb["stats"][:, 1064:1072]
            c10 = self.Ve(lambda v, o=ltb2, a=m1, b=ltb:
                          v.scalar_tensor_tensor(o, a, -4.0, b, ALU.mult, ALU.add),
                          deps=[c9], inc=True, sw=True)
            c11 = self.Ve(lambda v, o=cells[:, 10:11], i=ltb2: v.reduce_max(o, i, AX.X),
                          deps=[c10], inc=True, sw=True)
            m2 = self.sb["stats"][:, 1072:1080]
            c12 = self.Ve(lambda v, o=m2, i=ltb2, sc=cells[:, 10:11]:
                          v.tensor_single_scalar(o, i, sc, ALU.is_equal), deps=[c11],
                          inc=True, sw=True)
            tmp8 = self.sb["stats"][:, 1080:1088]
            c13 = self.Ve(lambda v, o=tmp8, a=probs, b=m1: v.tensor_mul(o, a, b),
                          deps=[c12], inc=True, sw=True)
            c14 = self.Ve(lambda v, o=cells[:, 11:12], i=tmp8: v.reduce_sum(o, i, AX.X),
                          deps=[c13], inc=True, sw=True)
            c15 = self.Ve(lambda v, o=tmp8, a=probs, b=m2: v.tensor_mul(o, a, b),
                          deps=[c14], inc=True, sw=True)
            c16 = self.Ve(lambda v, o=cells[:, 12:13], i=tmp8: v.reduce_sum(o, i, AX.X),
                          deps=[c15], inc=True, sw=True)
            c17 = self.Ve(lambda v, o=cells[:, 13:14], a=cells[:, 11:12], b=cells[:, 12:13]:
                          v.tensor_sub(o, a, b), deps=[c16], inc=True, sw=True)
            c18 = self.Sc(lambda s, o=cells[:, 14:15], i=cells[:, 13:14]:
                          s.activation(o, i, AF.Sigmoid), deps=[c17], inc=True)
            c19 = self.Ve(lambda v, o=cells[:, 15:16], i=cells[:, 14:15]:
                          v.tensor_scalar(o, i, -1.0, 1.0, ALU.mult, ALU.add),
                          deps=[c18], inc=True)
            cm = self.sb["mtok"][:, 0:8]
            c20 = self.Ve(lambda v, o=cm, i=m1, sc=cells[:, 14:15]:
                          v.tensor_single_scalar(o, i, sc, ALU.mult), deps=[c19],
                          inc=True, sw=True)
            self.Ve(lambda v, o=cm, a=m2, sc=cells[:, 15:16], b=cm:
                    v.scalar_tensor_tensor(o, a, sc, b, ALU.mult, ALU.add),
                    deps=[c20], inc=True, sw=True)
            v1 = self.p.tok("V")
            t2 = self.T(lambda t, o=self.pb_bf7[0:8, 128:256], i=cm,
                        d=self.sb["idbf"][:, :]: t.transpose(o, i, d), deps=[v1], inc=True)
            self.Ve(lambda v, o=self.sb["stats"].bitcast(BF16)[0:8, 2560:4608][0:8, tt * 128:(tt + 1) * 128],
                    i=self.pb_bf7[0:8, 128:256]: v.tensor_copy(o, i), deps=[t2], inc=True)
        self.p.barrier()
        egr = self.w["L1_ffn_e_gate"].rearrange("e (c p) n -> e p c n", p=128)
        eur = self.w["L1_ffn_e_up"].rearrange("e (c p) n -> e p c n", p=128)
        edr = self.w["L1_ffn_e_down"].rearrange("e (c p) n -> e p c n", p=128)
        shg = self.w["L1_ffn_sh_gate"].rearrange("(c p) n -> p c n", p=128)
        shu = self.w["L1_ffn_sh_up"].rearrange("(c p) n -> p c n", p=128)
        shd = self.w["L1_ffn_sh_down"].rearrange("(c p) n -> p c n", p=128)
        for e in range(9):
            sel_tok = self.make_sel(e) if e < 8 else None
            for tbh in range(2):
                if e < 8:
                    for colh in range(2):
                        col = tbh * 1024 + colh * 512
                        tbc = self.T(lambda t, o=pb[7][:, 0:512],
                                     l=self.sb["stats"].bitcast(BF16)[0:8, 2432:2560][0:8, 0:128],
                                     r=self.sb["stats"].bitcast(BF16)[0:8, 2560:4608][0:8, col:col + 512]:
                                     t.matmul(o, l, r, start=True, stop=True),
                                     deps=[self.p.tok("V"), sel_tok], inc=True)
                        for kc in range(NC):
                            self.Ve(lambda v,
                                    o=self.sb["xn"][:, kc * 1024 + colh * 512:
                                                    kc * 1024 + colh * 512 + 512],
                                    a=self.fm(h2_slot, kc, col, 512), b=pb[7][:, 0:512]:
                                    v.tensor_mul(o, a, b), deps=[tbc],
                                    inc=(kc == NC - 1))
                mup_tok = self.p.tok("V")
                for ffp in range(8):
                    if e < 8:
                        sg_, tg_ = pool.load(egr[e, :, :, ffp * 256:(ffp + 1) * 256], NC, 256)
                        su_, tu_ = pool.load(eur[e, :, :, ffp * 256:(ffp + 1) * 256], NC, 256)
                    else:
                        sg_, tg_ = pool.load(shg[:, :, ffp * 256:(ffp + 1) * 256], NC, 256)
                        su_, tu_ = pool.load(shu[:, :, ffp * 256:(ffp + 1) * 256], NC, 256)
                    for sub in range(2):
                        ffc = ffp * 2 + sub
                        for colh in range(2):
                            gbank = pb[colh]
                            ubank = pb[2 + colh]
                            sil_dep = self.sil_tok[colh]
                            mex_dep = self.mex_tok[colh]
                            for kc in range(NC):
                                self.T(lambda t, o=gbank[:, 0:512],
                                       l=pool.tile(sg_, kc, 256)[:, sub * 128:(sub + 1) * 128],
                                       r=self.fm(h2_slot, kc, tbh * 1024 + colh * 512, 512),
                                       st=(kc == 0), sp=(kc == NC - 1):
                                       t.matmul(o, l, r, start=st, stop=sp),
                                       deps=[tg_, sil_dep], inc=(kc == NC - 1))
                            gt = self.p.tok("T")
                            for kc in range(NC):
                                if e < 8:
                                    r = self.sb["xn"][:, kc * 1024 + colh * 512:
                                                      kc * 1024 + colh * 512 + 512]
                                else:
                                    r = self.fm(h2_slot, kc, tbh * 1024 + colh * 512, 512)
                                self.T(lambda t, o=ubank[:, 0:512],
                                       l=pool.tile(su_, kc, 256)[:, sub * 128:(sub + 1) * 128],
                                       rr=r, st=(kc == 0), sp=(kc == NC - 1):
                                       t.matmul(o, l, rr, start=st, stop=sp),
                                       deps=[tu_, mup_tok, mex_dep], inc=(kc == NC - 1))
                            ut = self.p.tok("T")
                            sgt = self.sb["scr"][:, SCR_SG + colh * 512:SCR_SG + colh * 512 + 512]
                            self.Sc(lambda s, o=sgt, i=gbank[:, 0:512]:
                                    s.activation(o, i, AF.Silu),
                                    deps=[gt, self.mex_tok[colh]], inc=True)
                            self.sil_tok[colh] = self.p.tok("S")
                            st_ = self.sil_tok[colh]
                            self.Ve(lambda v, o=self.mexp_view(ffc, colh),
                                    a=ubank[:, 0:512], b=sgt:
                                    v.tensor_mul(o, a, b), deps=[ut, st_], inc=True)
                            self.mex_tok[colh] = self.p.tok("V")
                    pool.done(sg_, self.p.tok("T"))
                    pool.done(su_, self.p.tok("T"))
                mexp_tok = self.p.tok("V")
                for dout in range(NC):
                    if e < 8:
                        sw, tw = pool.load(edr[e, :, :, dout * 128:(dout + 1) * 128], 16, 128)
                    else:
                        sw, tw = pool.load(shd[:, :, dout * 128:(dout + 1) * 128], 16, 128)
                    for colh in range(2):
                        bi = 4 + (2 * dout + colh) % 3
                        bank = pb[bi]
                        for ffc in range(16):
                            self.T(lambda t, o=bank[:, 0:512], l=pool.tile(sw, ffc, 128),
                                   r=self.mexp_view(ffc, colh), st=(ffc == 0), sp=(ffc == 15):
                                   t.matmul(o, l, r, start=st, stop=sp),
                                   deps=[tw, mexp_tok, self.ye_tok[bi]], inc=(ffc == 15))
                        tk = self.p.tok("T")
                        xa = self.fm("x_acc", dout, tbh * 1024 + colh * 512, 512)
                        self.Ve(lambda v, o=xa, a=bank[:, 0:512], x=xa:
                                v.tensor_add(o, a, x), deps=[tk], inc=True)
                        self.ye_tok[bi] = self.p.tok("V")
                    pool.done(sw, self.p.tok("T"))
        self.p.barrier()

    def mexp_view(self, ffc, colh):
        return self.sb["ctx"][:, ffc * 1024 + colh * 512: ffc * 1024 + colh * 512 + 512]


    def layer0(self, upto=99):
        w = self.w
        self.layernorm(None, "xn", "plain", src_f32="x_acc")
        self.dump_slot("xn0", "xn")
        if "stats" in self.dump:
            self.p.barrier()
            self.Q(lambda q, o=self.dump["stats"], s=self.sb["stats"][:, :]:
                   q.dma_start(out=o, in_=s), inc=True, dma=True)
            self.p.barrier()
        if upto < 2:
            return
        wq = lambda p: w["L0_attn_Wq"].rearrange("(c p) n -> p c n", p=128)[:, :, p * 128:(p + 1) * 128]
        wk = lambda p: w["L0_attn_Wk"].rearrange("(c p) n -> p c n", p=128)[:, :, p * 128:(p + 1) * 128]
        wv = lambda p: w["L0_attn_Wv"].rearrange("(c p) n -> p c n", p=128)[:, :, p * 128:(p + 1) * 128]
        wo = lambda m: w["L0_attn_Wo"].rearrange("(c p) n -> p c n", p=128)[:, :, m * 128:(m + 1) * 128]
        self.attention("xn", wq, wk, wv, wo, scale=D ** -0.5, causal=True, pos=True,
                       out_mode="acc")
        self.dump_slot("acc_a0", "x_acc")
        if upto < 3:
            return
        self.layernorm("xn", "hA", "affine", w_col=0, b_col=8, eps_col=81)
        self.dump_slot("h1_l0", "hA")
        if upto < 4:
            return
        qkv = w["L0_ffn_w_qkv"].rearrange("(c p) n -> p c n", p=128)
        mq = lambda p: qkv[:, :, p * 128:(p + 1) * 128]
        mkk = lambda p: qkv[:, :, 1024 + p * 128: 1024 + (p + 1) * 128]
        mv = lambda p: qkv[:, :, 2048 + p * 128: 2048 + (p + 1) * 128]
        mwo = lambda m: w["L0_ffn_w_proj"].rearrange("(c p) n -> p c n", p=128)[:, :, m * 128:(m + 1) * 128]
        self.attention("hA", mq, mkk, mv, mwo, scale=0.125, causal=False, pos=False,
                       out_mode="res", out_slot="xn", res_slot="xn")
        self.dump_slot("y1", "xn")
        if upto < 5:
            return
        self.layernorm("xn", "hA", "affine", w_col=16, b_col=24, eps_col=81)
        self.dump_slot("h2_l0", "hA")
        if upto < 6:
            return
        self.moe_l0("hA", sub=(upto - 5 if upto < 99 else 99))
        self.dump_slot("x1", "x_acc")

    def layer1(self, upto=99):
        w = self.w
        self.layernorm(None, "xn", "plain", src_f32="x_acc")
        self.dump_slot("l1_xn", "xn")
        if upto < 2:
            return
        qkv = w["L1_attn_Wqkv"].rearrange("(c p) n -> p c n", p=128)
        mq = lambda p: qkv[:, :, p * 128:(p + 1) * 128]
        mkk = lambda p: qkv[:, :, 1024 + p * 128: 1024 + (p + 1) * 128]
        mv = lambda p: qkv[:, :, 2048 + p * 128: 2048 + (p + 1) * 128]
        mwo = lambda m: w["L1_attn_Wproj"].rearrange("(c p) n -> p c n", p=128)[:, :, m * 128:(m + 1) * 128]
        vb = lambda p: self.srow(0, 3072 + p * 128, 128)
        self.Q(lambda q, o=self.srow(0, 3072, 1024), i=self.w["L1_attn_bqkv"][None, 2048:3072]:
               q.dma_start(out=o, in_=i), inc=True, dma=True)
        self.p.barrier()
        self.attention("xn", mq, mkk, mv, mwo, scale=0.5, causal=False, pos=False,
                       bq_col=48, bk_col=56, vbias_row=vb, bo_col=72, out_mode="acc")
        self.dump_slot("l1_acc", "x_acc")
        if upto < 3:
            return
        self.layernorm("xn", "hA", "rms", w_col=32, eps_col=82)
        self.dump_slot("l1_h", "hA")
        if upto < 4:
            return
        fq = lambda p: w["L1_ffn_wq"].rearrange("(c p) n -> p c n", p=128)[:, :, p * 128:(p + 1) * 128]
        fk = lambda p: w["L1_ffn_wk"].rearrange("(c p) n -> p c n", p=128)[:, :, p * 128:(p + 1) * 128]
        fv = lambda p: w["L1_ffn_wv"].rearrange("(c p) n -> p c n", p=128)[:, :, p * 128:(p + 1) * 128]
        fwo = lambda m: w["L1_ffn_wo"].rearrange("(c p) n -> p c n", p=128)[:, :, m * 128:(m + 1) * 128]
        self.attention("hA", fq, fk, fv, fwo, scale=0.125, causal=False, pos=False,
                       out_mode="res", out_slot="xn", res_slot="xn")
        self.dump_slot("l1_x2", "xn")
        if upto < 5:
            return
        self.layernorm("xn", "hA", "rms", w_col=40, eps_col=82)
        self.dump_slot("l1_h2", "hA")
        if upto < 6:
            return
        self.moe_l1("hA")


def build_nc(layers=(0, 1), dump=(), upto=99):
    nc = bass.Bass("TRN2", target_bir_lowering=False,
                   dynamic_dma_scratch_size=16384)
    prog = Prog()
    w = {}
    shapes = {
        "L0_attn_Wq": [D, D], "L0_attn_Wk": [D, D], "L0_attn_Wv": [D, D],
        "L0_attn_Wo": [D, D], "L0_ffn_w_qkv": [D, 3 * D], "L0_ffn_w_proj": [D, D],
        "L0_ffn_gate_w": [D, 8], "L0_ffn_w1": [8, D, 4 * D], "L0_ffn_w2": [8, 4 * D, D],
        "L1_attn_Wqkv": [D, 3 * D], "L1_attn_bqkv": [3 * D], "L1_attn_Wproj": [D, D],
        "L1_ffn_wq": [D, D], "L1_ffn_wk": [D, D], "L1_ffn_wv": [D, D],
        "L1_ffn_wo": [D, D], "L1_ffn_router_w": [D, 8],
        "L1_ffn_sh_gate": [D, 2048], "L1_ffn_sh_up": [D, 2048],
        "L1_ffn_sh_down": [2048, D],
        "L1_ffn_e_gate": [8, D, 2048], "L1_ffn_e_up": [8, D, 2048],
        "L1_ffn_e_down": [8, 2048, D],
    }
    need = set()
    if 0 in layers:
        need |= {k for k in shapes if k.startswith("L0")}
    if 1 in layers:
        need |= {k for k in shapes if k.startswith("L1")}
    for k in sorted(need):
        w[k] = nc.dram_tensor(k, shapes[k], F32, kind="ExternalInput").ap()
    xin = nc.dram_tensor("xT", [D, S], F32, kind="ExternalInput").ap()
    pos2_in = nc.dram_tensor("pos2c", [128, S], F32, kind="ExternalInput").ap()
    pp_in = nc.dram_tensor("ppc", [128, 96], F32, kind="ExternalInput").ap()
    w["onesrc"] = nc.dram_tensor("onesrc", [1, 128], F32, kind="ExternalInput").ap()
    out = nc.dram_tensor("out", [D, S], F32, kind="ExternalOutput").ap()
    dump_t = {}
    for name in dump:
        if name == "bnd":
            dump_t[name] = nc.dram_tensor("DUMP_bnd", [128, 96], F32,
                                          kind="ExternalOutput").ap()
            continue
        if name == "qkv0":
            dump_t[name] = nc.dram_tensor("DUMP_qkv0", [128, 8192], BF16,
                                          kind="ExternalOutput").ap()
            continue
        if name == "stats":
            dump_t[name] = nc.dram_tensor("DUMP_stats", [128, 4096], F32,
                                          kind="ExternalOutput").ap()
            continue
        dt = F32 if name in ("acc_a0", "x1", "accpre", "l1_acc") else BF16
        dump_t[name] = nc.dram_tensor("DUMP_" + name, [128, 16384], dt,
                                      kind="ExternalOutput").ap()

    import contextlib
    with contextlib.ExitStack() as st:
        sb = {}
        def sbt(name, shape, dtype):
            sb[name] = st.enter_context(nc.sbuf_tensor(name, shape, dtype))
        sbt("x_acc", [128, 16384], F32)
        sbt("xn", [128, 16384], BF16)
        sbt("hA", [128, 16384], BF16)
        sbt("ctx", [128, 16384], BF16)
        sbt("scr", [128, SCR_COLS], BF16)
        sbt("wpan", [128, WPAN_SLOT * WPAN_NSLOT], BF16)
        sbt("stats", [128, 4096], F32)
        sbt("idbf", [128, 128], BF16)
        sbt("onesf", [64, 128], F32)
        sbt("pp", [128, 96], F32)
        sbt("onesb", [128, 128], BF16)
        sbt("negB", [128, 1], F32)
        sbt("cells", [128, 16], F32)
        sbt("mtok", [128, 8], BF16)
        sbt("cb", [128, 8], BF16)
        pb = [st.enter_context(nc.psum_tensor(f"pb{i}", [128, 512], F32))
              for i in range(8)]
        sems = {e: st.enter_context(nc.semaphore(f"sem{e}")) for e in Prog.SEMS}

        mk = MK(nc, prog, sb, pb, w, dump_t)
        mk.pb_bf7 = pb[7].bitcast(BF16)
        mk.pb_bf6 = pb[6].bitcast(BF16)
        mk.pos2v = sb["hA"].bitcast(F32)[:, 0:S]
        mk.Q(lambda q, o=sb["x_acc"][:, :].rearrange("p (c t) -> p c t", c=NC),
             i=xin.rearrange("(c p) t -> p c t", p=128): q.dma_start(out=o, in_=i),
             inc=True, dma=True)
        mk.Q(lambda q, o=mk.pos2v, i=pos2_in: q.dma_start(out=o, in_=i),
             inc=True, dma=True)
        mk.Q(lambda q, o=sb["pp"][:, :], i=pp_in: q.dma_start(out=o, in_=i),
             inc=True, dma=True)
        mk.G(lambda g: g.memset(sb["onesb"][:, :], 1.0))
        mk.G(lambda g: g.memset(sb["onesf"][:, :], 1.0))
        g1 = mk.G(lambda g: g.memset(sb["idbf"][:, :], 1.0), inc=True)
        mk.p.emit("G", lambda g: g.affine_select(sb["idbf"][:, :], sb["idbf"][:, :],
                                                 [[1, 128]], ALU.is_equal, 0.0,
                                                 base=0, channel_multiplier=-1),
                  deps=[g1], inc=True, self_wait=True)
        prog.barrier()
        if 0 in layers:
            mk.layer0(upto)
        if 1 in layers:
            mk.layer1(upto if layers == (1,) else 99)
        prog.barrier()
        mk.Q(lambda q, o=out.rearrange("(c p) t -> p c t", p=128),
             i=sb["x_acc"][:, :].rearrange("p (c t) -> p c t", c=NC):
             q.dma_start(out=o, in_=i), inc=True, dma=True)
        mk.Q(lambda q: q.nop(), deps=[prog.tok("Q")], self_wait=True)

        with nc.Block() as block:
            def runner(engname):
                def body(eng):
                    ops = prog.ops[engname]
                    sem_map = sems
                    for fn, waits, incn, semname in ops:
                        for (sname, v) in waits:
                            eng.wait_ge(sem_map[sname], v)
                        inst = fn(eng)
                        if incn:
                            inst.then_inc(sem_map[semname], incn)
                return body
            block.tensor(runner("T"))
            block.scalar(runner("S"))
            block.vector(runner("V"))
            block.gpsimd(runner("G"))
            block.sync(runner("Q"))
    return nc



def _host_consts():
    idx = np.arange(S, dtype=np.float64)
    dim = 64
    eps = float(np.finfo(np.float32).eps)
    scale = 1.0 / (10000.0 ** (np.arange(0, dim, 2, dtype=np.float64) /
                               np.float64(dim)) + eps)
    idx32 = np.arange(S, dtype=np.float32)
    sc32 = (1.0 / (10000.0 ** (np.arange(0, dim, 2, dtype=np.float32) / np.float32(dim))
                   + np.float32(eps))).astype(np.float32)
    ang = idx32[:, None] * sc32[None, :]
    pos = np.stack([np.sin(ang), np.cos(ang)], axis=-1).reshape(S, dim).astype(np.float32)
    pos2 = np.concatenate([pos.T, pos.T], axis=0).astype(np.float32)
    return pos2


def _pack_pp(inputs):
    pp = np.zeros((128, 96), np.float32)
    def put(col, vec):
        v = np.asarray(vec, np.float32).reshape(-1)
        n = v.size // 128
        pp[:, col:col + n] = v.reshape(n, 128).T
    put(0, inputs["L0_ffn_ln1_w"]); put(8, inputs["L0_ffn_ln1_b"])
    put(16, inputs["L0_ffn_ln2_w"]); put(24, inputs["L0_ffn_ln2_b"])
    put(32, inputs["L1_ffn_norm1_w"]); put(40, inputs["L1_ffn_norm2_w"])
    put(48, inputs["L1_attn_bqkv"]); put(72, inputs["L1_attn_bproj"])
    pp[0:8, 80] = np.asarray(inputs["L0_ffn_gate_b"], np.float32)
    pp[:, 81] = 1e-5
    pp[:, 82] = 1e-6
    for e in range(8):
        pp[:, 83 + e] = -e * 1e-6
    return pp


_WEIGHT_KEYS = [
    "L0_attn_Wq", "L0_attn_Wk", "L0_attn_Wv", "L0_attn_Wo",
    "L0_ffn_w_qkv", "L0_ffn_w_proj", "L0_ffn_gate_w", "L0_ffn_w1", "L0_ffn_w2",
    "L1_attn_Wqkv", "L1_attn_bqkv", "L1_attn_Wproj",
    "L1_ffn_wq", "L1_ffn_wk", "L1_ffn_wv", "L1_ffn_wo", "L1_ffn_router_w",
    "L1_ffn_sh_gate", "L1_ffn_sh_up", "L1_ffn_sh_down",
    "L1_ffn_e_gate", "L1_ffn_e_up", "L1_ffn_e_down",
]


def kernel(**inputs):
    x = np.asarray(inputs["L0_attn_x"], np.float32)
    pos2 = _host_consts()
    pp = _pack_pp(inputs)
    warr = {k: np.ascontiguousarray(np.asarray(inputs[k], np.float32))
            for k in _WEIGHT_KEYS}
    xTs = [np.ascontiguousarray(x[i].T) for i in range(NCORES)]
    consts = {"pos2c": pos2, "ppc": pp,
              "onesrc": np.ones((1, 128), np.float32)}

    if FUSE:
        nc = build_nc(layers=(0, 1))
        in_maps = []
        for i in range(NCORES):
            m = dict(consts)
            m.update(warr)
            m["xT"] = xTs[i]
            in_maps.append(m)
        res = run_bass_kernel_spmd(nc, in_maps, core_ids=list(range(NCORES)))
        outs = [res.results[i]["out"] for i in range(NCORES)]
    else:
        nc0 = build_nc(layers=(0,))
        in_maps = []
        w0 = {k: v for k, v in warr.items() if k.startswith("L0")}
        for i in range(NCORES):
            m = dict(consts)
            m.update(w0)
            m["xT"] = xTs[i]
            in_maps.append(m)
        res0 = run_bass_kernel_spmd(nc0, in_maps, core_ids=list(range(NCORES)))
        mids = [np.asarray(res0.results[i]["out"]) for i in range(NCORES)]
        nc1 = build_nc(layers=(1,))
        w1 = {k: v for k, v in warr.items() if k.startswith("L1")}
        in_maps = []
        for i in range(NCORES):
            m = dict(consts)
            m.update(w1)
            m["xT"] = mids[i]
            in_maps.append(m)
        res = run_bass_kernel_spmd(nc1, in_maps, core_ids=list(range(NCORES)))
        outs = [res.results[i]["out"] for i in range(NCORES)]

    return np.stack([np.asarray(o).T for o in outs], axis=0).astype(np.float32)


# version 09
# speedup vs baseline: 1.3776x; 1.0101x over previous
import numpy as np
import concourse.bass as bass
import concourse.mybir as mybir
from concourse.bass_utils import run_bass_kernel_spmd

F32 = mybir.dt.float32
BF16 = mybir.dt.bfloat16
AF = mybir.ActivationFunctionType
ALU = mybir.AluOpType
AX = mybir.AxisListType

D = 1024
S = 2048
NC = 8
NCORES = 8
HPAIRS = 8
FUSE = True

SCR_Q = 0
SCR_K = 2048
SCR_V = 4096
SCR_PROBS = 6176
SCR_MH2 = 0
SCR_SG = 6176
SCR_COLS = 6176 + 2 * 2 * 512

WPAN_SLOT = 2048
WPAN_NSLOT = 3


class Prog:
    ENGS = "TSVGQ"
    GD = ("G0", "G1", "G2", "G3")
    QD = ("Q0", "Q1")
    SEMS = ("T", "S", "V", "Gc", "Qc") + GD + QD

    def __init__(self):
        self.ops = {e: [] for e in self.ENGS}
        self.cnt = {s: 0 for s in self.SEMS}
        self.marks = {e: dict.fromkeys(self.SEMS, 0) for e in self.ENGS}
        self.pending = {e: [] for e in self.ENGS}
        self.dmai = {"G": 0, "Q": 0}

    def emit(self, eng, fn, deps=(), inc=False, dma=False, self_wait=False):
        deps = [d for d in deps if d is not None]
        if dma:
            assert inc and eng in ("G", "Q")
            ring = self.GD if eng == "G" else self.QD
            sem = ring[self.dmai[eng] % len(ring)]
            self.dmai[eng] += 1
            deps = deps + [(sem, self.cnt[sem])]
        else:
            sem = eng + "c" if eng in ("G", "Q") else eng
        want = {}
        for d in list(self.pending[eng]) + deps:
            if d is None:
                continue
            de, dc = d
            if dc <= 0:
                continue
            if dc > self.marks[eng][de]:
                self.marks[eng][de] = dc
                want[de] = max(want.get(de, 0), dc)
        self.pending[eng] = []
        waits = sorted(want.items())
        amt = 0
        if inc:
            amt = 16 if dma else 1
            self.cnt[sem] += amt
        self.ops[eng].append([fn, waits, amt, sem])
        if inc:
            return (sem, self.cnt[sem])
        return None

    def tok(self, sem):
        if sem == "G":
            sem = "Gc"
        if sem == "Q":
            ring = self.QD
            i = (self.dmai["Q"] - 1) % len(ring) if self.dmai["Q"] else 0
            return (ring[i], self.cnt[ring[i]])
        return (sem, self.cnt[sem])

    def barrier(self):
        for e in self.ENGS:
            if self.ops[e]:
                last = self.ops[e][-1]
                if last[2] == 0:
                    sem = last[3]
                    assert sem not in self.GD + self.QD
                    last[2] = 1
                    self.cnt[sem] += last[2]
        toks = [(s, self.cnt[s]) for s in self.SEMS]
        for e in self.ENGS:
            self.pending[e] = list(toks)


class Pool:

    def __init__(self, mk, nslots=WPAN_NSLOT):
        self.mk = mk
        self.n = nslots
        self.freetok = [None] * nslots
        self.i = 0

    def load(self, in_ap, nch, ncols):
        mk = self.mk
        s = self.i % self.n
        self.i += 1
        base = s * WPAN_SLOT
        assert nch * ncols <= WPAN_SLOT
        out = mk.sb["wpan"][:, base:base + nch * ncols].rearrange(
            "p (c n) -> p c n", c=nch)
        tok = mk.p.emit("G", lambda g, o=out, i=in_ap: g.dma_start(out=o, in_=i),
                        deps=[self.freetok[s]], inc=True, dma=True)
        return s, tok

    def tile(self, s, kc, ncols):
        base = s * WPAN_SLOT
        return self.mk.sb["wpan"][:, base + kc * ncols: base + (kc + 1) * ncols]

    def done(self, s, tok):
        self.freetok[s] = tok


class MK:
    def __init__(self, nc, prog, sb, pb, w, dump):
        self.nc = nc
        self.p = prog
        self.sb = sb
        self.pb = pb
        self.w = w
        self.dump = dump

    def make_sel(self, e):
        tprev = self.p.tok("T")
        g1 = self.G(lambda g: g.memset(self.sb["stats"].bitcast(BF16)[0:8, 2432:2560][0:8, 0:128], 0.0),
                    deps=[tprev], inc=True)
        return self.p.emit("G", lambda g, o=self.sb["stats"].bitcast(BF16)[0:8, 2432:2560][e:e + 1, 0:128],
                           i=self.w["onesrc"]: g.dma_start(out=o, in_=i),
                           deps=[g1], inc=True, dma=True, self_wait=True)

    def fm(self, slot, c, t0, n, h0=0, hp=128):
        return self.sb[slot][h0:h0 + hp, c * S + t0: c * S + t0 + n]

    def pprow(self, col, p0=0, np_=128):
        return self.sb["pp"][p0:p0 + np_, col:col + 1]

    def srow(self, p, c0, n):
        return self.sb["stats"][p:p + 1, c0:c0 + n]

    def T(self, fn, deps=(), inc=False):
        return self.p.emit("T", fn, deps, inc)

    def Sc(self, fn, deps=(), inc=False, sw=False):
        return self.p.emit("S", fn, deps, inc, False, sw)

    def Ve(self, fn, deps=(), inc=False, sw=False):
        return self.p.emit("V", fn, deps, inc, False, sw)

    def G(self, fn, deps=(), inc=False, dma=False):
        return self.p.emit("G", fn, deps, inc, dma)

    def Q(self, fn, deps=(), inc=False, dma=False, self_wait=False):
        return self.p.emit("Q", fn, deps, inc, dma, self_wait)

    def dump_slot(self, name, slot, dtype_cols=16384):
        if name not in self.dump:
            return
        out = self.dump[name]
        src = self.sb[slot][:, 0:dtype_cols]
        self.p.barrier()
        self.Q(lambda q, o=out, s=src: q.dma_start(out=o, in_=s), inc=True, dma=True)
        self.p.barrier()


    def layernorm(self, src_slot, dst_slot, kind, w_col=None, b_col=None,
                  eps_col=81, src_f32=None):
        mk = self
        pb = self.pb
        if src_f32 is not None:
            for c in range(NC):
                self.Ve(lambda v, o=self.fm(dst_slot, c, 0, S),
                        i=self.fm(src_f32, c, 0, S): v.tensor_copy(o, i))
            src_b = dst_slot
        else:
            src_b = src_slot
        self.p.barrier()
        need_mean = kind != "rms"
        sq_mm_tok = [None, None]
        st_ev_tok = {}
        for qc in range(4):
            sb6 = pb[6] if qc % 2 == 0 else pb[4]
            sb5 = pb[5] if qc % 2 == 0 else pb[7]
            vdep = st_ev_tok.get(qc % 2)
            for c in range(NC):
                buf = c % 2
                sqv = self.sb["scr"][:, SCR_PROBS + buf * 512:SCR_PROBS + buf * 512 + 512]
                t = self.Sc(lambda s, o=sqv, i=self.fm(src_b, c, qc * 512, 512):
                            s.activation(o, i, AF.Square),
                            deps=[sq_mm_tok[buf]], inc=True)
                if need_mean:
                    self.T(lambda tt, o=sb6[0:1, :], l=self.sb["onesb"][:, 0:1],
                           r=self.fm(src_b, c, qc * 512, 512), st=(c == 0), sp=(c == NC - 1):
                           tt.matmul(o, l, r, start=st, stop=sp),
                           deps=[vdep])
                self.T(lambda tt, o=sb5[0:1, :], l=self.sb["onesb"][:, 0:1],
                       r=sqv, st=(c == 0), sp=(c == NC - 1):
                       tt.matmul(o, l, r, start=st, stop=sp),
                       deps=[t, vdep], inc=True)
                sq_mm_tok[buf] = self.p.tok("T")
            tk = self.p.tok("T")
            if need_mean:
                self.Ve(lambda v, o=self.srow(0, qc * 512, 512), i=sb6[0:1, 0:512]:
                        v.tensor_copy(o, i), deps=[tk])
            self.Ve(lambda v, o=self.srow(32, qc * 512, 512), i=sb5[0:1, 0:512]:
                    v.tensor_copy(o, i), deps=[tk], inc=True)
            st_ev_tok[qc % 2] = self.p.tok("V")
        self.p.barrier()
        a_row = lambda c0, n: self.srow(32, S + c0, n)
        b_row = lambda c0, n: self.srow(0, c0, n)
        if need_mean:
            k1 = self.Ve(lambda v, o=self.srow(0, 0, S), i=self.srow(0, 0, S):
                         v.tensor_scalar_mul(o, i, 1.0 / D), inc=True)
            k2 = self.Ve(lambda v, o=self.srow(0, S, S), a=self.srow(0, 0, S),
                         b=self.srow(0, 0, S): v.tensor_mul(o, a, b),
                         deps=[k1], inc=True, sw=True)
            k3 = self.Ve(lambda v, o=self.srow(32, S, S), i=self.srow(0, S, S):
                         v.tensor_copy(o, i), deps=[k2], inc=True, sw=True)
            self.Ve(lambda v, o=self.srow(32, 0, S), a=self.srow(32, 0, S),
                    b=self.srow(32, S, S):
                    v.scalar_tensor_tensor(o, a, 1.0 / D, b, ALU.mult, ALU.subtract),
                    deps=[k3], inc=True, sw=True)
        else:
            self.Ve(lambda v, o=self.srow(32, 0, S), i=self.srow(32, 0, S):
                    v.tensor_scalar_mul(o, i, 1.0 / D), inc=True)
        sv = self.p.tok("V")
        l1 = self.Sc(lambda s, o=self.srow(32, S, S), i=self.srow(32, 0, S),
                     e=self.sb["pp"][32:33, eps_col:eps_col + 1]:
                     s.activation(o, i, AF.Ln, bias=e), deps=[sv], inc=True)
        self.Sc(lambda s, o=self.srow(32, S, S), i=self.srow(32, S, S):
                s.activation(o, i, AF.Exp, scale=-0.5), deps=[l1], inc=True, sw=True)
        ss = self.p.tok("S")
        if need_mean:
            k4 = self.Ve(lambda v, o=self.srow(0, S, S), i=self.srow(32, S, S):
                         v.tensor_copy(o, i), deps=[ss], inc=True)
            self.Ve(lambda v, o=self.srow(0, 0, S), a=self.srow(0, 0, S),
                    b=self.srow(0, S, S):
                    v.scalar_tensor_tensor(o, a, -1.0, b, ALU.mult, ALU.mult),
                    deps=[k4], inc=True, sw=True)
        self.p.barrier()
        uprev = None
        for qc in range(4):
            vdep = self.p.tok("V")
            ta = self.T(lambda t, o=pb[7][:, 0:512], l=self.sb["onesf"][32:33, :],
                        r=a_row(qc * 512, 512): t.matmul(o, l, r, start=True, stop=True),
                        deps=[vdep], inc=True)
            tb = None
            if need_mean:
                tb = self.T(lambda t, o=pb[4][:, 0:512], l=self.sb["onesf"][0:1, :],
                            r=b_row(qc * 512, 512): t.matmul(o, l, r, start=True, stop=True),
                            deps=[vdep], inc=True)
            for c in range(NC):
                src = (self.fm(src_f32, c, qc * 512, 512) if src_f32 is not None
                       else self.fm(src_b, c, qc * 512, 512))
                if kind == "plain":
                    u1 = self.Ve(lambda v, o=self.sb["scr"].bitcast(F32)[:, 3088:3600], a=src, b=pb[7][:, 0:512]:
                                 v.tensor_mul(o, a, b), deps=[ta, uprev], inc=True)
                    uprev = self.Ve(lambda v, o=self.fm(dst_slot, c, qc * 512, 512),
                                    a=self.sb["scr"].bitcast(F32)[:, 3088:3600], b=pb[4][:, 0:512]:
                                    v.tensor_add(o, a, b), deps=[tb, u1], inc=True)
                elif kind == "affine":
                    u1 = self.Ve(lambda v, o=self.sb["scr"].bitcast(F32)[:, 3088:3600], a=src, b=pb[7][:, 0:512]:
                                 v.tensor_mul(o, a, b), deps=[ta, uprev], inc=True)
                    u2 = self.Ve(lambda v, o=self.sb["scr"].bitcast(F32)[:, 3088:3600],
                                 a=self.sb["scr"].bitcast(F32)[:, 3088:3600], b=pb[4][:, 0:512]:
                                 v.tensor_add(o, a, b), deps=[tb, u1], inc=True)
                    uprev = self.Ve(lambda v, o=self.fm(dst_slot, c, qc * 512, 512),
                                    a=self.sb["scr"].bitcast(F32)[:, 3088:3600], wc=self.pprow(w_col + c),
                                    bc=self.pprow(b_col + c):
                                    v.tensor_scalar(o, a, wc, bc, ALU.mult, ALU.add),
                                    deps=[u2], inc=True)
                else:
                    u1 = self.Ve(lambda v, o=self.sb["scr"].bitcast(F32)[:, 3088:3600], a=src, b=pb[7][:, 0:512]:
                                 v.tensor_mul(o, a, b), deps=[ta, uprev], inc=True)
                    uprev = self.Ve(lambda v, o=self.fm(dst_slot, c, qc * 512, 512),
                                    a=self.sb["scr"].bitcast(F32)[:, 3088:3600], wc=self.pprow(w_col + c):
                                    v.tensor_scalar_mul(o, a, wc), deps=[u1], inc=True)
        self.p.barrier()

    def attention(self, inp_slot, mkq, mkk, mkv, mkwo, scale, causal=False,
                  pos=False, bq_col=None, bk_col=None, vbias_row=None,
                  bo_col=None, out_mode="acc", out_slot=None, res_slot=None):
        mk = self
        pb = self.pb
        pool = Pool(self)
        self.sc_free = [None, None, None]
        self.probs_free = [None, None]
        self.pv_free = [None, None]
        self.sum_free = [None, None]
        self.prj_tok = {}
        self._pend_epi = None
        for pr in range(HPAIRS):
            sq, tq = pool.load(mkq(pr), NC, 128)
            sk, tk_ = pool.load(mkk(pr), NC, 128)
            sdep = self.p.tok("S")
            for which, slot_off, wsl, wtok, bcol in (
                    (0, SCR_Q, sq, tq, bq_col), (1, SCR_K, sk, tk_, bk_col)):
                for qc in range(4):
                    bi = which * 2 + (qc % 2)
                    bank = pb[bi]
                    dep_prev = self.prj_tok.get(bi)
                    for kc in range(NC):
                        self.T(lambda t, o=bank[:, 0:512],
                               l=pool.tile(wsl, kc, 128),
                               r=self.fm(inp_slot, kc, qc * 512, 512), st=(kc == 0), sp=(kc == NC - 1):
                               t.matmul(o, l, r, start=st, stop=sp),
                               deps=[wtok, dep_prev, sdep], inc=(kc == NC - 1))
                    ttok = self.p.tok("T")
                    dst = self.sb["scr"][:, slot_off + qc * 512: slot_off + qc * 512 + 512]
                    if pos:
                        self.Ve(lambda v, o=dst, a=bank[:, 0:512],
                                b=self.pos2v[:, qc * 512:qc * 512 + 512]:
                                v.tensor_add(o, a, b), deps=[ttok], inc=True)
                    elif bcol is not None:
                        self.Ve(lambda v, o=dst, a=bank[:, 0:512],
                                bc=self.pprow(bcol + pr):
                                v.tensor_scalar_add(o, a, bc), deps=[ttok], inc=True)
                    else:
                        self.Ve(lambda v, o=dst, a=bank[:, 0:512]:
                                v.tensor_copy(o, a), deps=[ttok], inc=True)
                    self.prj_tok[bi] = self.p.tok("V")
            pool.done(sq, self.p.tok("T"))
            pool.done(sk, self.p.tok("T"))
            sv, tv = pool.load(mkv(pr), NC, 128)
            if pr == 0:
                ov = self.sb["scr"][:, SCR_V:SCR_V + 2080].rearrange(
                    "p (t c) -> p t c", c=65)[:, :, 64:65]
                self.Ve(lambda v, o=ov: v.memset(o, 1.0),
                        deps=[self.p.tok("T")], inc=True)
            cells = self.sb["cells"]
            sq_jobs = [(0, SCR_Q, qc) for qc in range(4)] + [(1, SCR_K, qc) for qc in range(4)]
            sq_buf_tok = [None, None]
            cs_toks = []

            def emit_square(i):
                which, slot_off, qc = sq_jobs[i]
                buf = i % 2
                sqv = self.sb["scr"][:, SCR_PROBS + buf * 512:SCR_PROBS + buf * 512 + 512]
                t = self.Sc(lambda s, o=sqv,
                            i_=self.sb["scr"][:, slot_off + qc * 512:slot_off + qc * 512 + 512]:
                            s.activation(o, i_, AF.Square),
                            deps=[self.prj_tok.get(which * 2 + qc % 2), sdep,
                                  self.probs_free[0], self.probs_free[1],
                                  sq_buf_tok[buf]], inc=True)
                return t, sqv

            sq_emitted = [emit_square(0), emit_square(1)]

            def emit_colsum(i):
                t, sqv = sq_emitted[i]
                which, slot_off, qc = sq_jobs[i]
                p0 = 32 * which
                self.T(lambda tt, o=pb[qc][p0:p0 + 1, 0:512], l=self.sb["onesb"][:, 0:1],
                       r=sqv, tp=(0, p0): tt.matmul(o, l, r, start=True, stop=True,
                                                    tile_position=tp),
                       deps=[t, self.sc_free[qc % 3] if qc < 3 else self.sum_free[1]],
                       inc=True)
                cs_toks.append(self.p.tok("T"))
                sq_buf_tok[i % 2] = cs_toks[-1]
                if i + 2 < len(sq_jobs):
                    sq_emitted.append(emit_square(i + 2))

            for tt in range(16):
                if tt % 2 == 0 and tt // 2 < 8:
                    emit_colsum(tt // 2)
                bank = pb[4 + (tt % 2)]
                vdep = self.pv_free[tt % 2]
                last_plain = (vbias_row is None)
                for kc in range(NC):
                    self.T(lambda t, o=bank[:, 0:128],
                           l=self.fm(inp_slot, kc, tt * 128, 128),
                           r=pool.tile(sv, kc, 128), st=(kc == 0),
                           sp=(last_plain and kc == NC - 1):
                           t.matmul(o, l, r, start=st, stop=sp),
                           deps=[tv, vdep],
                           inc=(last_plain and kc == NC - 1))
                if vbias_row is not None:
                    self.T(lambda t, o=bank[:, 0:128], l=self.sb["onesf"][0:1, :],
                           r=vbias_row(pr): t.matmul(o, l, r, start=False, stop=True),
                           inc=True)
                ttok = self.p.tok("T")
                self.Ve(lambda v, o=self.sb["scr"][:, SCR_V + tt * 130: SCR_V + tt * 130 + 64],
                        a=bank[:, 0:64]: v.tensor_copy(o, a), deps=[ttok], inc=True)
                self.Ve(lambda v, o=self.sb["scr"][:, SCR_V + tt * 130 + 65: SCR_V + tt * 130 + 129],
                        a=bank[:, 64:128]: v.tensor_copy(o, a), inc=True)
                self.pv_free[tt % 2] = self.p.tok("V")
            pool.done(sv, self.p.tok("T"))
            for which in (0, 1):
                p0 = 32 * which
                mtk = None
                for qc in range(4):
                    rt = self.Ve(lambda v, o=cells[p0:p0 + 1, 8 + qc:9 + qc],
                                 i=pb[qc][p0:p0 + 1, 0:512]:
                                 v.reduce_max(o, i, AX.X),
                                 deps=[cs_toks[which * 4 + qc], mtk], inc=True, sw=True)
                    mtk = rt
                m1t = self.Ve(lambda v, o=cells[p0:p0 + 1, 0:1], a=cells[p0:p0 + 1, 8:9],
                              b=cells[p0:p0 + 1, 9:10]: v.tensor_max(o, a, b),
                              deps=[mtk], inc=True, sw=True)
                m2t = self.Ve(lambda v, o=cells[p0:p0 + 1, 12:13], a=cells[p0:p0 + 1, 10:11],
                              b=cells[p0:p0 + 1, 11:12]: v.tensor_max(o, a, b),
                              deps=[m1t], inc=True, sw=True)
                self.Ve(lambda v, o=cells[p0:p0 + 1, 0:1], a=cells[p0:p0 + 1, 0:1],
                        b=cells[p0:p0 + 1, 12:13]: v.tensor_max(o, a, b),
                        deps=[m2t], inc=True, sw=True)
            kc1 = self.Ve(lambda v, o=cells[0:1, 1:2], i=cells[32:33, 0:1]:
                          v.tensor_copy(o, i), deps=[self.p.tok("V")], inc=True, sw=True)
            s1 = self.Ve(lambda v, o=cells[0:1, 4:5], a=cells[0:1, 0:1],
                         b=cells[0:1, 1:2]: v.tensor_add(o, a, b),
                         deps=[kc1], inc=True, sw=True)
            self.Ve(lambda v, o=self.sb["stats"].bitcast(BF16)[0:8, 2560:4608][0:1, 0:1], i=cells[0:1, 4:5]:
                    v.tensor_scalar_mul(o, i, -0.5 * scale), deps=[s1], inc=True, sw=True)
            vb = self.p.tok("V")
            tb = self.T(lambda t, o=pb[6][:, 256:257], l=self.sb["onesb"][0:1, :],
                        r=self.sb["stats"].bitcast(BF16)[0:8, 2560:4608][0:1, 0:1]:
                        t.matmul(o, l, r, start=True, stop=True),
                        deps=[vb, self.sum_free[0]], inc=True)
            self.Ve(lambda v, o=self.sb["negB"][:, 0:1], i=pb[6][:, 256:257]:
                    v.tensor_copy(o, i), deps=[tb, sdep], inc=True)
            self.probs_free[0] = cs_toks[-1]
            self.probs_free[1] = cs_toks[-1]
            negB_tok = self.p.tok("V")
            if "bnd" in self.dump:
                self.p.barrier()
                self.Q(lambda q, o=self.dump["bnd"][:, pr * 4:pr * 4 + 1],
                       s=self.sb["negB"][:, 0:1]:
                       q.dma_start(out=o, in_=s, allow_slow_non_contiguous=True),
                       inc=True, dma=True)
                self.Q(lambda q, o=self.dump["bnd"][0:1, 32 + pr * 8:32 + pr * 8 + 5],
                       s=self.sb["cells"][0:1, 0:5]:
                       q.dma_start(out=o, in_=s, allow_slow_non_contiguous=True),
                       inc=True, dma=True)
                self.p.barrier()
            for hh, h0 in ((0, 0), (1, 64)):
                for qb in range(4):
                    pvb_i = qb % 2
                    pvbank = pb[4 + pvb_i]
                    par = (4 * hh + qb) % 2
                    sumbank = pb[6] if par == 0 else pb[3]
                    kmax = (4 * qb + 4) if causal else 16
                    kcs = list(range(kmax))
                    strips = [kcs[i:i + 2] for i in range(0, kmax, 2)]
                    nst = len(strips)
                    exp_toks = {}

                    def emit_se(st_i, h0=h0, qb=qb):
                        strip = strips[st_i]
                        buf = st_i % 2
                        pbase = SCR_PROBS + buf * 1024
                        toks = []
                        for j, kc in enumerate(strip):
                            scb = pb[kc % 3]
                            self.T(lambda t, o=scb[:, 0:512],
                                   l=self.sb["scr"][h0:h0 + 64, SCR_K + kc * 128:SCR_K + kc * 128 + 128],
                                   r=self.sb["scr"][h0:h0 + 64, SCR_Q + qb * 512:SCR_Q + qb * 512 + 512]:
                                   t.matmul(o, l, r, start=True, stop=True),
                                   deps=[negB_tok, self.sc_free[kc % 3]], inc=True)
                            sctok = self.p.tok("T")
                            ptile = self.sb["scr"][:, pbase + j * 512: pbase + j * 512 + 512]
                            diag = causal and (kc >= 4 * qb)
                            et = self.Sc(lambda s, o=ptile, i=scb[:, 0:512],
                                         b=self.sb["negB"][:, 0:1], sc=scale:
                                         s.activation(o, i, AF.Exp, bias=b, scale=sc),
                                         deps=[sctok, self.probs_free[buf]],
                                         inc=True)
                            self.sc_free[kc % 3] = self.p.tok("S")
                            if diag:
                                gt = self.G(lambda g, o=ptile, base=512 * qb - 128 * kc:
                                            g.affine_select(o, o, [[1, 512]], ALU.is_ge,
                                                            0.0, base=base,
                                                            channel_multiplier=-1),
                                            deps=[et], inc=True)
                                toks.append(gt)
                            else:
                                toks.append(et)
                        exp_toks[st_i] = toks

                    def emit_cspv(st_i, h0=h0, qb=qb, pvbank=pvbank, sumbank=sumbank,
                                  par=par, pvb_i=pvb_i, nst_=None):
                        strip = strips[st_i]
                        buf = st_i % 2
                        pbase = SCR_PROBS + buf * 1024
                        dep_all = list(exp_toks[st_i])
                        for j, kc in enumerate(strip):
                            ptile = self.sb["scr"][:, pbase + j * 512: pbase + j * 512 + 512]
                            first = (st_i == 0 and j == 0)
                            last = (st_i == nst - 1 and j == len(strip) - 1)
                            self.T(lambda t, o=pvbank[0:65, 0:512],
                                   l=self.sb["scr"][:, SCR_V + kc * 130 + hh * 65: SCR_V + kc * 130 + hh * 65 + 65],
                                   r=ptile, st=first, sp=last:
                                   t.matmul(o, l, r, start=st, stop=sp),
                                   deps=dep_all + [self.pv_free[pvb_i]], inc=True)
                            dep_all = []
                        self.probs_free[buf] = self.p.tok("T")

                    emit_se(0)
                    if self._pend_epi is not None:
                        self._pend_epi()
                        self._pend_epi = None
                    for st_i in range(nst):
                        if st_i + 1 < nst:
                            emit_se(st_i + 1)
                        emit_cspv(st_i)
                    tk = self.p.tok("T")

                    def epilogue(tk=tk, sumbank=sumbank, h0=h0, qb=qb, pr=pr,
                                 par=par, pvbank=pvbank, pvb_i=pvb_i):
                        e1 = self.Ve(lambda v, o=self.srow(0, 0, 512), i=pvbank[64:65, 0:512]:
                                     v.tensor_copy(o, i), deps=[tk], inc=True)
                        rc = self.Ve(lambda v, o=self.srow(0, 0, 512), i=self.srow(0, 0, 512):
                                     v.reciprocal(o, i), deps=[e1], inc=True)
                        self.Ve(lambda v, o=self.sb["stats"].bitcast(BF16)[0:8, 2560:4608][0:1, 64:576], i=self.srow(0, 0, 512):
                                v.tensor_copy(o, i), deps=[rc], inc=True)
                        rv = self.p.tok("V")
                        tbc = self.T(lambda t, o=sumbank[0:64, 0:512],
                                     l=self.sb["onesb"][0:1, 0:64],
                                     r=self.sb["stats"].bitcast(BF16)[0:8, 2560:4608][0:1, 64:576]:
                                     t.matmul(o, l, r, start=True, stop=True),
                                     deps=[rv], inc=True)
                        rb = self.Ve(lambda v, o=self.sb["stats"][0:64, 512:1024],
                                     i=sumbank[0:64, 0:512]:
                                     v.tensor_copy(o, i), deps=[tbc], inc=True)
                        self.sum_free[par] = self.p.tok("V")
                        if h0 == 0:
                            self.Ve(lambda v, o=self.fm("ctx", pr, qb * 512, 512, 0, 64),
                                    a=pvbank[0:64, 0:512],
                                    b=self.sb["stats"][0:64, 512:1024]:
                                    v.tensor_mul(o, a, b), deps=[rb], inc=True)
                        else:
                            sg = self.Ve(lambda v, o=self.sb["stats"][0:64, 2304:2816],
                                         a=pvbank[0:64, 0:512],
                                         b=self.sb["stats"][0:64, 512:1024]:
                                         v.tensor_mul(o, a, b), deps=[rb], inc=True)
                            self.Ve(lambda v, o=self.fm("ctx", pr, qb * 512, 512, 64, 64),
                                    i=self.sb["stats"][0:64, 2304:2816]:
                                    v.tensor_copy(o, i), deps=[sg], inc=True)
                        self.pv_free[pvb_i] = self.p.tok("V")

                    self._pend_epi = epilogue
            if self._pend_epi is not None:
                self._pend_epi()
                self._pend_epi = None
        self.dump_slot("ctx", "ctx")
        self.dump_slot("accpre", "x_acc")
        self.p.barrier()
        pool = Pool(self)
        wo_tok = {}
        for mc in range(NC):
            sw, tw = pool.load(mkwo(mc), NC, 128)
            for qc in range(4):
                bi = qc % 2
                bank = pb[bi]
                vdep = wo_tok.get(bi)
                for kc in range(NC):
                    self.T(lambda t, o=bank[:, 0:512], l=pool.tile(sw, kc, 128),
                           r=self.fm("ctx", kc, qc * 512, 512), st=(kc == 0), sp=(kc == NC - 1):
                           t.matmul(o, l, r, start=st, stop=sp),
                           deps=[tw, vdep], inc=(kc == NC - 1))
                ttok = self.p.tok("T")
                xa = self.fm("x_acc", mc, qc * 512, 512)
                if out_mode == "acc":
                    if bo_col is not None:
                        self.Ve(lambda v, o=xa, a=bank[:, 0:512], bc=self.pprow(bo_col + mc), x=xa:
                                v.scalar_tensor_tensor(o, a, bc, x, ALU.add, ALU.add),
                                deps=[ttok], inc=True)
                    else:
                        self.Ve(lambda v, o=xa, a=bank[:, 0:512], x=xa:
                                v.tensor_add(o, a, x), deps=[ttok], inc=True)
                else:
                    oslot = self.fm(out_slot, mc, qc * 512, 512)
                    r1 = self.Ve(lambda v, o=oslot, a=bank[:, 0:512],
                                 b=self.fm(res_slot, mc, qc * 512, 512):
                                 v.tensor_add(o, a, b), deps=[ttok], inc=True)
                    self.Ve(lambda v, o=xa, a=oslot, x=xa:
                            v.tensor_add(o, a, x), deps=[r1], inc=True)
                wo_tok[bi] = self.p.tok("V")
            pool.done(sw, self.p.tok("T"))
        self.p.barrier()

    def moe_l0(self, h2_slot, sub=99):
        mk = self
        pb = self.pb
        pool = Pool(self)
        self.gel_tok = {0: None, 1: None, 2: None}
        self.ye_tok = {3: None, 4: None, 5: None, 6: None}
        sgw, tgw = pool.load(self.w["L0_ffn_gate_w"].rearrange("(c p) n -> p c n", p=128),
                             NC, 8)
        for qc in range(4):
            vdep = self.p.tok("V")
            for kc in range(NC):
                self.T(lambda t, o=pb[6][0:8, 0:512], l=pool.tile(sgw, kc, 8),
                       r=self.fm(h2_slot, kc, qc * 512, 512), st=(kc == 0), sp=(kc == NC - 1):
                       t.matmul(o, l, r, start=st, stop=sp),
                       deps=[tgw, vdep], inc=(kc == NC - 1))
            tk = self.p.tok("T")
            self.Ve(lambda v, o=self.sb["stats"].bitcast(BF16)[0:8, 2560:4608][0:8, qc * 512:qc * 512 + 512],
                    a=pb[6][0:8, 0:512], bc=self.pprow(80, 0, 8):
                    v.tensor_scalar_add(o, a, bc), deps=[tk], inc=True)
        self.p.barrier()
        if sub < 1:
            return
        for tt in range(16):
            t1 = self.T(lambda t, o=self.pb_bf7[:, 0:8],
                        i=self.sb["stats"].bitcast(BF16)[0:8, 2560:4608][0:8, tt * 128:(tt + 1) * 128],
                        d=self.sb["idbf"][0:8, 0:8]: t.transpose(o, i, d),
                        deps=[self.p.tok("V")], inc=True)
            a1 = self.Ve(lambda v, o=self.sb["stats"][:, 1024:1032], a=self.pb_bf7[:, 0:8],
                         b=self.sb["pp"][:, 83:91]: v.tensor_add(o, a, b), deps=[t1],
                         inc=True)
            a2 = self.Ve(lambda v, o=self.sb["cells"][:, 5:6], i=self.sb["stats"][:, 1024:1032]:
                         v.reduce_max(o, i, AX.X), deps=[a1], inc=True, sw=True)
            self.Ve(lambda v, o=self.sb["mtok"][:, 0:8], i=self.sb["stats"][:, 1024:1032],
                    sc=self.sb["cells"][:, 5:6]:
                    v.tensor_single_scalar(o, i, sc, ALU.is_equal), deps=[a2],
                    inc=True, sw=True)
            v1 = self.p.tok("V")
            t2 = self.T(lambda t, o=self.pb_bf7[0:8, 128:256], i=self.sb["mtok"][:, 0:8],
                        d=self.sb["idbf"][:, :]: t.transpose(o, i, d), deps=[v1], inc=True)
            self.Ve(lambda v, o=self.sb["stats"].bitcast(BF16)[0:8, 2560:4608][0:8, tt * 128:(tt + 1) * 128],
                    i=self.pb_bf7[0:8, 128:256]: v.tensor_copy(o, i), deps=[t2], inc=True)
        self.p.barrier()
        if sub < 2:
            return
        for e in range(min(sub - 1, 8)):
            sel_tok = self.make_sel(e)
            w1r = self.w["L0_ffn_w1"].rearrange("e (c p) n -> e p c n", p=128)
            w2r = self.w["L0_ffn_w2"].rearrange("e (c p) n -> e p c n", p=128)
            for tb in range(2):
                heb_free = self.p.tok("T")
                for colh in range(2):
                    col = tb * 1024 + colh * 512
                    tbc = self.T(lambda t, o=pb[7][:, 0:512],
                                 l=self.sb["stats"].bitcast(BF16)[0:8, 2432:2560][0:8, 0:128],
                                 r=self.sb["stats"].bitcast(BF16)[0:8, 2560:4608][0:8, col:col + 512]:
                                 t.matmul(o, l, r, start=True, stop=True),
                                 deps=[self.p.tok("V"), sel_tok], inc=True)
                    for kc in range(NC):
                        self.Ve(lambda v,
                                o=self.sb["scr"][:, SCR_MH2 + kc * 1024 + colh * 512:
                                                 SCR_MH2 + kc * 1024 + colh * 512 + 512],
                                a=self.fm(h2_slot, kc, col, 512), b=pb[7][:, 0:512]:
                                v.tensor_mul(o, a, b), deps=[tbc],
                                inc=(kc == NC - 1))
                mh2_tok = self.p.tok("V")
                if sub == 2:
                    continue
                for hp in range(16):
                    sw, tw = pool.load(w1r[e, :, :, hp * 256:(hp + 1) * 256], NC, 256)
                    for hsub in range(2):
                        hid = hp * 2 + hsub
                        for colh in range(2):
                            bi = (2 * hsub + colh) % 3
                            bank = pb[bi]
                            gel_dep = self.gel_tok[bi]
                            for kc in range(NC):
                                self.T(lambda t, o=bank[:, 0:512],
                                       l=pool.tile(sw, kc, 256)[:, hsub * 128:(hsub + 1) * 128],
                                       r=self.sb["scr"][:, SCR_MH2 + kc * 1024 + colh * 512:
                                                        SCR_MH2 + kc * 1024 + colh * 512 + 512],
                                       st=(kc == 0), sp=(kc == NC - 1):
                                       t.matmul(o, l, r, start=st, stop=sp),
                                       deps=[tw, mh2_tok, gel_dep], inc=(kc == NC - 1))
                            tk = self.p.tok("T")
                            hb = self.heb_view(hid, colh)
                            self.Sc(lambda s, o=hb, i=bank[:, 0:512]:
                                    s.activation(o, i, AF.Gelu_apprx_tanh),
                                    deps=[tk, heb_free], inc=True)
                            self.gel_tok[bi] = self.p.tok("S")
                    pool.done(sw, self.p.tok("T"))
                heb_tok = self.p.tok("S")
                if sub == 3:
                    continue
                for dout in range(NC):
                    for hh in range(2):
                        sw, tw = pool.load(
                            w2r[e, :, hh * 16:(hh + 1) * 16, dout * 128:(dout + 1) * 128],
                            16, 128)
                        for colh in range(2):
                            bi = 3 + (2 * dout + colh) % 4
                            bank = pb[bi]
                            for hc in range(16):
                                hid = hh * 16 + hc
                                self.T(lambda t, o=bank[:, 0:512],
                                       l=pool.tile(sw, hc, 128),
                                       r=self.heb_view(hid, colh), st=(hc == 0 and hh == 0), sp=(hc == 15 and hh == 1):
                                       t.matmul(o, l, r, start=st, stop=sp),
                                       deps=[tw, heb_tok, self.ye_tok[bi]],
                                       inc=(hc == 15))
                        pool.done(sw, self.p.tok("T"))
                    for colh in range(2):
                        bi = 3 + (2 * dout + colh) % 4
                        bank = pb[bi]
                        tk = self.p.tok("T")
                        xa = self.fm("x_acc", dout, tb * 1024 + colh * 512, 512)
                        self.Ve(lambda v, o=xa, a=bank[:, 0:512], x=xa:
                                v.tensor_add(o, a, x), deps=[tk], inc=True)
                        self.ye_tok[bi] = self.p.tok("V")
        self.p.barrier()

    def heb_view(self, hid, colh):
        slot = "ctx" if hid < 16 else "xn"
        base = (hid % 16) * 1024 + colh * 512
        return self.sb[slot][:, base:base + 512]

    def moe_l1(self, h2_slot):
        mk = self
        pb = self.pb
        pool = Pool(self)
        self.sil_tok = {0: None, 1: None}
        self.mex_tok = {0: None, 1: None}
        self.ye_tok = {4: None, 5: None, 6: None}
        srw, trw = pool.load(self.w["L1_ffn_router_w"].rearrange("(c p) n -> p c n", p=128),
                             NC, 8)
        for qc in range(4):
            vdep = self.p.tok("V")
            for kc in range(NC):
                self.T(lambda t, o=pb[6][0:8, 0:512], l=pool.tile(srw, kc, 8),
                       r=self.fm(h2_slot, kc, qc * 512, 512), st=(kc == 0), sp=(kc == NC - 1):
                       t.matmul(o, l, r, start=st, stop=sp),
                       deps=[trw, vdep], inc=(kc == NC - 1))
            tk = self.p.tok("T")
            self.Ve(lambda v, o=self.sb["stats"].bitcast(BF16)[0:8, 2560:4608][0:8, qc * 512:qc * 512 + 512],
                    a=pb[6][0:8, 0:512]: v.tensor_copy(o, a), deps=[tk], inc=True)
        self.p.barrier()
        cells = self.sb["cells"]
        for tt in range(16):
            t1 = self.T(lambda t, o=self.pb_bf7[:, 0:8],
                        i=self.sb["stats"].bitcast(BF16)[0:8, 2560:4608][0:8, tt * 128:(tt + 1) * 128],
                        d=self.sb["idbf"][0:8, 0:8]: t.transpose(o, i, d),
                        deps=[self.p.tok("V")], inc=True)
            lt = self.sb["stats"][:, 1024:1032]
            c1 = self.Ve(lambda v, o=lt, i=self.pb_bf7[:, 0:8]: v.tensor_copy(o, i),
                         deps=[t1], inc=True)
            c2 = self.Ve(lambda v, o=cells[:, 6:7], i=lt:
                         v.reduce_max(o, i, AX.X, negate=True), deps=[c1], inc=True, sw=True)
            c3 = self.Sc(lambda s, o=self.sb["stats"][:, 1032:1040], i=lt, b=cells[:, 6:7]:
                         s.activation(o, i, AF.Exp, bias=b), deps=[c2], inc=True)
            et = self.sb["stats"][:, 1032:1040]
            c4 = self.Ve(lambda v, o=cells[:, 7:8], i=et: v.reduce_sum(o, i, AX.X),
                         deps=[c3], inc=True)
            c5 = self.Ve(lambda v, o=cells[:, 8:9], i=cells[:, 7:8]: v.reciprocal(o, i),
                         deps=[c4], inc=True, sw=True)
            probs = self.sb["stats"][:, 1040:1048]
            c6 = self.Ve(lambda v, o=probs, i=et, sc=cells[:, 8:9]:
                         v.tensor_single_scalar(o, i, sc, ALU.mult), deps=[c5],
                         inc=True, sw=True)
            ltb = self.sb["stats"][:, 1048:1056]
            c7 = self.Ve(lambda v, o=ltb, a=probs, b=self.sb["pp"][:, 83:91]:
                         v.tensor_add(o, a, b), deps=[c6], inc=True, sw=True)
            c8 = self.Ve(lambda v, o=cells[:, 9:10], i=ltb: v.reduce_max(o, i, AX.X),
                         deps=[c7], inc=True, sw=True)
            m1 = self.sb["stats"][:, 1056:1064]
            c9 = self.Ve(lambda v, o=m1, i=ltb, sc=cells[:, 9:10]:
                         v.tensor_single_scalar(o, i, sc, ALU.is_equal), deps=[c8],
                         inc=True, sw=True)
            ltb2 = self.sb["stats"][:, 1064:1072]
            c10 = self.Ve(lambda v, o=ltb2, a=m1, b=ltb:
                          v.scalar_tensor_tensor(o, a, -4.0, b, ALU.mult, ALU.add),
                          deps=[c9], inc=True, sw=True)
            c11 = self.Ve(lambda v, o=cells[:, 10:11], i=ltb2: v.reduce_max(o, i, AX.X),
                          deps=[c10], inc=True, sw=True)
            m2 = self.sb["stats"][:, 1072:1080]
            c12 = self.Ve(lambda v, o=m2, i=ltb2, sc=cells[:, 10:11]:
                          v.tensor_single_scalar(o, i, sc, ALU.is_equal), deps=[c11],
                          inc=True, sw=True)
            tmp8 = self.sb["stats"][:, 1080:1088]
            c13 = self.Ve(lambda v, o=tmp8, a=probs, b=m1: v.tensor_mul(o, a, b),
                          deps=[c12], inc=True, sw=True)
            c14 = self.Ve(lambda v, o=cells[:, 11:12], i=tmp8: v.reduce_sum(o, i, AX.X),
                          deps=[c13], inc=True, sw=True)
            c15 = self.Ve(lambda v, o=tmp8, a=probs, b=m2: v.tensor_mul(o, a, b),
                          deps=[c14], inc=True, sw=True)
            c16 = self.Ve(lambda v, o=cells[:, 12:13], i=tmp8: v.reduce_sum(o, i, AX.X),
                          deps=[c15], inc=True, sw=True)
            c17 = self.Ve(lambda v, o=cells[:, 13:14], a=cells[:, 11:12], b=cells[:, 12:13]:
                          v.tensor_sub(o, a, b), deps=[c16], inc=True, sw=True)
            c18 = self.Sc(lambda s, o=cells[:, 14:15], i=cells[:, 13:14]:
                          s.activation(o, i, AF.Sigmoid), deps=[c17], inc=True)
            c19 = self.Ve(lambda v, o=cells[:, 15:16], i=cells[:, 14:15]:
                          v.tensor_scalar(o, i, -1.0, 1.0, ALU.mult, ALU.add),
                          deps=[c18], inc=True)
            cm = self.sb["mtok"][:, 0:8]
            c20 = self.Ve(lambda v, o=cm, i=m1, sc=cells[:, 14:15]:
                          v.tensor_single_scalar(o, i, sc, ALU.mult), deps=[c19],
                          inc=True, sw=True)
            self.Ve(lambda v, o=cm, a=m2, sc=cells[:, 15:16], b=cm:
                    v.scalar_tensor_tensor(o, a, sc, b, ALU.mult, ALU.add),
                    deps=[c20], inc=True, sw=True)
            v1 = self.p.tok("V")
            t2 = self.T(lambda t, o=self.pb_bf7[0:8, 128:256], i=cm,
                        d=self.sb["idbf"][:, :]: t.transpose(o, i, d), deps=[v1], inc=True)
            self.Ve(lambda v, o=self.sb["stats"].bitcast(BF16)[0:8, 2560:4608][0:8, tt * 128:(tt + 1) * 128],
                    i=self.pb_bf7[0:8, 128:256]: v.tensor_copy(o, i), deps=[t2], inc=True)
        self.p.barrier()
        egr = self.w["L1_ffn_e_gate"].rearrange("e (c p) n -> e p c n", p=128)
        eur = self.w["L1_ffn_e_up"].rearrange("e (c p) n -> e p c n", p=128)
        edr = self.w["L1_ffn_e_down"].rearrange("e (c p) n -> e p c n", p=128)
        shg = self.w["L1_ffn_sh_gate"].rearrange("(c p) n -> p c n", p=128)
        shu = self.w["L1_ffn_sh_up"].rearrange("(c p) n -> p c n", p=128)
        shd = self.w["L1_ffn_sh_down"].rearrange("(c p) n -> p c n", p=128)
        for e in range(9):
            sel_tok = self.make_sel(e) if e < 8 else None
            for tbh in range(2):
                if e < 8:
                    for colh in range(2):
                        col = tbh * 1024 + colh * 512
                        tbc = self.T(lambda t, o=pb[7][:, 0:512],
                                     l=self.sb["stats"].bitcast(BF16)[0:8, 2432:2560][0:8, 0:128],
                                     r=self.sb["stats"].bitcast(BF16)[0:8, 2560:4608][0:8, col:col + 512]:
                                     t.matmul(o, l, r, start=True, stop=True),
                                     deps=[self.p.tok("V"), sel_tok], inc=True)
                        for kc in range(NC):
                            self.Ve(lambda v,
                                    o=self.sb["xn"][:, kc * 1024 + colh * 512:
                                                    kc * 1024 + colh * 512 + 512],
                                    a=self.fm(h2_slot, kc, col, 512), b=pb[7][:, 0:512]:
                                    v.tensor_mul(o, a, b), deps=[tbc],
                                    inc=(kc == NC - 1))
                mup_tok = self.p.tok("V")
                for ffp in range(8):
                    if e < 8:
                        sg_, tg_ = pool.load(egr[e, :, :, ffp * 256:(ffp + 1) * 256], NC, 256)
                        su_, tu_ = pool.load(eur[e, :, :, ffp * 256:(ffp + 1) * 256], NC, 256)
                    else:
                        sg_, tg_ = pool.load(shg[:, :, ffp * 256:(ffp + 1) * 256], NC, 256)
                        su_, tu_ = pool.load(shu[:, :, ffp * 256:(ffp + 1) * 256], NC, 256)
                    for sub in range(2):
                        ffc = ffp * 2 + sub
                        for colh in range(2):
                            gbank = pb[colh]
                            ubank = pb[2 + colh]
                            sil_dep = self.sil_tok[colh]
                            mex_dep = self.mex_tok[colh]
                            for kc in range(NC):
                                self.T(lambda t, o=gbank[:, 0:512],
                                       l=pool.tile(sg_, kc, 256)[:, sub * 128:(sub + 1) * 128],
                                       r=self.fm(h2_slot, kc, tbh * 1024 + colh * 512, 512),
                                       st=(kc == 0), sp=(kc == NC - 1):
                                       t.matmul(o, l, r, start=st, stop=sp),
                                       deps=[tg_, sil_dep], inc=(kc == NC - 1))
                            gt = self.p.tok("T")
                            for kc in range(NC):
                                if e < 8:
                                    r = self.sb["xn"][:, kc * 1024 + colh * 512:
                                                      kc * 1024 + colh * 512 + 512]
                                else:
                                    r = self.fm(h2_slot, kc, tbh * 1024 + colh * 512, 512)
                                self.T(lambda t, o=ubank[:, 0:512],
                                       l=pool.tile(su_, kc, 256)[:, sub * 128:(sub + 1) * 128],
                                       rr=r, st=(kc == 0), sp=(kc == NC - 1):
                                       t.matmul(o, l, rr, start=st, stop=sp),
                                       deps=[tu_, mup_tok, mex_dep], inc=(kc == NC - 1))
                            ut = self.p.tok("T")
                            sgt = self.sb["scr"][:, SCR_SG + colh * 512:SCR_SG + colh * 512 + 512]
                            self.Sc(lambda s, o=sgt, i=gbank[:, 0:512]:
                                    s.activation(o, i, AF.Silu),
                                    deps=[gt, self.mex_tok[colh]], inc=True)
                            self.sil_tok[colh] = self.p.tok("S")
                            st_ = self.sil_tok[colh]
                            self.Ve(lambda v, o=self.mexp_view(ffc, colh),
                                    a=ubank[:, 0:512], b=sgt:
                                    v.tensor_mul(o, a, b), deps=[ut, st_], inc=True)
                            self.mex_tok[colh] = self.p.tok("V")
                    pool.done(sg_, self.p.tok("T"))
                    pool.done(su_, self.p.tok("T"))
                mexp_tok = self.p.tok("V")
                for dout in range(NC):
                    if e < 8:
                        sw, tw = pool.load(edr[e, :, :, dout * 128:(dout + 1) * 128], 16, 128)
                    else:
                        sw, tw = pool.load(shd[:, :, dout * 128:(dout + 1) * 128], 16, 128)
                    for colh in range(2):
                        bi = 4 + (2 * dout + colh) % 3
                        bank = pb[bi]
                        for ffc in range(16):
                            self.T(lambda t, o=bank[:, 0:512], l=pool.tile(sw, ffc, 128),
                                   r=self.mexp_view(ffc, colh), st=(ffc == 0), sp=(ffc == 15):
                                   t.matmul(o, l, r, start=st, stop=sp),
                                   deps=[tw, mexp_tok, self.ye_tok[bi]], inc=(ffc == 15))
                        tk = self.p.tok("T")
                        xa = self.fm("x_acc", dout, tbh * 1024 + colh * 512, 512)
                        self.Ve(lambda v, o=xa, a=bank[:, 0:512], x=xa:
                                v.tensor_add(o, a, x), deps=[tk], inc=True)
                        self.ye_tok[bi] = self.p.tok("V")
                    pool.done(sw, self.p.tok("T"))
        self.p.barrier()

    def mexp_view(self, ffc, colh):
        return self.sb["ctx"][:, ffc * 1024 + colh * 512: ffc * 1024 + colh * 512 + 512]


    def layer0(self, upto=99):
        w = self.w
        self.layernorm(None, "xn", "plain", src_f32="x_acc")
        self.dump_slot("xn0", "xn")
        if "stats" in self.dump:
            self.p.barrier()
            self.Q(lambda q, o=self.dump["stats"], s=self.sb["stats"][:, :]:
                   q.dma_start(out=o, in_=s), inc=True, dma=True)
            self.p.barrier()
        if upto < 2:
            return
        wq = lambda p: w["L0_attn_Wq"].rearrange("(c p) n -> p c n", p=128)[:, :, p * 128:(p + 1) * 128]
        wk = lambda p: w["L0_attn_Wk"].rearrange("(c p) n -> p c n", p=128)[:, :, p * 128:(p + 1) * 128]
        wv = lambda p: w["L0_attn_Wv"].rearrange("(c p) n -> p c n", p=128)[:, :, p * 128:(p + 1) * 128]
        wo = lambda m: w["L0_attn_Wo"].rearrange("(c p) n -> p c n", p=128)[:, :, m * 128:(m + 1) * 128]
        self.attention("xn", wq, wk, wv, wo, scale=D ** -0.5, causal=True, pos=True,
                       out_mode="acc")
        self.dump_slot("acc_a0", "x_acc")
        if upto < 3:
            return
        self.layernorm("xn", "hA", "affine", w_col=0, b_col=8, eps_col=81)
        self.dump_slot("h1_l0", "hA")
        if upto < 4:
            return
        qkv = w["L0_ffn_w_qkv"].rearrange("(c p) n -> p c n", p=128)
        mq = lambda p: qkv[:, :, p * 128:(p + 1) * 128]
        mkk = lambda p: qkv[:, :, 1024 + p * 128: 1024 + (p + 1) * 128]
        mv = lambda p: qkv[:, :, 2048 + p * 128: 2048 + (p + 1) * 128]
        mwo = lambda m: w["L0_ffn_w_proj"].rearrange("(c p) n -> p c n", p=128)[:, :, m * 128:(m + 1) * 128]
        self.attention("hA", mq, mkk, mv, mwo, scale=0.125, causal=False, pos=False,
                       out_mode="res", out_slot="xn", res_slot="xn")
        self.dump_slot("y1", "xn")
        if upto < 5:
            return
        self.layernorm("xn", "hA", "affine", w_col=16, b_col=24, eps_col=81)
        self.dump_slot("h2_l0", "hA")
        if upto < 6:
            return
        self.moe_l0("hA", sub=(upto - 5 if upto < 99 else 99))
        self.dump_slot("x1", "x_acc")

    def layer1(self, upto=99):
        w = self.w
        self.layernorm(None, "xn", "plain", src_f32="x_acc")
        self.dump_slot("l1_xn", "xn")
        if upto < 2:
            return
        qkv = w["L1_attn_Wqkv"].rearrange("(c p) n -> p c n", p=128)
        mq = lambda p: qkv[:, :, p * 128:(p + 1) * 128]
        mkk = lambda p: qkv[:, :, 1024 + p * 128: 1024 + (p + 1) * 128]
        mv = lambda p: qkv[:, :, 2048 + p * 128: 2048 + (p + 1) * 128]
        mwo = lambda m: w["L1_attn_Wproj"].rearrange("(c p) n -> p c n", p=128)[:, :, m * 128:(m + 1) * 128]
        vb = lambda p: self.srow(0, 3072 + p * 128, 128)
        self.Q(lambda q, o=self.srow(0, 3072, 1024), i=self.w["L1_attn_bqkv"][None, 2048:3072]:
               q.dma_start(out=o, in_=i), inc=True, dma=True)
        self.p.barrier()
        self.attention("xn", mq, mkk, mv, mwo, scale=0.5, causal=False, pos=False,
                       bq_col=48, bk_col=56, vbias_row=vb, bo_col=72, out_mode="acc")
        self.dump_slot("l1_acc", "x_acc")
        if upto < 3:
            return
        self.layernorm("xn", "hA", "rms", w_col=32, eps_col=82)
        self.dump_slot("l1_h", "hA")
        if upto < 4:
            return
        fq = lambda p: w["L1_ffn_wq"].rearrange("(c p) n -> p c n", p=128)[:, :, p * 128:(p + 1) * 128]
        fk = lambda p: w["L1_ffn_wk"].rearrange("(c p) n -> p c n", p=128)[:, :, p * 128:(p + 1) * 128]
        fv = lambda p: w["L1_ffn_wv"].rearrange("(c p) n -> p c n", p=128)[:, :, p * 128:(p + 1) * 128]
        fwo = lambda m: w["L1_ffn_wo"].rearrange("(c p) n -> p c n", p=128)[:, :, m * 128:(m + 1) * 128]
        self.attention("hA", fq, fk, fv, fwo, scale=0.125, causal=False, pos=False,
                       out_mode="res", out_slot="xn", res_slot="xn")
        self.dump_slot("l1_x2", "xn")
        if upto < 5:
            return
        self.layernorm("xn", "hA", "rms", w_col=40, eps_col=82)
        self.dump_slot("l1_h2", "hA")
        if upto < 6:
            return
        self.moe_l1("hA")


def build_nc(layers=(0, 1), dump=(), upto=99):
    nc = bass.Bass("TRN2", target_bir_lowering=False,
                   dynamic_dma_scratch_size=16384)
    prog = Prog()
    w = {}
    shapes = {
        "L0_attn_Wq": [D, D], "L0_attn_Wk": [D, D], "L0_attn_Wv": [D, D],
        "L0_attn_Wo": [D, D], "L0_ffn_w_qkv": [D, 3 * D], "L0_ffn_w_proj": [D, D],
        "L0_ffn_gate_w": [D, 8], "L0_ffn_w1": [8, D, 4 * D], "L0_ffn_w2": [8, 4 * D, D],
        "L1_attn_Wqkv": [D, 3 * D], "L1_attn_bqkv": [3 * D], "L1_attn_Wproj": [D, D],
        "L1_ffn_wq": [D, D], "L1_ffn_wk": [D, D], "L1_ffn_wv": [D, D],
        "L1_ffn_wo": [D, D], "L1_ffn_router_w": [D, 8],
        "L1_ffn_sh_gate": [D, 2048], "L1_ffn_sh_up": [D, 2048],
        "L1_ffn_sh_down": [2048, D],
        "L1_ffn_e_gate": [8, D, 2048], "L1_ffn_e_up": [8, D, 2048],
        "L1_ffn_e_down": [8, 2048, D],
    }
    need = set()
    if 0 in layers:
        need |= {k for k in shapes if k.startswith("L0")}
    if 1 in layers:
        need |= {k for k in shapes if k.startswith("L1")}
    for k in sorted(need):
        w[k] = nc.dram_tensor(k, shapes[k], F32, kind="ExternalInput").ap()
    xin = nc.dram_tensor("xT", [D, S], F32, kind="ExternalInput").ap()
    pos2_in = nc.dram_tensor("pos2c", [128, S], F32, kind="ExternalInput").ap()
    pp_in = nc.dram_tensor("ppc", [128, 96], F32, kind="ExternalInput").ap()
    w["onesrc"] = nc.dram_tensor("onesrc", [1, 128], F32, kind="ExternalInput").ap()
    out = nc.dram_tensor("out", [D, S], F32, kind="ExternalOutput").ap()
    dump_t = {}
    for name in dump:
        if name == "bnd":
            dump_t[name] = nc.dram_tensor("DUMP_bnd", [128, 96], F32,
                                          kind="ExternalOutput").ap()
            continue
        if name == "qkv0":
            dump_t[name] = nc.dram_tensor("DUMP_qkv0", [128, 8192], BF16,
                                          kind="ExternalOutput").ap()
            continue
        if name == "stats":
            dump_t[name] = nc.dram_tensor("DUMP_stats", [128, 4096], F32,
                                          kind="ExternalOutput").ap()
            continue
        dt = F32 if name in ("acc_a0", "x1", "accpre", "l1_acc") else BF16
        dump_t[name] = nc.dram_tensor("DUMP_" + name, [128, 16384], dt,
                                      kind="ExternalOutput").ap()

    import contextlib
    with contextlib.ExitStack() as st:
        sb = {}
        def sbt(name, shape, dtype):
            sb[name] = st.enter_context(nc.sbuf_tensor(name, shape, dtype))
        sbt("x_acc", [128, 16384], F32)
        sbt("xn", [128, 16384], BF16)
        sbt("hA", [128, 16384], BF16)
        sbt("ctx", [128, 16384], BF16)
        sbt("scr", [128, SCR_COLS], BF16)
        sbt("wpan", [128, WPAN_SLOT * WPAN_NSLOT], BF16)
        sbt("stats", [128, 4096], F32)
        sbt("idbf", [128, 128], BF16)
        sbt("onesf", [64, 128], F32)
        sbt("pp", [128, 96], F32)
        sbt("onesb", [128, 128], BF16)
        sbt("negB", [128, 1], F32)
        sbt("cells", [128, 16], F32)
        sbt("mtok", [128, 8], BF16)
        sbt("cb", [128, 8], BF16)
        pb = [st.enter_context(nc.psum_tensor(f"pb{i}", [128, 512], F32))
              for i in range(8)]
        sems = {e: st.enter_context(nc.semaphore(f"sem{e}")) for e in Prog.SEMS}

        mk = MK(nc, prog, sb, pb, w, dump_t)
        mk.pb_bf7 = pb[7].bitcast(BF16)
        mk.pb_bf6 = pb[6].bitcast(BF16)
        mk.pos2v = sb["hA"].bitcast(F32)[:, 0:S]
        mk.Q(lambda q, o=sb["x_acc"][:, :].rearrange("p (c t) -> p c t", c=NC),
             i=xin.rearrange("(c p) t -> p c t", p=128): q.dma_start(out=o, in_=i),
             inc=True, dma=True)
        mk.Q(lambda q, o=mk.pos2v, i=pos2_in: q.dma_start(out=o, in_=i),
             inc=True, dma=True)
        mk.Q(lambda q, o=sb["pp"][:, :], i=pp_in: q.dma_start(out=o, in_=i),
             inc=True, dma=True)
        mk.G(lambda g: g.memset(sb["onesb"][:, :], 1.0))
        mk.G(lambda g: g.memset(sb["onesf"][:, :], 1.0))
        g1 = mk.G(lambda g: g.memset(sb["idbf"][:, :], 1.0), inc=True)
        mk.p.emit("G", lambda g: g.affine_select(sb["idbf"][:, :], sb["idbf"][:, :],
                                                 [[1, 128]], ALU.is_equal, 0.0,
                                                 base=0, channel_multiplier=-1),
                  deps=[g1], inc=True, self_wait=True)
        prog.barrier()
        if 0 in layers:
            mk.layer0(upto)
        if 1 in layers:
            mk.layer1(upto if layers == (1,) else 99)
        prog.barrier()
        mk.Q(lambda q, o=out.rearrange("(c p) t -> p c t", p=128),
             i=sb["x_acc"][:, :].rearrange("p (c t) -> p c t", c=NC):
             q.dma_start(out=o, in_=i), inc=True, dma=True)
        mk.Q(lambda q: q.nop(), deps=[prog.tok("Q")], self_wait=True)

        with nc.Block() as block:
            def runner(engname):
                def body(eng):
                    ops = prog.ops[engname]
                    sem_map = sems
                    for fn, waits, incn, semname in ops:
                        for (sname, v) in waits:
                            eng.wait_ge(sem_map[sname], v)
                        inst = fn(eng)
                        if incn:
                            inst.then_inc(sem_map[semname], incn)
                return body
            block.tensor(runner("T"))
            block.scalar(runner("S"))
            block.vector(runner("V"))
            block.gpsimd(runner("G"))
            block.sync(runner("Q"))
    return nc



def _host_consts():
    idx = np.arange(S, dtype=np.float64)
    dim = 64
    eps = float(np.finfo(np.float32).eps)
    scale = 1.0 / (10000.0 ** (np.arange(0, dim, 2, dtype=np.float64) /
                               np.float64(dim)) + eps)
    idx32 = np.arange(S, dtype=np.float32)
    sc32 = (1.0 / (10000.0 ** (np.arange(0, dim, 2, dtype=np.float32) / np.float32(dim))
                   + np.float32(eps))).astype(np.float32)
    ang = idx32[:, None] * sc32[None, :]
    pos = np.stack([np.sin(ang), np.cos(ang)], axis=-1).reshape(S, dim).astype(np.float32)
    pos2 = np.concatenate([pos.T, pos.T], axis=0).astype(np.float32)
    return pos2


def _pack_pp(inputs):
    pp = np.zeros((128, 96), np.float32)
    def put(col, vec):
        v = np.asarray(vec, np.float32).reshape(-1)
        n = v.size // 128
        pp[:, col:col + n] = v.reshape(n, 128).T
    put(0, inputs["L0_ffn_ln1_w"]); put(8, inputs["L0_ffn_ln1_b"])
    put(16, inputs["L0_ffn_ln2_w"]); put(24, inputs["L0_ffn_ln2_b"])
    put(32, inputs["L1_ffn_norm1_w"]); put(40, inputs["L1_ffn_norm2_w"])
    put(48, inputs["L1_attn_bqkv"]); put(72, inputs["L1_attn_bproj"])
    pp[0:8, 80] = np.asarray(inputs["L0_ffn_gate_b"], np.float32)
    pp[:, 81] = 1e-5
    pp[:, 82] = 1e-6
    for e in range(8):
        pp[:, 83 + e] = -e * 1e-6
    return pp


_WEIGHT_KEYS = [
    "L0_attn_Wq", "L0_attn_Wk", "L0_attn_Wv", "L0_attn_Wo",
    "L0_ffn_w_qkv", "L0_ffn_w_proj", "L0_ffn_gate_w", "L0_ffn_w1", "L0_ffn_w2",
    "L1_attn_Wqkv", "L1_attn_bqkv", "L1_attn_Wproj",
    "L1_ffn_wq", "L1_ffn_wk", "L1_ffn_wv", "L1_ffn_wo", "L1_ffn_router_w",
    "L1_ffn_sh_gate", "L1_ffn_sh_up", "L1_ffn_sh_down",
    "L1_ffn_e_gate", "L1_ffn_e_up", "L1_ffn_e_down",
]


def kernel(**inputs):
    x = np.asarray(inputs["L0_attn_x"], np.float32)
    pos2 = _host_consts()
    pp = _pack_pp(inputs)
    warr = {k: np.ascontiguousarray(np.asarray(inputs[k], np.float32))
            for k in _WEIGHT_KEYS}
    xTs = [np.ascontiguousarray(x[i].T) for i in range(NCORES)]
    consts = {"pos2c": pos2, "ppc": pp,
              "onesrc": np.ones((1, 128), np.float32)}

    if FUSE:
        nc = build_nc(layers=(0, 1))
        in_maps = []
        for i in range(NCORES):
            m = dict(consts)
            m.update(warr)
            m["xT"] = xTs[i]
            in_maps.append(m)
        res = run_bass_kernel_spmd(nc, in_maps, core_ids=list(range(NCORES)))
        outs = [res.results[i]["out"] for i in range(NCORES)]
    else:
        nc0 = build_nc(layers=(0,))
        in_maps = []
        w0 = {k: v for k, v in warr.items() if k.startswith("L0")}
        for i in range(NCORES):
            m = dict(consts)
            m.update(w0)
            m["xT"] = xTs[i]
            in_maps.append(m)
        res0 = run_bass_kernel_spmd(nc0, in_maps, core_ids=list(range(NCORES)))
        mids = [np.asarray(res0.results[i]["out"]) for i in range(NCORES)]
        nc1 = build_nc(layers=(1,))
        w1 = {k: v for k, v in warr.items() if k.startswith("L1")}
        in_maps = []
        for i in range(NCORES):
            m = dict(consts)
            m.update(w1)
            m["xT"] = mids[i]
            in_maps.append(m)
        res = run_bass_kernel_spmd(nc1, in_maps, core_ids=list(range(NCORES)))
        outs = [res.results[i]["out"] for i in range(NCORES)]

    return np.stack([np.asarray(o).T for o in outs], axis=0).astype(np.float32)


# version 10
# speedup vs baseline: 1.3853x; 1.0056x over previous
import numpy as np
import concourse.bass as bass
import concourse.mybir as mybir
from concourse.bass_utils import run_bass_kernel_spmd

F32 = mybir.dt.float32
BF16 = mybir.dt.bfloat16
AF = mybir.ActivationFunctionType
ALU = mybir.AluOpType
AX = mybir.AxisListType

D = 1024
S = 2048
NC = 8
NCORES = 8
HPAIRS = 8
FUSE = True

SCR_Q = 0
SCR_K = 2048
SCR_V = 4096
SCR_PROBS = 6176
SCR_MH2 = 0
SCR_SG = 6176
SCR_COLS = 6176 + 2 * 2 * 512

WPAN_SLOT = 2048
WPAN_NSLOT = 3


class Prog:
    ENGS = "TSVGQ"
    GD = ("G0", "G1", "G2", "G3")
    QD = ("Q0", "Q1")
    SEMS = ("T", "S", "V", "Gc", "Qc") + GD + QD

    def __init__(self):
        self.ops = {e: [] for e in self.ENGS}
        self.cnt = {s: 0 for s in self.SEMS}
        self.marks = {e: dict.fromkeys(self.SEMS, 0) for e in self.ENGS}
        self.pending = {e: [] for e in self.ENGS}
        self.dmai = {"G": 0, "Q": 0}

    def emit(self, eng, fn, deps=(), inc=False, dma=False, self_wait=False):
        deps = [d for d in deps if d is not None]
        if dma:
            assert inc and eng in ("G", "Q")
            ring = self.GD if eng == "G" else self.QD
            sem = ring[self.dmai[eng] % len(ring)]
            self.dmai[eng] += 1
            deps = deps + [(sem, self.cnt[sem])]
        else:
            sem = eng + "c" if eng in ("G", "Q") else eng
        want = {}
        for d in list(self.pending[eng]) + deps:
            if d is None:
                continue
            de, dc = d
            if dc <= 0:
                continue
            if dc > self.marks[eng][de]:
                self.marks[eng][de] = dc
                want[de] = max(want.get(de, 0), dc)
        self.pending[eng] = []
        waits = sorted(want.items())
        amt = 0
        if inc:
            amt = 16 if dma else 1
            self.cnt[sem] += amt
        self.ops[eng].append([fn, waits, amt, sem])
        if inc:
            return (sem, self.cnt[sem])
        return None

    def tok(self, sem):
        if sem == "G":
            sem = "Gc"
        if sem == "Q":
            ring = self.QD
            i = (self.dmai["Q"] - 1) % len(ring) if self.dmai["Q"] else 0
            return (ring[i], self.cnt[ring[i]])
        return (sem, self.cnt[sem])

    def barrier(self):
        for e in self.ENGS:
            if self.ops[e]:
                last = self.ops[e][-1]
                if last[2] == 0:
                    sem = last[3]
                    assert sem not in self.GD + self.QD
                    last[2] = 1
                    self.cnt[sem] += last[2]
        toks = [(s, self.cnt[s]) for s in self.SEMS]
        for e in self.ENGS:
            self.pending[e] = list(toks)


class Pool:

    def __init__(self, mk, nslots=WPAN_NSLOT):
        self.mk = mk
        self.n = nslots
        self.freetok = [None] * nslots
        self.i = 0

    def load(self, in_ap, nch, ncols):
        mk = self.mk
        s = self.i % self.n
        self.i += 1
        base = s * WPAN_SLOT
        assert nch * ncols <= WPAN_SLOT
        out = mk.sb["wpan"][:, base:base + nch * ncols].rearrange(
            "p (c n) -> p c n", c=nch)
        tok = mk.p.emit("G", lambda g, o=out, i=in_ap: g.dma_start(out=o, in_=i),
                        deps=[self.freetok[s]], inc=True, dma=True)
        return s, tok

    def tile(self, s, kc, ncols):
        base = s * WPAN_SLOT
        return self.mk.sb["wpan"][:, base + kc * ncols: base + (kc + 1) * ncols]

    def done(self, s, tok):
        self.freetok[s] = tok


class MK:
    def __init__(self, nc, prog, sb, pb, w, dump):
        self.nc = nc
        self.p = prog
        self.sb = sb
        self.pb = pb
        self.w = w
        self.dump = dump

    def make_sel(self, e):
        tprev = self.p.tok("T")
        g1 = self.G(lambda g: g.memset(self.sb["stats"].bitcast(BF16)[0:8, 2432:2560][0:8, 0:128], 0.0),
                    deps=[tprev], inc=True)
        return self.p.emit("G", lambda g, o=self.sb["stats"].bitcast(BF16)[0:8, 2432:2560][e:e + 1, 0:128],
                           i=self.w["onesrc"]: g.dma_start(out=o, in_=i),
                           deps=[g1], inc=True, dma=True, self_wait=True)

    def fm(self, slot, c, t0, n, h0=0, hp=128):
        return self.sb[slot][h0:h0 + hp, c * S + t0: c * S + t0 + n]

    def pprow(self, col, p0=0, np_=128):
        return self.sb["pp"][p0:p0 + np_, col:col + 1]

    def srow(self, p, c0, n):
        return self.sb["stats"][p:p + 1, c0:c0 + n]

    def T(self, fn, deps=(), inc=False):
        return self.p.emit("T", fn, deps, inc)

    def Sc(self, fn, deps=(), inc=False, sw=False):
        return self.p.emit("S", fn, deps, inc, False, sw)

    def Ve(self, fn, deps=(), inc=False, sw=False):
        return self.p.emit("V", fn, deps, inc, False, sw)

    def G(self, fn, deps=(), inc=False, dma=False):
        return self.p.emit("G", fn, deps, inc, dma)

    def Q(self, fn, deps=(), inc=False, dma=False, self_wait=False):
        return self.p.emit("Q", fn, deps, inc, dma, self_wait)

    def dump_slot(self, name, slot, dtype_cols=16384):
        if name not in self.dump:
            return
        out = self.dump[name]
        src = self.sb[slot][:, 0:dtype_cols]
        self.p.barrier()
        self.Q(lambda q, o=out, s=src: q.dma_start(out=o, in_=s), inc=True, dma=True)
        self.p.barrier()


    def layernorm(self, src_slot, dst_slot, kind, w_col=None, b_col=None,
                  eps_col=81, src_f32=None):
        mk = self
        pb = self.pb
        if src_f32 is not None:
            for c in range(NC):
                self.Ve(lambda v, o=self.fm(dst_slot, c, 0, S),
                        i=self.fm(src_f32, c, 0, S): v.tensor_copy(o, i))
            src_b = dst_slot
        else:
            src_b = src_slot
        self.p.barrier()
        need_mean = kind != "rms"
        sq_mm_tok = [None, None]
        st_ev_tok = {}
        for qc in range(4):
            sb6 = pb[6] if qc % 2 == 0 else pb[4]
            sb5 = pb[5] if qc % 2 == 0 else pb[7]
            vdep = st_ev_tok.get(qc % 2)
            for c in range(NC):
                buf = c % 2
                sqv = self.sb["scr"][:, SCR_PROBS + buf * 512:SCR_PROBS + buf * 512 + 512]
                t = self.Sc(lambda s, o=sqv, i=self.fm(src_b, c, qc * 512, 512):
                            s.activation(o, i, AF.Square),
                            deps=[sq_mm_tok[buf]], inc=True)
                if need_mean:
                    self.T(lambda tt, o=sb6[0:1, :], l=self.sb["onesb"][:, 0:1],
                           r=self.fm(src_b, c, qc * 512, 512), st=(c == 0), sp=(c == NC - 1):
                           tt.matmul(o, l, r, start=st, stop=sp),
                           deps=[vdep])
                self.T(lambda tt, o=sb5[0:1, :], l=self.sb["onesb"][:, 0:1],
                       r=sqv, st=(c == 0), sp=(c == NC - 1):
                       tt.matmul(o, l, r, start=st, stop=sp),
                       deps=[t, vdep], inc=True)
                sq_mm_tok[buf] = self.p.tok("T")
            tk = self.p.tok("T")
            if need_mean:
                self.Ve(lambda v, o=self.srow(0, qc * 512, 512), i=sb6[0:1, 0:512]:
                        v.tensor_copy(o, i), deps=[tk])
            self.Ve(lambda v, o=self.srow(32, qc * 512, 512), i=sb5[0:1, 0:512]:
                    v.tensor_copy(o, i), deps=[tk], inc=True)
            st_ev_tok[qc % 2] = self.p.tok("V")
        self.p.barrier()
        a_row = lambda c0, n: self.srow(32, S + c0, n)
        b_row = lambda c0, n: self.srow(0, c0, n)
        if need_mean:
            k1 = self.Ve(lambda v, o=self.srow(0, 0, S), i=self.srow(0, 0, S):
                         v.tensor_scalar_mul(o, i, 1.0 / D), inc=True)
            k2 = self.Ve(lambda v, o=self.srow(0, S, S), a=self.srow(0, 0, S),
                         b=self.srow(0, 0, S): v.tensor_mul(o, a, b),
                         deps=[k1], inc=True, sw=True)
            k3 = self.Ve(lambda v, o=self.srow(32, S, S), i=self.srow(0, S, S):
                         v.tensor_copy(o, i), deps=[k2], inc=True, sw=True)
            self.Ve(lambda v, o=self.srow(32, 0, S), a=self.srow(32, 0, S),
                    b=self.srow(32, S, S):
                    v.scalar_tensor_tensor(o, a, 1.0 / D, b, ALU.mult, ALU.subtract),
                    deps=[k3], inc=True, sw=True)
        else:
            self.Ve(lambda v, o=self.srow(32, 0, S), i=self.srow(32, 0, S):
                    v.tensor_scalar_mul(o, i, 1.0 / D), inc=True)
        sv = self.p.tok("V")
        l1 = self.Sc(lambda s, o=self.srow(32, S, S), i=self.srow(32, 0, S),
                     e=self.sb["pp"][32:33, eps_col:eps_col + 1]:
                     s.activation(o, i, AF.Ln, bias=e), deps=[sv], inc=True)
        self.Sc(lambda s, o=self.srow(32, S, S), i=self.srow(32, S, S):
                s.activation(o, i, AF.Exp, scale=-0.5), deps=[l1], inc=True, sw=True)
        ss = self.p.tok("S")
        if need_mean:
            k4 = self.Ve(lambda v, o=self.srow(0, S, S), i=self.srow(32, S, S):
                         v.tensor_copy(o, i), deps=[ss], inc=True)
            self.Ve(lambda v, o=self.srow(0, 0, S), a=self.srow(0, 0, S),
                    b=self.srow(0, S, S):
                    v.scalar_tensor_tensor(o, a, -1.0, b, ALU.mult, ALU.mult),
                    deps=[k4], inc=True, sw=True)
        self.p.barrier()
        uprev = None
        for qc in range(4):
            vdep = self.p.tok("V")
            ta = self.T(lambda t, o=pb[7][:, 0:512], l=self.sb["onesf"][32:33, :],
                        r=a_row(qc * 512, 512): t.matmul(o, l, r, start=True, stop=True),
                        deps=[vdep], inc=True)
            tb = None
            if need_mean:
                tb = self.T(lambda t, o=pb[4][:, 0:512], l=self.sb["onesf"][0:1, :],
                            r=b_row(qc * 512, 512): t.matmul(o, l, r, start=True, stop=True),
                            deps=[vdep], inc=True)
            for c in range(NC):
                src = (self.fm(src_f32, c, qc * 512, 512) if src_f32 is not None
                       else self.fm(src_b, c, qc * 512, 512))
                if kind == "plain":
                    u1 = self.Ve(lambda v, o=self.sb["scr"].bitcast(F32)[:, 3088:3600], a=src, b=pb[7][:, 0:512]:
                                 v.tensor_mul(o, a, b), deps=[ta, uprev], inc=True)
                    uprev = self.Ve(lambda v, o=self.fm(dst_slot, c, qc * 512, 512),
                                    a=self.sb["scr"].bitcast(F32)[:, 3088:3600], b=pb[4][:, 0:512]:
                                    v.tensor_add(o, a, b), deps=[tb, u1], inc=True)
                elif kind == "affine":
                    u1 = self.Ve(lambda v, o=self.sb["scr"].bitcast(F32)[:, 3088:3600], a=src, b=pb[7][:, 0:512]:
                                 v.tensor_mul(o, a, b), deps=[ta, uprev], inc=True)
                    u2 = self.Ve(lambda v, o=self.sb["scr"].bitcast(F32)[:, 3088:3600],
                                 a=self.sb["scr"].bitcast(F32)[:, 3088:3600], b=pb[4][:, 0:512]:
                                 v.tensor_add(o, a, b), deps=[tb, u1], inc=True)
                    uprev = self.Ve(lambda v, o=self.fm(dst_slot, c, qc * 512, 512),
                                    a=self.sb["scr"].bitcast(F32)[:, 3088:3600], wc=self.pprow(w_col + c),
                                    bc=self.pprow(b_col + c):
                                    v.tensor_scalar(o, a, wc, bc, ALU.mult, ALU.add),
                                    deps=[u2], inc=True)
                else:
                    u1 = self.Ve(lambda v, o=self.sb["scr"].bitcast(F32)[:, 3088:3600], a=src, b=pb[7][:, 0:512]:
                                 v.tensor_mul(o, a, b), deps=[ta, uprev], inc=True)
                    uprev = self.Ve(lambda v, o=self.fm(dst_slot, c, qc * 512, 512),
                                    a=self.sb["scr"].bitcast(F32)[:, 3088:3600], wc=self.pprow(w_col + c):
                                    v.tensor_scalar_mul(o, a, wc), deps=[u1], inc=True)
        self.p.barrier()

    def attention(self, inp_slot, mkq, mkk, mkv, mkwo, scale, causal=False,
                  pos=False, bq_col=None, bk_col=None, vbias_row=None,
                  bo_col=None, out_mode="acc", out_slot=None, res_slot=None):
        mk = self
        pb = self.pb
        pool = Pool(self)
        self.sc_free = [None, None, None]
        self.probs_free = [None, None]
        self.pv_free = [None, None]
        self.sum_free = [None, None]
        self.prj_tok = {}
        self.bnd_v_tok = None
        self._pend_epi = None
        for pr in range(HPAIRS):
            sq, tq = pool.load(mkq(pr), NC, 128)
            sk, tk_ = pool.load(mkk(pr), NC, 128)
            sdep = self.p.tok("S")
            for which, slot_off, wsl, wtok, bcol in (
                    (0, SCR_Q, sq, tq, bq_col), (1, SCR_K, sk, tk_, bk_col)):
                for qc in range(4):
                    bi = which * 2 + (qc % 2)
                    bank = pb[bi]
                    dep_prev = self.prj_tok.get(bi)
                    bdep = self.sc_free[bi] if bi < 3 else self.bnd_v_tok
                    for kc in range(NC):
                        self.T(lambda t, o=bank[:, 0:512],
                               l=pool.tile(wsl, kc, 128),
                               r=self.fm(inp_slot, kc, qc * 512, 512), st=(kc == 0), sp=(kc == NC - 1):
                               t.matmul(o, l, r, start=st, stop=sp),
                               deps=[wtok, dep_prev, bdep, self.bnd_v_tok],
                               inc=(kc == NC - 1))
                    ttok = self.p.tok("T")
                    dst = self.sb["scr"][:, slot_off + qc * 512: slot_off + qc * 512 + 512]
                    if pos:
                        self.Ve(lambda v, o=dst, a=bank[:, 0:512],
                                b=self.pos2v[:, qc * 512:qc * 512 + 512]:
                                v.tensor_add(o, a, b), deps=[ttok], inc=True)
                    elif bcol is not None:
                        self.Ve(lambda v, o=dst, a=bank[:, 0:512],
                                bc=self.pprow(bcol + pr):
                                v.tensor_scalar_add(o, a, bc), deps=[ttok], inc=True)
                    else:
                        self.Ve(lambda v, o=dst, a=bank[:, 0:512]:
                                v.tensor_copy(o, a), deps=[ttok], inc=True)
                    self.prj_tok[bi] = self.p.tok("V")
            pool.done(sq, self.p.tok("T"))
            pool.done(sk, self.p.tok("T"))
            sv, tv = pool.load(mkv(pr), NC, 128)
            if pr == 0:
                ov = self.sb["scr"][:, SCR_V:SCR_V + 2080].rearrange(
                    "p (t c) -> p t c", c=65)[:, :, 64:65]
                self.Ve(lambda v, o=ov: v.memset(o, 1.0),
                        deps=[self.p.tok("T")], inc=True)
            cells = self.sb["cells"]
            sq_jobs = [(0, SCR_Q, qc) for qc in range(4)] + [(1, SCR_K, qc) for qc in range(4)]
            sq_buf_tok = [None, None]
            cs_toks = []

            def emit_square(i):
                which, slot_off, qc = sq_jobs[i]
                buf = i % 2
                sqv = self.sb["scr"][:, SCR_PROBS + buf * 512:SCR_PROBS + buf * 512 + 512]
                t = self.Sc(lambda s, o=sqv,
                            i_=self.sb["scr"][:, slot_off + qc * 512:slot_off + qc * 512 + 512]:
                            s.activation(o, i_, AF.Square),
                            deps=[self.prj_tok.get(which * 2 + qc % 2), sdep,
                                  self.probs_free[0], self.probs_free[1],
                                  sq_buf_tok[buf]], inc=True)
                return t, sqv

            sq_emitted = [emit_square(0), emit_square(1)]

            def emit_colsum(i):
                t, sqv = sq_emitted[i]
                which, slot_off, qc = sq_jobs[i]
                p0 = 32 * which
                self.T(lambda tt, o=pb[qc][p0:p0 + 1, 0:512], l=self.sb["onesb"][:, 0:1],
                       r=sqv, tp=(0, p0): tt.matmul(o, l, r, start=True, stop=True,
                                                    tile_position=tp),
                       deps=[t, self.sc_free[qc % 3] if qc < 3 else self.sum_free[1]],
                       inc=True)
                cs_toks.append(self.p.tok("T"))
                sq_buf_tok[i % 2] = cs_toks[-1]
                if i + 2 < len(sq_jobs):
                    sq_emitted.append(emit_square(i + 2))

            for tt in range(16):
                if tt % 2 == 0 and tt // 2 < 8:
                    emit_colsum(tt // 2)
                bank = pb[4 + (tt % 2)]
                vdep = self.pv_free[tt % 2]
                last_plain = (vbias_row is None)
                for kc in range(NC):
                    self.T(lambda t, o=bank[:, 0:128],
                           l=self.fm(inp_slot, kc, tt * 128, 128),
                           r=pool.tile(sv, kc, 128), st=(kc == 0),
                           sp=(last_plain and kc == NC - 1):
                           t.matmul(o, l, r, start=st, stop=sp),
                           deps=[tv, vdep],
                           inc=(last_plain and kc == NC - 1))
                if vbias_row is not None:
                    self.T(lambda t, o=bank[:, 0:128], l=self.sb["onesf"][0:1, :],
                           r=vbias_row(pr): t.matmul(o, l, r, start=False, stop=True),
                           inc=True)
                ttok = self.p.tok("T")
                self.Ve(lambda v, o=self.sb["scr"][:, SCR_V + tt * 130: SCR_V + tt * 130 + 64],
                        a=bank[:, 0:64]: v.tensor_copy(o, a), deps=[ttok], inc=True)
                self.Ve(lambda v, o=self.sb["scr"][:, SCR_V + tt * 130 + 65: SCR_V + tt * 130 + 129],
                        a=bank[:, 64:128]: v.tensor_copy(o, a), inc=True)
                self.pv_free[tt % 2] = self.p.tok("V")
            pool.done(sv, self.p.tok("T"))
            for which in (0, 1):
                p0 = 32 * which
                mtk = None
                for qc in range(4):
                    rt = self.Ve(lambda v, o=cells[p0:p0 + 1, 8 + qc:9 + qc],
                                 i=pb[qc][p0:p0 + 1, 0:512]:
                                 v.reduce_max(o, i, AX.X),
                                 deps=[cs_toks[which * 4 + qc], mtk], inc=True, sw=True)
                    mtk = rt
                m1t = self.Ve(lambda v, o=cells[p0:p0 + 1, 0:1], a=cells[p0:p0 + 1, 8:9],
                              b=cells[p0:p0 + 1, 9:10]: v.tensor_max(o, a, b),
                              deps=[mtk], inc=True, sw=True)
                m2t = self.Ve(lambda v, o=cells[p0:p0 + 1, 12:13], a=cells[p0:p0 + 1, 10:11],
                              b=cells[p0:p0 + 1, 11:12]: v.tensor_max(o, a, b),
                              deps=[m1t], inc=True, sw=True)
                self.Ve(lambda v, o=cells[p0:p0 + 1, 0:1], a=cells[p0:p0 + 1, 0:1],
                        b=cells[p0:p0 + 1, 12:13]: v.tensor_max(o, a, b),
                        deps=[m2t], inc=True, sw=True)
            kc1 = self.Ve(lambda v, o=cells[0:1, 1:2], i=cells[32:33, 0:1]:
                          v.tensor_copy(o, i), deps=[self.p.tok("V")], inc=True, sw=True)
            s1 = self.Ve(lambda v, o=cells[0:1, 4:5], a=cells[0:1, 0:1],
                         b=cells[0:1, 1:2]: v.tensor_add(o, a, b),
                         deps=[kc1], inc=True, sw=True)
            self.Ve(lambda v, o=self.sb["stats"].bitcast(BF16)[0:8, 2560:4608][0:1, 0:1], i=cells[0:1, 4:5]:
                    v.tensor_scalar_mul(o, i, -0.5 * scale), deps=[s1], inc=True, sw=True)
            vb = self.p.tok("V")
            tb = self.T(lambda t, o=pb[6][:, 256:257], l=self.sb["onesb"][0:1, :],
                        r=self.sb["stats"].bitcast(BF16)[0:8, 2560:4608][0:1, 0:1]:
                        t.matmul(o, l, r, start=True, stop=True),
                        deps=[vb, self.sum_free[0]], inc=True)
            self.Ve(lambda v, o=self.sb["negB"][:, 0:1], i=pb[6][:, 256:257]:
                    v.tensor_copy(o, i), deps=[tb, sdep], inc=True)
            self.bnd_v_tok = self.p.tok("V")
            self.probs_free[0] = cs_toks[-1]
            self.probs_free[1] = cs_toks[-1]
            negB_tok = self.p.tok("V")
            if "bnd" in self.dump:
                self.p.barrier()
                self.Q(lambda q, o=self.dump["bnd"][:, pr * 4:pr * 4 + 1],
                       s=self.sb["negB"][:, 0:1]:
                       q.dma_start(out=o, in_=s, allow_slow_non_contiguous=True),
                       inc=True, dma=True)
                self.Q(lambda q, o=self.dump["bnd"][0:1, 32 + pr * 8:32 + pr * 8 + 5],
                       s=self.sb["cells"][0:1, 0:5]:
                       q.dma_start(out=o, in_=s, allow_slow_non_contiguous=True),
                       inc=True, dma=True)
                self.p.barrier()
            for hh, h0 in ((0, 0), (1, 64)):
                for qb in range(4):
                    pvb_i = qb % 2
                    pvbank = pb[4 + pvb_i]
                    par = (4 * hh + qb) % 2
                    sumbank = pb[6] if par == 0 else pb[3]
                    kmax = (4 * qb + 4) if causal else 16
                    kcs = list(range(kmax))
                    strips = [kcs[i:i + 2] for i in range(0, kmax, 2)]
                    nst = len(strips)
                    exp_toks = {}

                    def emit_se(st_i, h0=h0, qb=qb):
                        strip = strips[st_i]
                        buf = st_i % 2
                        pbase = SCR_PROBS + buf * 1024
                        toks = []
                        for j, kc in enumerate(strip):
                            scb = pb[kc % 3]
                            self.T(lambda t, o=scb[:, 0:512],
                                   l=self.sb["scr"][h0:h0 + 64, SCR_K + kc * 128:SCR_K + kc * 128 + 128],
                                   r=self.sb["scr"][h0:h0 + 64, SCR_Q + qb * 512:SCR_Q + qb * 512 + 512]:
                                   t.matmul(o, l, r, start=True, stop=True),
                                   deps=[negB_tok, self.sc_free[kc % 3]], inc=True)
                            sctok = self.p.tok("T")
                            ptile = self.sb["scr"][:, pbase + j * 512: pbase + j * 512 + 512]
                            diag = causal and (kc >= 4 * qb)
                            et = self.Sc(lambda s, o=ptile, i=scb[:, 0:512],
                                         b=self.sb["negB"][:, 0:1], sc=scale:
                                         s.activation(o, i, AF.Exp, bias=b, scale=sc),
                                         deps=[sctok, self.probs_free[buf]],
                                         inc=True)
                            self.sc_free[kc % 3] = self.p.tok("S")
                            if diag:
                                gt = self.G(lambda g, o=ptile, base=512 * qb - 128 * kc:
                                            g.affine_select(o, o, [[1, 512]], ALU.is_ge,
                                                            0.0, base=base,
                                                            channel_multiplier=-1),
                                            deps=[et], inc=True)
                                toks.append(gt)
                            else:
                                toks.append(et)
                        exp_toks[st_i] = toks

                    def emit_cspv(st_i, h0=h0, qb=qb, pvbank=pvbank, sumbank=sumbank,
                                  par=par, pvb_i=pvb_i, nst_=None):
                        strip = strips[st_i]
                        buf = st_i % 2
                        pbase = SCR_PROBS + buf * 1024
                        dep_all = list(exp_toks[st_i])
                        for j, kc in enumerate(strip):
                            ptile = self.sb["scr"][:, pbase + j * 512: pbase + j * 512 + 512]
                            first = (st_i == 0 and j == 0)
                            last = (st_i == nst - 1 and j == len(strip) - 1)
                            self.T(lambda t, o=pvbank[0:65, 0:512],
                                   l=self.sb["scr"][:, SCR_V + kc * 130 + hh * 65: SCR_V + kc * 130 + hh * 65 + 65],
                                   r=ptile, st=first, sp=last:
                                   t.matmul(o, l, r, start=st, stop=sp),
                                   deps=dep_all + [self.pv_free[pvb_i]], inc=True)
                            dep_all = []
                        self.probs_free[buf] = self.p.tok("T")

                    emit_se(0)
                    if self._pend_epi is not None:
                        self._pend_epi()
                        self._pend_epi = None
                    for st_i in range(nst):
                        if st_i + 1 < nst:
                            emit_se(st_i + 1)
                        emit_cspv(st_i)
                    tk = self.p.tok("T")

                    def epilogue(tk=tk, sumbank=sumbank, h0=h0, qb=qb, pr=pr,
                                 par=par, pvbank=pvbank, pvb_i=pvb_i):
                        e1 = self.Ve(lambda v, o=self.srow(0, 0, 512), i=pvbank[64:65, 0:512]:
                                     v.tensor_copy(o, i), deps=[tk], inc=True)
                        rc = self.Ve(lambda v, o=self.srow(0, 0, 512), i=self.srow(0, 0, 512):
                                     v.reciprocal(o, i), deps=[e1], inc=True)
                        self.Ve(lambda v, o=self.sb["stats"].bitcast(BF16)[0:8, 2560:4608][0:1, 64:576], i=self.srow(0, 0, 512):
                                v.tensor_copy(o, i), deps=[rc], inc=True)
                        rv = self.p.tok("V")
                        tbc = self.T(lambda t, o=sumbank[0:64, 0:512],
                                     l=self.sb["onesb"][0:1, 0:64],
                                     r=self.sb["stats"].bitcast(BF16)[0:8, 2560:4608][0:1, 64:576]:
                                     t.matmul(o, l, r, start=True, stop=True),
                                     deps=[rv], inc=True)
                        rb = self.Ve(lambda v, o=self.sb["stats"][0:64, 512:1024],
                                     i=sumbank[0:64, 0:512]:
                                     v.tensor_copy(o, i), deps=[tbc], inc=True)
                        self.sum_free[par] = self.p.tok("V")
                        if h0 == 0:
                            self.Ve(lambda v, o=self.fm("ctx", pr, qb * 512, 512, 0, 64),
                                    a=pvbank[0:64, 0:512],
                                    b=self.sb["stats"][0:64, 512:1024]:
                                    v.tensor_mul(o, a, b), deps=[rb], inc=True)
                        else:
                            sg = self.Ve(lambda v, o=self.sb["stats"][0:64, 2304:2816],
                                         a=pvbank[0:64, 0:512],
                                         b=self.sb["stats"][0:64, 512:1024]:
                                         v.tensor_mul(o, a, b), deps=[rb], inc=True)
                            self.Ve(lambda v, o=self.fm("ctx", pr, qb * 512, 512, 64, 64),
                                    i=self.sb["stats"][0:64, 2304:2816]:
                                    v.tensor_copy(o, i), deps=[sg], inc=True)
                        self.pv_free[pvb_i] = self.p.tok("V")

                    self._pend_epi = epilogue
            if self._pend_epi is not None:
                self._pend_epi()
                self._pend_epi = None
        self.dump_slot("ctx", "ctx")
        self.dump_slot("accpre", "x_acc")
        self.p.barrier()
        pool = Pool(self)
        wo_tok = {}
        for mc in range(NC):
            sw, tw = pool.load(mkwo(mc), NC, 128)
            for qc in range(4):
                bi = qc % 2
                bank = pb[bi]
                vdep = wo_tok.get(bi)
                for kc in range(NC):
                    self.T(lambda t, o=bank[:, 0:512], l=pool.tile(sw, kc, 128),
                           r=self.fm("ctx", kc, qc * 512, 512), st=(kc == 0), sp=(kc == NC - 1):
                           t.matmul(o, l, r, start=st, stop=sp),
                           deps=[tw, vdep], inc=(kc == NC - 1))
                ttok = self.p.tok("T")
                xa = self.fm("x_acc", mc, qc * 512, 512)
                if out_mode == "acc":
                    if bo_col is not None:
                        self.Ve(lambda v, o=xa, a=bank[:, 0:512], bc=self.pprow(bo_col + mc), x=xa:
                                v.scalar_tensor_tensor(o, a, bc, x, ALU.add, ALU.add),
                                deps=[ttok], inc=True)
                    else:
                        self.Ve(lambda v, o=xa, a=bank[:, 0:512], x=xa:
                                v.tensor_add(o, a, x), deps=[ttok], inc=True)
                else:
                    oslot = self.fm(out_slot, mc, qc * 512, 512)
                    r1 = self.Ve(lambda v, o=oslot, a=bank[:, 0:512],
                                 b=self.fm(res_slot, mc, qc * 512, 512):
                                 v.tensor_add(o, a, b), deps=[ttok], inc=True)
                    self.Ve(lambda v, o=xa, a=oslot, x=xa:
                            v.tensor_add(o, a, x), deps=[r1], inc=True)
                wo_tok[bi] = self.p.tok("V")
            pool.done(sw, self.p.tok("T"))
        self.p.barrier()

    def moe_l0(self, h2_slot, sub=99):
        mk = self
        pb = self.pb
        pool = Pool(self)
        self.gel_tok = {0: None, 1: None, 2: None}
        self.ye_tok = {3: None, 4: None, 5: None, 6: None}
        sgw, tgw = pool.load(self.w["L0_ffn_gate_w"].rearrange("(c p) n -> p c n", p=128),
                             NC, 8)
        for qc in range(4):
            vdep = self.p.tok("V")
            for kc in range(NC):
                self.T(lambda t, o=pb[6][0:8, 0:512], l=pool.tile(sgw, kc, 8),
                       r=self.fm(h2_slot, kc, qc * 512, 512), st=(kc == 0), sp=(kc == NC - 1):
                       t.matmul(o, l, r, start=st, stop=sp),
                       deps=[tgw, vdep], inc=(kc == NC - 1))
            tk = self.p.tok("T")
            self.Ve(lambda v, o=self.sb["stats"].bitcast(BF16)[0:8, 2560:4608][0:8, qc * 512:qc * 512 + 512],
                    a=pb[6][0:8, 0:512], bc=self.pprow(80, 0, 8):
                    v.tensor_scalar_add(o, a, bc), deps=[tk], inc=True)
        self.p.barrier()
        if sub < 1:
            return
        for tt in range(16):
            t1 = self.T(lambda t, o=self.pb_bf7[:, 0:8],
                        i=self.sb["stats"].bitcast(BF16)[0:8, 2560:4608][0:8, tt * 128:(tt + 1) * 128],
                        d=self.sb["idbf"][0:8, 0:8]: t.transpose(o, i, d),
                        deps=[self.p.tok("V")], inc=True)
            a1 = self.Ve(lambda v, o=self.sb["stats"][:, 1024:1032], a=self.pb_bf7[:, 0:8],
                         b=self.sb["pp"][:, 83:91]: v.tensor_add(o, a, b), deps=[t1],
                         inc=True)
            a2 = self.Ve(lambda v, o=self.sb["cells"][:, 5:6], i=self.sb["stats"][:, 1024:1032]:
                         v.reduce_max(o, i, AX.X), deps=[a1], inc=True, sw=True)
            self.Ve(lambda v, o=self.sb["mtok"][:, 0:8], i=self.sb["stats"][:, 1024:1032],
                    sc=self.sb["cells"][:, 5:6]:
                    v.tensor_single_scalar(o, i, sc, ALU.is_equal), deps=[a2],
                    inc=True, sw=True)
            v1 = self.p.tok("V")
            t2 = self.T(lambda t, o=self.pb_bf7[0:8, 128:256], i=self.sb["mtok"][:, 0:8],
                        d=self.sb["idbf"][:, :]: t.transpose(o, i, d), deps=[v1], inc=True)
            self.Ve(lambda v, o=self.sb["stats"].bitcast(BF16)[0:8, 2560:4608][0:8, tt * 128:(tt + 1) * 128],
                    i=self.pb_bf7[0:8, 128:256]: v.tensor_copy(o, i), deps=[t2], inc=True)
        self.p.barrier()
        if sub < 2:
            return
        for e in range(min(sub - 1, 8)):
            sel_tok = self.make_sel(e)
            w1r = self.w["L0_ffn_w1"].rearrange("e (c p) n -> e p c n", p=128)
            w2r = self.w["L0_ffn_w2"].rearrange("e (c p) n -> e p c n", p=128)
            for tb in range(2):
                heb_free = self.p.tok("T")
                for colh in range(2):
                    col = tb * 1024 + colh * 512
                    tbc = self.T(lambda t, o=pb[7][:, 0:512],
                                 l=self.sb["stats"].bitcast(BF16)[0:8, 2432:2560][0:8, 0:128],
                                 r=self.sb["stats"].bitcast(BF16)[0:8, 2560:4608][0:8, col:col + 512]:
                                 t.matmul(o, l, r, start=True, stop=True),
                                 deps=[self.p.tok("V"), sel_tok], inc=True)
                    for kc in range(NC):
                        self.Ve(lambda v,
                                o=self.sb["scr"][:, SCR_MH2 + kc * 1024 + colh * 512:
                                                 SCR_MH2 + kc * 1024 + colh * 512 + 512],
                                a=self.fm(h2_slot, kc, col, 512), b=pb[7][:, 0:512]:
                                v.tensor_mul(o, a, b), deps=[tbc],
                                inc=(kc == NC - 1))
                mh2_tok = self.p.tok("V")
                if sub == 2:
                    continue
                for hp in range(16):
                    sw, tw = pool.load(w1r[e, :, :, hp * 256:(hp + 1) * 256], NC, 256)
                    for hsub in range(2):
                        hid = hp * 2 + hsub
                        for colh in range(2):
                            bi = (2 * hsub + colh) % 3
                            bank = pb[bi]
                            gel_dep = self.gel_tok[bi]
                            for kc in range(NC):
                                self.T(lambda t, o=bank[:, 0:512],
                                       l=pool.tile(sw, kc, 256)[:, hsub * 128:(hsub + 1) * 128],
                                       r=self.sb["scr"][:, SCR_MH2 + kc * 1024 + colh * 512:
                                                        SCR_MH2 + kc * 1024 + colh * 512 + 512],
                                       st=(kc == 0), sp=(kc == NC - 1):
                                       t.matmul(o, l, r, start=st, stop=sp),
                                       deps=[tw, mh2_tok, gel_dep], inc=(kc == NC - 1))
                            tk = self.p.tok("T")
                            hb = self.heb_view(hid, colh)
                            self.Sc(lambda s, o=hb, i=bank[:, 0:512]:
                                    s.activation(o, i, AF.Gelu_apprx_tanh),
                                    deps=[tk, heb_free], inc=True)
                            self.gel_tok[bi] = self.p.tok("S")
                    pool.done(sw, self.p.tok("T"))
                heb_tok = self.p.tok("S")
                if sub == 3:
                    continue
                for dout in range(NC):
                    for hh in range(2):
                        sw, tw = pool.load(
                            w2r[e, :, hh * 16:(hh + 1) * 16, dout * 128:(dout + 1) * 128],
                            16, 128)
                        for colh in range(2):
                            bi = 3 + (2 * dout + colh) % 4
                            bank = pb[bi]
                            for hc in range(16):
                                hid = hh * 16 + hc
                                self.T(lambda t, o=bank[:, 0:512],
                                       l=pool.tile(sw, hc, 128),
                                       r=self.heb_view(hid, colh), st=(hc == 0 and hh == 0), sp=(hc == 15 and hh == 1):
                                       t.matmul(o, l, r, start=st, stop=sp),
                                       deps=[tw, heb_tok, self.ye_tok[bi]],
                                       inc=(hc == 15))
                        pool.done(sw, self.p.tok("T"))
                    for colh in range(2):
                        bi = 3 + (2 * dout + colh) % 4
                        bank = pb[bi]
                        tk = self.p.tok("T")
                        xa = self.fm("x_acc", dout, tb * 1024 + colh * 512, 512)
                        self.Ve(lambda v, o=xa, a=bank[:, 0:512], x=xa:
                                v.tensor_add(o, a, x), deps=[tk], inc=True)
                        self.ye_tok[bi] = self.p.tok("V")
        self.p.barrier()

    def heb_view(self, hid, colh):
        slot = "ctx" if hid < 16 else "xn"
        base = (hid % 16) * 1024 + colh * 512
        return self.sb[slot][:, base:base + 512]

    def moe_l1(self, h2_slot):
        mk = self
        pb = self.pb
        pool = Pool(self)
        self.sil_tok = {0: None, 1: None}
        self.mex_tok = {0: None, 1: None}
        self.ye_tok = {4: None, 5: None, 6: None}
        srw, trw = pool.load(self.w["L1_ffn_router_w"].rearrange("(c p) n -> p c n", p=128),
                             NC, 8)
        for qc in range(4):
            vdep = self.p.tok("V")
            for kc in range(NC):
                self.T(lambda t, o=pb[6][0:8, 0:512], l=pool.tile(srw, kc, 8),
                       r=self.fm(h2_slot, kc, qc * 512, 512), st=(kc == 0), sp=(kc == NC - 1):
                       t.matmul(o, l, r, start=st, stop=sp),
                       deps=[trw, vdep], inc=(kc == NC - 1))
            tk = self.p.tok("T")
            self.Ve(lambda v, o=self.sb["stats"].bitcast(BF16)[0:8, 2560:4608][0:8, qc * 512:qc * 512 + 512],
                    a=pb[6][0:8, 0:512]: v.tensor_copy(o, a), deps=[tk], inc=True)
        self.p.barrier()
        cells = self.sb["cells"]
        for tt in range(16):
            t1 = self.T(lambda t, o=self.pb_bf7[:, 0:8],
                        i=self.sb["stats"].bitcast(BF16)[0:8, 2560:4608][0:8, tt * 128:(tt + 1) * 128],
                        d=self.sb["idbf"][0:8, 0:8]: t.transpose(o, i, d),
                        deps=[self.p.tok("V")], inc=True)
            lt = self.sb["stats"][:, 1024:1032]
            c1 = self.Ve(lambda v, o=lt, i=self.pb_bf7[:, 0:8]: v.tensor_copy(o, i),
                         deps=[t1], inc=True)
            c2 = self.Ve(lambda v, o=cells[:, 6:7], i=lt:
                         v.reduce_max(o, i, AX.X, negate=True), deps=[c1], inc=True, sw=True)
            c3 = self.Sc(lambda s, o=self.sb["stats"][:, 1032:1040], i=lt, b=cells[:, 6:7]:
                         s.activation(o, i, AF.Exp, bias=b), deps=[c2], inc=True)
            et = self.sb["stats"][:, 1032:1040]
            c4 = self.Ve(lambda v, o=cells[:, 7:8], i=et: v.reduce_sum(o, i, AX.X),
                         deps=[c3], inc=True)
            c5 = self.Ve(lambda v, o=cells[:, 8:9], i=cells[:, 7:8]: v.reciprocal(o, i),
                         deps=[c4], inc=True, sw=True)
            probs = self.sb["stats"][:, 1040:1048]
            c6 = self.Ve(lambda v, o=probs, i=et, sc=cells[:, 8:9]:
                         v.tensor_single_scalar(o, i, sc, ALU.mult), deps=[c5],
                         inc=True, sw=True)
            ltb = self.sb["stats"][:, 1048:1056]
            c7 = self.Ve(lambda v, o=ltb, a=probs, b=self.sb["pp"][:, 83:91]:
                         v.tensor_add(o, a, b), deps=[c6], inc=True, sw=True)
            c8 = self.Ve(lambda v, o=cells[:, 9:10], i=ltb: v.reduce_max(o, i, AX.X),
                         deps=[c7], inc=True, sw=True)
            m1 = self.sb["stats"][:, 1056:1064]
            c9 = self.Ve(lambda v, o=m1, i=ltb, sc=cells[:, 9:10]:
                         v.tensor_single_scalar(o, i, sc, ALU.is_equal), deps=[c8],
                         inc=True, sw=True)
            ltb2 = self.sb["stats"][:, 1064:1072]
            c10 = self.Ve(lambda v, o=ltb2, a=m1, b=ltb:
                          v.scalar_tensor_tensor(o, a, -4.0, b, ALU.mult, ALU.add),
                          deps=[c9], inc=True, sw=True)
            c11 = self.Ve(lambda v, o=cells[:, 10:11], i=ltb2: v.reduce_max(o, i, AX.X),
                          deps=[c10], inc=True, sw=True)
            m2 = self.sb["stats"][:, 1072:1080]
            c12 = self.Ve(lambda v, o=m2, i=ltb2, sc=cells[:, 10:11]:
                          v.tensor_single_scalar(o, i, sc, ALU.is_equal), deps=[c11],
                          inc=True, sw=True)
            tmp8 = self.sb["stats"][:, 1080:1088]
            c13 = self.Ve(lambda v, o=tmp8, a=probs, b=m1: v.tensor_mul(o, a, b),
                          deps=[c12], inc=True, sw=True)
            c14 = self.Ve(lambda v, o=cells[:, 11:12], i=tmp8: v.reduce_sum(o, i, AX.X),
                          deps=[c13], inc=True, sw=True)
            c15 = self.Ve(lambda v, o=tmp8, a=probs, b=m2: v.tensor_mul(o, a, b),
                          deps=[c14], inc=True, sw=True)
            c16 = self.Ve(lambda v, o=cells[:, 12:13], i=tmp8: v.reduce_sum(o, i, AX.X),
                          deps=[c15], inc=True, sw=True)
            c17 = self.Ve(lambda v, o=cells[:, 13:14], a=cells[:, 11:12], b=cells[:, 12:13]:
                          v.tensor_sub(o, a, b), deps=[c16], inc=True, sw=True)
            c18 = self.Sc(lambda s, o=cells[:, 14:15], i=cells[:, 13:14]:
                          s.activation(o, i, AF.Sigmoid), deps=[c17], inc=True)
            c19 = self.Ve(lambda v, o=cells[:, 15:16], i=cells[:, 14:15]:
                          v.tensor_scalar(o, i, -1.0, 1.0, ALU.mult, ALU.add),
                          deps=[c18], inc=True)
            cm = self.sb["mtok"][:, 0:8]
            c20 = self.Ve(lambda v, o=cm, i=m1, sc=cells[:, 14:15]:
                          v.tensor_single_scalar(o, i, sc, ALU.mult), deps=[c19],
                          inc=True, sw=True)
            self.Ve(lambda v, o=cm, a=m2, sc=cells[:, 15:16], b=cm:
                    v.scalar_tensor_tensor(o, a, sc, b, ALU.mult, ALU.add),
                    deps=[c20], inc=True, sw=True)
            v1 = self.p.tok("V")
            t2 = self.T(lambda t, o=self.pb_bf7[0:8, 128:256], i=cm,
                        d=self.sb["idbf"][:, :]: t.transpose(o, i, d), deps=[v1], inc=True)
            self.Ve(lambda v, o=self.sb["stats"].bitcast(BF16)[0:8, 2560:4608][0:8, tt * 128:(tt + 1) * 128],
                    i=self.pb_bf7[0:8, 128:256]: v.tensor_copy(o, i), deps=[t2], inc=True)
        self.p.barrier()
        egr = self.w["L1_ffn_e_gate"].rearrange("e (c p) n -> e p c n", p=128)
        eur = self.w["L1_ffn_e_up"].rearrange("e (c p) n -> e p c n", p=128)
        edr = self.w["L1_ffn_e_down"].rearrange("e (c p) n -> e p c n", p=128)
        shg = self.w["L1_ffn_sh_gate"].rearrange("(c p) n -> p c n", p=128)
        shu = self.w["L1_ffn_sh_up"].rearrange("(c p) n -> p c n", p=128)
        shd = self.w["L1_ffn_sh_down"].rearrange("(c p) n -> p c n", p=128)
        jobs = [(e, tbh) for e in range(9) for tbh in range(2)]
        sel_toks = {}
        mup_toks = {}

        def build_mup(ji):
            e, tbh = jobs[ji]
            if e >= 8:
                mup_toks[ji] = None
                return
            if e not in sel_toks:
                sel_toks[e] = self.make_sel(e)
            mb = (ji % 2) * 8192
            for colh in range(2):
                col = tbh * 1024 + colh * 512
                tbc = self.T(lambda t, o=pb[7][:, 0:512],
                             l=self.sb["stats"].bitcast(BF16)[0:8, 2432:2560][0:8, 0:128],
                             r=self.sb["stats"].bitcast(BF16)[0:8, 2560:4608][0:8, col:col + 512]:
                             t.matmul(o, l, r, start=True, stop=True),
                             deps=[self.p.tok("V"), sel_toks[e]], inc=True)
                for kc in range(NC):
                    self.Ve(lambda v,
                            o=self.sb["xn"][:, mb + kc * 1024 + colh * 512:
                                            mb + kc * 1024 + colh * 512 + 512],
                            a=self.fm(h2_slot, kc, col, 512), b=pb[7][:, 0:512]:
                            v.tensor_mul(o, a, b), deps=[tbc],
                            inc=(kc == NC - 1))
            mup_toks[ji] = self.p.tok("V")

        build_mup(0)
        for ji, (e, tbh) in enumerate(jobs):
            if True:
                mup_tok = mup_toks[ji]
                mb = (ji % 2) * 8192
                for ffp in range(8):
                    if e < 8:
                        sg_, tg_ = pool.load(egr[e, :, :, ffp * 256:(ffp + 1) * 256], NC, 256)
                        su_, tu_ = pool.load(eur[e, :, :, ffp * 256:(ffp + 1) * 256], NC, 256)
                    else:
                        sg_, tg_ = pool.load(shg[:, :, ffp * 256:(ffp + 1) * 256], NC, 256)
                        su_, tu_ = pool.load(shu[:, :, ffp * 256:(ffp + 1) * 256], NC, 256)
                    for sub in range(2):
                        ffc = ffp * 2 + sub
                        for colh in range(2):
                            gbank = pb[colh]
                            ubank = pb[2 + colh]
                            sil_dep = self.sil_tok[colh]
                            mex_dep = self.mex_tok[colh]
                            for kc in range(NC):
                                self.T(lambda t, o=gbank[:, 0:512],
                                       l=pool.tile(sg_, kc, 256)[:, sub * 128:(sub + 1) * 128],
                                       r=self.fm(h2_slot, kc, tbh * 1024 + colh * 512, 512),
                                       st=(kc == 0), sp=(kc == NC - 1):
                                       t.matmul(o, l, r, start=st, stop=sp),
                                       deps=[tg_, sil_dep], inc=(kc == NC - 1))
                            gt = self.p.tok("T")
                            for kc in range(NC):
                                if e < 8:
                                    r = self.sb["xn"][:, mb + kc * 1024 + colh * 512:
                                                      mb + kc * 1024 + colh * 512 + 512]
                                else:
                                    r = self.fm(h2_slot, kc, tbh * 1024 + colh * 512, 512)
                                self.T(lambda t, o=ubank[:, 0:512],
                                       l=pool.tile(su_, kc, 256)[:, sub * 128:(sub + 1) * 128],
                                       rr=r, st=(kc == 0), sp=(kc == NC - 1):
                                       t.matmul(o, l, rr, start=st, stop=sp),
                                       deps=[tu_, mup_tok, mex_dep], inc=(kc == NC - 1))
                            ut = self.p.tok("T")
                            sgt = self.sb["scr"][:, SCR_SG + colh * 512:SCR_SG + colh * 512 + 512]
                            self.Sc(lambda s, o=sgt, i=gbank[:, 0:512]:
                                    s.activation(o, i, AF.Silu),
                                    deps=[gt, self.mex_tok[colh]], inc=True)
                            self.sil_tok[colh] = self.p.tok("S")
                            st_ = self.sil_tok[colh]
                            self.Ve(lambda v, o=self.mexp_view(ffc, colh),
                                    a=ubank[:, 0:512], b=sgt:
                                    v.tensor_mul(o, a, b), deps=[ut, st_], inc=True)
                            self.mex_tok[colh] = self.p.tok("V")
                    pool.done(sg_, self.p.tok("T"))
                    pool.done(su_, self.p.tok("T"))
                mexp_tok = self.p.tok("V")
                if ji + 1 < len(jobs):
                    build_mup(ji + 1)
                for dout in range(NC):
                    if e < 8:
                        sw, tw = pool.load(edr[e, :, :, dout * 128:(dout + 1) * 128], 16, 128)
                    else:
                        sw, tw = pool.load(shd[:, :, dout * 128:(dout + 1) * 128], 16, 128)
                    for colh in range(2):
                        bi = 4 + (2 * dout + colh) % 3
                        bank = pb[bi]
                        for ffc in range(16):
                            self.T(lambda t, o=bank[:, 0:512], l=pool.tile(sw, ffc, 128),
                                   r=self.mexp_view(ffc, colh), st=(ffc == 0), sp=(ffc == 15):
                                   t.matmul(o, l, r, start=st, stop=sp),
                                   deps=[tw, mexp_tok, self.ye_tok[bi]], inc=(ffc == 15))
                        tk = self.p.tok("T")
                        xa = self.fm("x_acc", dout, tbh * 1024 + colh * 512, 512)
                        self.Ve(lambda v, o=xa, a=bank[:, 0:512], x=xa:
                                v.tensor_add(o, a, x), deps=[tk], inc=True)
                        self.ye_tok[bi] = self.p.tok("V")
                    pool.done(sw, self.p.tok("T"))
        self.p.barrier()

    def mexp_view(self, ffc, colh):
        return self.sb["ctx"][:, ffc * 1024 + colh * 512: ffc * 1024 + colh * 512 + 512]


    def layer0(self, upto=99):
        w = self.w
        self.layernorm(None, "xn", "plain", src_f32="x_acc")
        self.dump_slot("xn0", "xn")
        if "stats" in self.dump:
            self.p.barrier()
            self.Q(lambda q, o=self.dump["stats"], s=self.sb["stats"][:, :]:
                   q.dma_start(out=o, in_=s), inc=True, dma=True)
            self.p.barrier()
        if upto < 2:
            return
        wq = lambda p: w["L0_attn_Wq"].rearrange("(c p) n -> p c n", p=128)[:, :, p * 128:(p + 1) * 128]
        wk = lambda p: w["L0_attn_Wk"].rearrange("(c p) n -> p c n", p=128)[:, :, p * 128:(p + 1) * 128]
        wv = lambda p: w["L0_attn_Wv"].rearrange("(c p) n -> p c n", p=128)[:, :, p * 128:(p + 1) * 128]
        wo = lambda m: w["L0_attn_Wo"].rearrange("(c p) n -> p c n", p=128)[:, :, m * 128:(m + 1) * 128]
        self.attention("xn", wq, wk, wv, wo, scale=D ** -0.5, causal=True, pos=True,
                       out_mode="acc")
        self.dump_slot("acc_a0", "x_acc")
        if upto < 3:
            return
        self.layernorm("xn", "hA", "affine", w_col=0, b_col=8, eps_col=81)
        self.dump_slot("h1_l0", "hA")
        if upto < 4:
            return
        qkv = w["L0_ffn_w_qkv"].rearrange("(c p) n -> p c n", p=128)
        mq = lambda p: qkv[:, :, p * 128:(p + 1) * 128]
        mkk = lambda p: qkv[:, :, 1024 + p * 128: 1024 + (p + 1) * 128]
        mv = lambda p: qkv[:, :, 2048 + p * 128: 2048 + (p + 1) * 128]
        mwo = lambda m: w["L0_ffn_w_proj"].rearrange("(c p) n -> p c n", p=128)[:, :, m * 128:(m + 1) * 128]
        self.attention("hA", mq, mkk, mv, mwo, scale=0.125, causal=False, pos=False,
                       out_mode="res", out_slot="xn", res_slot="xn")
        self.dump_slot("y1", "xn")
        if upto < 5:
            return
        self.layernorm("xn", "hA", "affine", w_col=16, b_col=24, eps_col=81)
        self.dump_slot("h2_l0", "hA")
        if upto < 6:
            return
        self.moe_l0("hA", sub=(upto - 5 if upto < 99 else 99))
        self.dump_slot("x1", "x_acc")

    def layer1(self, upto=99):
        w = self.w
        self.layernorm(None, "xn", "plain", src_f32="x_acc")
        self.dump_slot("l1_xn", "xn")
        if upto < 2:
            return
        qkv = w["L1_attn_Wqkv"].rearrange("(c p) n -> p c n", p=128)
        mq = lambda p: qkv[:, :, p * 128:(p + 1) * 128]
        mkk = lambda p: qkv[:, :, 1024 + p * 128: 1024 + (p + 1) * 128]
        mv = lambda p: qkv[:, :, 2048 + p * 128: 2048 + (p + 1) * 128]
        mwo = lambda m: w["L1_attn_Wproj"].rearrange("(c p) n -> p c n", p=128)[:, :, m * 128:(m + 1) * 128]
        vb = lambda p: self.srow(0, 3072 + p * 128, 128)
        self.Q(lambda q, o=self.srow(0, 3072, 1024), i=self.w["L1_attn_bqkv"][None, 2048:3072]:
               q.dma_start(out=o, in_=i), inc=True, dma=True)
        self.p.barrier()
        self.attention("xn", mq, mkk, mv, mwo, scale=0.5, causal=False, pos=False,
                       bq_col=48, bk_col=56, vbias_row=vb, bo_col=72, out_mode="acc")
        self.dump_slot("l1_acc", "x_acc")
        if upto < 3:
            return
        self.layernorm("xn", "hA", "rms", w_col=32, eps_col=82)
        self.dump_slot("l1_h", "hA")
        if upto < 4:
            return
        fq = lambda p: w["L1_ffn_wq"].rearrange("(c p) n -> p c n", p=128)[:, :, p * 128:(p + 1) * 128]
        fk = lambda p: w["L1_ffn_wk"].rearrange("(c p) n -> p c n", p=128)[:, :, p * 128:(p + 1) * 128]
        fv = lambda p: w["L1_ffn_wv"].rearrange("(c p) n -> p c n", p=128)[:, :, p * 128:(p + 1) * 128]
        fwo = lambda m: w["L1_ffn_wo"].rearrange("(c p) n -> p c n", p=128)[:, :, m * 128:(m + 1) * 128]
        self.attention("hA", fq, fk, fv, fwo, scale=0.125, causal=False, pos=False,
                       out_mode="res", out_slot="xn", res_slot="xn")
        self.dump_slot("l1_x2", "xn")
        if upto < 5:
            return
        self.layernorm("xn", "hA", "rms", w_col=40, eps_col=82)
        self.dump_slot("l1_h2", "hA")
        if upto < 6:
            return
        self.moe_l1("hA")


def build_nc(layers=(0, 1), dump=(), upto=99):
    nc = bass.Bass("TRN2", target_bir_lowering=False,
                   dynamic_dma_scratch_size=16384)
    prog = Prog()
    w = {}
    shapes = {
        "L0_attn_Wq": [D, D], "L0_attn_Wk": [D, D], "L0_attn_Wv": [D, D],
        "L0_attn_Wo": [D, D], "L0_ffn_w_qkv": [D, 3 * D], "L0_ffn_w_proj": [D, D],
        "L0_ffn_gate_w": [D, 8], "L0_ffn_w1": [8, D, 4 * D], "L0_ffn_w2": [8, 4 * D, D],
        "L1_attn_Wqkv": [D, 3 * D], "L1_attn_bqkv": [3 * D], "L1_attn_Wproj": [D, D],
        "L1_ffn_wq": [D, D], "L1_ffn_wk": [D, D], "L1_ffn_wv": [D, D],
        "L1_ffn_wo": [D, D], "L1_ffn_router_w": [D, 8],
        "L1_ffn_sh_gate": [D, 2048], "L1_ffn_sh_up": [D, 2048],
        "L1_ffn_sh_down": [2048, D],
        "L1_ffn_e_gate": [8, D, 2048], "L1_ffn_e_up": [8, D, 2048],
        "L1_ffn_e_down": [8, 2048, D],
    }
    need = set()
    if 0 in layers:
        need |= {k for k in shapes if k.startswith("L0")}
    if 1 in layers:
        need |= {k for k in shapes if k.startswith("L1")}
    for k in sorted(need):
        w[k] = nc.dram_tensor(k, shapes[k], F32, kind="ExternalInput").ap()
    xin = nc.dram_tensor("xT", [D, S], F32, kind="ExternalInput").ap()
    pos2_in = nc.dram_tensor("pos2c", [128, S], F32, kind="ExternalInput").ap()
    pp_in = nc.dram_tensor("ppc", [128, 96], F32, kind="ExternalInput").ap()
    w["onesrc"] = nc.dram_tensor("onesrc", [1, 128], F32, kind="ExternalInput").ap()
    out = nc.dram_tensor("out", [D, S], F32, kind="ExternalOutput").ap()
    dump_t = {}
    for name in dump:
        if name == "bnd":
            dump_t[name] = nc.dram_tensor("DUMP_bnd", [128, 96], F32,
                                          kind="ExternalOutput").ap()
            continue
        if name == "qkv0":
            dump_t[name] = nc.dram_tensor("DUMP_qkv0", [128, 8192], BF16,
                                          kind="ExternalOutput").ap()
            continue
        if name == "stats":
            dump_t[name] = nc.dram_tensor("DUMP_stats", [128, 4096], F32,
                                          kind="ExternalOutput").ap()
            continue
        dt = F32 if name in ("acc_a0", "x1", "accpre", "l1_acc") else BF16
        dump_t[name] = nc.dram_tensor("DUMP_" + name, [128, 16384], dt,
                                      kind="ExternalOutput").ap()

    import contextlib
    with contextlib.ExitStack() as st:
        sb = {}
        def sbt(name, shape, dtype):
            sb[name] = st.enter_context(nc.sbuf_tensor(name, shape, dtype))
        sbt("x_acc", [128, 16384], F32)
        sbt("xn", [128, 16384], BF16)
        sbt("hA", [128, 16384], BF16)
        sbt("ctx", [128, 16384], BF16)
        sbt("scr", [128, SCR_COLS], BF16)
        sbt("wpan", [128, WPAN_SLOT * WPAN_NSLOT], BF16)
        sbt("stats", [128, 4096], F32)
        sbt("idbf", [128, 128], BF16)
        sbt("onesf", [64, 128], F32)
        sbt("pp", [128, 96], F32)
        sbt("onesb", [128, 128], BF16)
        sbt("negB", [128, 1], F32)
        sbt("cells", [128, 16], F32)
        sbt("mtok", [128, 8], BF16)
        sbt("cb", [128, 8], BF16)
        pb = [st.enter_context(nc.psum_tensor(f"pb{i}", [128, 512], F32))
              for i in range(8)]
        sems = {e: st.enter_context(nc.semaphore(f"sem{e}")) for e in Prog.SEMS}

        mk = MK(nc, prog, sb, pb, w, dump_t)
        mk.pb_bf7 = pb[7].bitcast(BF16)
        mk.pb_bf6 = pb[6].bitcast(BF16)
        mk.pos2v = sb["hA"].bitcast(F32)[:, 0:S]
        mk.Q(lambda q, o=sb["x_acc"][:, :].rearrange("p (c t) -> p c t", c=NC),
             i=xin.rearrange("(c p) t -> p c t", p=128): q.dma_start(out=o, in_=i),
             inc=True, dma=True)
        mk.Q(lambda q, o=mk.pos2v, i=pos2_in: q.dma_start(out=o, in_=i),
             inc=True, dma=True)
        mk.Q(lambda q, o=sb["pp"][:, :], i=pp_in: q.dma_start(out=o, in_=i),
             inc=True, dma=True)
        mk.G(lambda g: g.memset(sb["onesb"][:, :], 1.0))
        mk.G(lambda g: g.memset(sb["onesf"][:, :], 1.0))
        g1 = mk.G(lambda g: g.memset(sb["idbf"][:, :], 1.0), inc=True)
        mk.p.emit("G", lambda g: g.affine_select(sb["idbf"][:, :], sb["idbf"][:, :],
                                                 [[1, 128]], ALU.is_equal, 0.0,
                                                 base=0, channel_multiplier=-1),
                  deps=[g1], inc=True, self_wait=True)
        prog.barrier()
        if 0 in layers:
            mk.layer0(upto)
        if 1 in layers:
            mk.layer1(upto if layers == (1,) else 99)
        prog.barrier()
        mk.Q(lambda q, o=out.rearrange("(c p) t -> p c t", p=128),
             i=sb["x_acc"][:, :].rearrange("p (c t) -> p c t", c=NC):
             q.dma_start(out=o, in_=i), inc=True, dma=True)
        mk.Q(lambda q: q.nop(), deps=[prog.tok("Q")], self_wait=True)

        with nc.Block() as block:
            def runner(engname):
                def body(eng):
                    ops = prog.ops[engname]
                    sem_map = sems
                    for fn, waits, incn, semname in ops:
                        for (sname, v) in waits:
                            eng.wait_ge(sem_map[sname], v)
                        inst = fn(eng)
                        if incn:
                            inst.then_inc(sem_map[semname], incn)
                return body
            block.tensor(runner("T"))
            block.scalar(runner("S"))
            block.vector(runner("V"))
            block.gpsimd(runner("G"))
            block.sync(runner("Q"))
    return nc



def _host_consts():
    idx = np.arange(S, dtype=np.float64)
    dim = 64
    eps = float(np.finfo(np.float32).eps)
    scale = 1.0 / (10000.0 ** (np.arange(0, dim, 2, dtype=np.float64) /
                               np.float64(dim)) + eps)
    idx32 = np.arange(S, dtype=np.float32)
    sc32 = (1.0 / (10000.0 ** (np.arange(0, dim, 2, dtype=np.float32) / np.float32(dim))
                   + np.float32(eps))).astype(np.float32)
    ang = idx32[:, None] * sc32[None, :]
    pos = np.stack([np.sin(ang), np.cos(ang)], axis=-1).reshape(S, dim).astype(np.float32)
    pos2 = np.concatenate([pos.T, pos.T], axis=0).astype(np.float32)
    return pos2


def _pack_pp(inputs):
    pp = np.zeros((128, 96), np.float32)
    def put(col, vec):
        v = np.asarray(vec, np.float32).reshape(-1)
        n = v.size // 128
        pp[:, col:col + n] = v.reshape(n, 128).T
    put(0, inputs["L0_ffn_ln1_w"]); put(8, inputs["L0_ffn_ln1_b"])
    put(16, inputs["L0_ffn_ln2_w"]); put(24, inputs["L0_ffn_ln2_b"])
    put(32, inputs["L1_ffn_norm1_w"]); put(40, inputs["L1_ffn_norm2_w"])
    put(48, inputs["L1_attn_bqkv"]); put(72, inputs["L1_attn_bproj"])
    pp[0:8, 80] = np.asarray(inputs["L0_ffn_gate_b"], np.float32)
    pp[:, 81] = 1e-5
    pp[:, 82] = 1e-6
    for e in range(8):
        pp[:, 83 + e] = -e * 1e-6
    return pp


_WEIGHT_KEYS = [
    "L0_attn_Wq", "L0_attn_Wk", "L0_attn_Wv", "L0_attn_Wo",
    "L0_ffn_w_qkv", "L0_ffn_w_proj", "L0_ffn_gate_w", "L0_ffn_w1", "L0_ffn_w2",
    "L1_attn_Wqkv", "L1_attn_bqkv", "L1_attn_Wproj",
    "L1_ffn_wq", "L1_ffn_wk", "L1_ffn_wv", "L1_ffn_wo", "L1_ffn_router_w",
    "L1_ffn_sh_gate", "L1_ffn_sh_up", "L1_ffn_sh_down",
    "L1_ffn_e_gate", "L1_ffn_e_up", "L1_ffn_e_down",
]


def kernel(**inputs):
    x = np.asarray(inputs["L0_attn_x"], np.float32)
    pos2 = _host_consts()
    pp = _pack_pp(inputs)
    warr = {k: np.ascontiguousarray(np.asarray(inputs[k], np.float32))
            for k in _WEIGHT_KEYS}
    xTs = [np.ascontiguousarray(x[i].T) for i in range(NCORES)]
    consts = {"pos2c": pos2, "ppc": pp,
              "onesrc": np.ones((1, 128), np.float32)}

    if FUSE:
        nc = build_nc(layers=(0, 1))
        in_maps = []
        for i in range(NCORES):
            m = dict(consts)
            m.update(warr)
            m["xT"] = xTs[i]
            in_maps.append(m)
        res = run_bass_kernel_spmd(nc, in_maps, core_ids=list(range(NCORES)))
        outs = [res.results[i]["out"] for i in range(NCORES)]
    else:
        nc0 = build_nc(layers=(0,))
        in_maps = []
        w0 = {k: v for k, v in warr.items() if k.startswith("L0")}
        for i in range(NCORES):
            m = dict(consts)
            m.update(w0)
            m["xT"] = xTs[i]
            in_maps.append(m)
        res0 = run_bass_kernel_spmd(nc0, in_maps, core_ids=list(range(NCORES)))
        mids = [np.asarray(res0.results[i]["out"]) for i in range(NCORES)]
        nc1 = build_nc(layers=(1,))
        w1 = {k: v for k, v in warr.items() if k.startswith("L1")}
        in_maps = []
        for i in range(NCORES):
            m = dict(consts)
            m.update(w1)
            m["xT"] = mids[i]
            in_maps.append(m)
        res = run_bass_kernel_spmd(nc1, in_maps, core_ids=list(range(NCORES)))
        outs = [res.results[i]["out"] for i in range(NCORES)]

    return np.stack([np.asarray(o).T for o in outs], axis=0).astype(np.float32)


# version 11 (final)
# speedup vs baseline: 1.3990x; 1.0099x over previous
import numpy as np
import concourse.bass as bass
import concourse.mybir as mybir
from concourse.bass_utils import run_bass_kernel_spmd

F32 = mybir.dt.float32
BF16 = mybir.dt.bfloat16
AF = mybir.ActivationFunctionType
ALU = mybir.AluOpType
AX = mybir.AxisListType

D = 1024
S = 2048
NC = 8
NCORES = 8
HPAIRS = 8
FUSE = True

SCR_Q = 0
SCR_K = 2048
SCR_V = 4096
SCR_PROBS = 6176
SCR_MH2 = 0
SCR_SG = 6176
SCR_COLS = 6176 + 2 * 2 * 512

WPAN_SLOT = 2048
WPAN_NSLOT = 3


class Prog:
    ENGS = "TSVGQ"
    GD = ("G0", "G1", "G2", "G3")
    QD = ("Q0", "Q1")
    SEMS = ("T", "S", "V", "Gc", "Qc") + GD + QD

    def __init__(self):
        self.ops = {e: [] for e in self.ENGS}
        self.cnt = {s: 0 for s in self.SEMS}
        self.marks = {e: dict.fromkeys(self.SEMS, 0) for e in self.ENGS}
        self.pending = {e: [] for e in self.ENGS}
        self.dmai = {"G": 0, "Q": 0}

    def emit(self, eng, fn, deps=(), inc=False, dma=False, self_wait=False):
        deps = [d for d in deps if d is not None]
        if dma:
            assert inc and eng in ("G", "Q")
            ring = self.GD if eng == "G" else self.QD
            sem = ring[self.dmai[eng] % len(ring)]
            self.dmai[eng] += 1
            deps = deps + [(sem, self.cnt[sem])]
        else:
            sem = eng + "c" if eng in ("G", "Q") else eng
        want = {}
        for d in list(self.pending[eng]) + deps:
            if d is None:
                continue
            de, dc = d
            if dc <= 0:
                continue
            if dc > self.marks[eng][de]:
                self.marks[eng][de] = dc
                want[de] = max(want.get(de, 0), dc)
        self.pending[eng] = []
        waits = sorted(want.items())
        amt = 0
        if inc:
            amt = 16 if dma else 1
            self.cnt[sem] += amt
        self.ops[eng].append([fn, waits, amt, sem])
        if inc:
            return (sem, self.cnt[sem])
        return None

    def tok(self, sem):
        if sem == "G":
            sem = "Gc"
        if sem == "Q":
            ring = self.QD
            i = (self.dmai["Q"] - 1) % len(ring) if self.dmai["Q"] else 0
            return (ring[i], self.cnt[ring[i]])
        return (sem, self.cnt[sem])

    def barrier(self):
        for e in self.ENGS:
            if self.ops[e]:
                last = self.ops[e][-1]
                if last[2] == 0:
                    sem = last[3]
                    assert sem not in self.GD + self.QD
                    last[2] = 1
                    self.cnt[sem] += last[2]
        toks = [(s, self.cnt[s]) for s in self.SEMS]
        for e in self.ENGS:
            self.pending[e] = list(toks)


class Pool:

    def __init__(self, mk, nslots=WPAN_NSLOT):
        self.mk = mk
        self.n = nslots
        self.freetok = [[] for _ in range(nslots)]
        self.i = 0

    def load(self, in_ap, nch, ncols, stash=None):
        mk = self.mk
        s = self.i % self.n
        self.i += 1
        base = s * WPAN_SLOT
        assert nch * ncols <= WPAN_SLOT
        out = mk.sb["wpan"][:, base:base + nch * ncols].rearrange(
            "p (c n) -> p c n", c=nch)
        tok = mk.p.emit("G", lambda g, o=out, i=in_ap: g.dma_start(out=o, in_=i),
                        deps=self.freetok[s], inc=True, dma=True)
        self.freetok[s] = [tok]
        if stash is not None:
            qt = mk.p.emit("Q", lambda q, o=stash, i=out: q.dma_start(out=o, in_=i),
                           deps=[tok], inc=True, dma=True)
            self.freetok[s].append(qt)
        return s, tok

    def load_b(self, in_ap, nch, ncols):
        mk = self.mk
        s = self.i % self.n
        self.i += 1
        base = s * WPAN_SLOT
        assert nch * ncols <= WPAN_SLOT
        out = mk.sb["wpan"][:, base:base + nch * ncols].rearrange(
            "p (c n) -> p c n", c=nch)
        tok = mk.p.emit("Q", lambda q, o=out, i=in_ap: q.dma_start(out=o, in_=i),
                        deps=self.freetok[s], inc=True, dma=True)
        self.freetok[s] = [tok]
        return s, tok

    def tile(self, s, kc, ncols):
        base = s * WPAN_SLOT
        return self.mk.sb["wpan"][:, base + kc * ncols: base + (kc + 1) * ncols]

    def done(self, s, tok):
        self.freetok[s].append(tok)


class MK:
    def __init__(self, nc, prog, sb, pb, w, dump):
        self.nc = nc
        self.p = prog
        self.sb = sb
        self.pb = pb
        self.w = w
        self.dump = dump

    def make_sel(self, e):
        tprev = self.p.tok("T")
        g1 = self.G(lambda g: g.memset(self.sb["stats"].bitcast(BF16)[0:8, 2432:2560][0:8, 0:128], 0.0),
                    deps=[tprev], inc=True)
        return self.p.emit("G", lambda g, o=self.sb["stats"].bitcast(BF16)[0:8, 2432:2560][e:e + 1, 0:128],
                           i=self.w["onesrc"]: g.dma_start(out=o, in_=i),
                           deps=[g1], inc=True, dma=True, self_wait=True)

    def fm(self, slot, c, t0, n, h0=0, hp=128):
        return self.sb[slot][h0:h0 + hp, c * S + t0: c * S + t0 + n]

    def pprow(self, col, p0=0, np_=128):
        return self.sb["pp"][p0:p0 + np_, col:col + 1]

    def srow(self, p, c0, n):
        return self.sb["stats"][p:p + 1, c0:c0 + n]

    def T(self, fn, deps=(), inc=False):
        return self.p.emit("T", fn, deps, inc)

    def Sc(self, fn, deps=(), inc=False, sw=False):
        return self.p.emit("S", fn, deps, inc, False, sw)

    def Ve(self, fn, deps=(), inc=False, sw=False):
        return self.p.emit("V", fn, deps, inc, False, sw)

    def G(self, fn, deps=(), inc=False, dma=False):
        return self.p.emit("G", fn, deps, inc, dma)

    def Q(self, fn, deps=(), inc=False, dma=False, self_wait=False):
        return self.p.emit("Q", fn, deps, inc, dma, self_wait)

    def dump_slot(self, name, slot, dtype_cols=16384):
        if name not in self.dump:
            return
        out = self.dump[name]
        src = self.sb[slot][:, 0:dtype_cols]
        self.p.barrier()
        self.Q(lambda q, o=out, s=src: q.dma_start(out=o, in_=s), inc=True, dma=True)
        self.p.barrier()


    def layernorm(self, src_slot, dst_slot, kind, w_col=None, b_col=None,
                  eps_col=81, src_f32=None):
        mk = self
        pb = self.pb
        if src_f32 is not None:
            for c in range(NC):
                self.Ve(lambda v, o=self.fm(dst_slot, c, 0, S),
                        i=self.fm(src_f32, c, 0, S): v.tensor_copy(o, i))
            src_b = dst_slot
        else:
            src_b = src_slot
        self.p.barrier()
        need_mean = kind != "rms"
        sq_mm_tok = [None, None]
        st_ev_tok = {}
        for qc in range(4):
            sb6 = pb[6] if qc % 2 == 0 else pb[4]
            sb5 = pb[5] if qc % 2 == 0 else pb[7]
            vdep = st_ev_tok.get(qc % 2)
            for c in range(NC):
                buf = c % 2
                sqv = self.sb["scr"][:, SCR_PROBS + buf * 512:SCR_PROBS + buf * 512 + 512]
                t = self.Sc(lambda s, o=sqv, i=self.fm(src_b, c, qc * 512, 512):
                            s.activation(o, i, AF.Square),
                            deps=[sq_mm_tok[buf]], inc=True)
                if need_mean:
                    self.T(lambda tt, o=sb6[0:1, :], l=self.sb["onesb"][:, 0:1],
                           r=self.fm(src_b, c, qc * 512, 512), st=(c == 0), sp=(c == NC - 1):
                           tt.matmul(o, l, r, start=st, stop=sp),
                           deps=[vdep])
                self.T(lambda tt, o=sb5[0:1, :], l=self.sb["onesb"][:, 0:1],
                       r=sqv, st=(c == 0), sp=(c == NC - 1):
                       tt.matmul(o, l, r, start=st, stop=sp),
                       deps=[t, vdep], inc=True)
                sq_mm_tok[buf] = self.p.tok("T")
            tk = self.p.tok("T")
            if need_mean:
                self.Ve(lambda v, o=self.srow(0, qc * 512, 512), i=sb6[0:1, 0:512]:
                        v.tensor_copy(o, i), deps=[tk])
            self.Ve(lambda v, o=self.srow(32, qc * 512, 512), i=sb5[0:1, 0:512]:
                    v.tensor_copy(o, i), deps=[tk], inc=True)
            st_ev_tok[qc % 2] = self.p.tok("V")
        self.p.barrier()
        a_row = lambda c0, n: self.srow(32, S + c0, n)
        b_row = lambda c0, n: self.srow(0, c0, n)
        if need_mean:
            k1 = self.Ve(lambda v, o=self.srow(0, 0, S), i=self.srow(0, 0, S):
                         v.tensor_scalar_mul(o, i, 1.0 / D), inc=True)
            k2 = self.Ve(lambda v, o=self.srow(0, S, S), a=self.srow(0, 0, S),
                         b=self.srow(0, 0, S): v.tensor_mul(o, a, b),
                         deps=[k1], inc=True, sw=True)
            k3 = self.Ve(lambda v, o=self.srow(32, S, S), i=self.srow(0, S, S):
                         v.tensor_copy(o, i), deps=[k2], inc=True, sw=True)
            self.Ve(lambda v, o=self.srow(32, 0, S), a=self.srow(32, 0, S),
                    b=self.srow(32, S, S):
                    v.scalar_tensor_tensor(o, a, 1.0 / D, b, ALU.mult, ALU.subtract),
                    deps=[k3], inc=True, sw=True)
        else:
            self.Ve(lambda v, o=self.srow(32, 0, S), i=self.srow(32, 0, S):
                    v.tensor_scalar_mul(o, i, 1.0 / D), inc=True)
        sv = self.p.tok("V")
        l1 = self.Sc(lambda s, o=self.srow(32, S, S), i=self.srow(32, 0, S),
                     e=self.sb["pp"][32:33, eps_col:eps_col + 1]:
                     s.activation(o, i, AF.Ln, bias=e), deps=[sv], inc=True)
        self.Sc(lambda s, o=self.srow(32, S, S), i=self.srow(32, S, S):
                s.activation(o, i, AF.Exp, scale=-0.5), deps=[l1], inc=True, sw=True)
        ss = self.p.tok("S")
        if need_mean:
            k4 = self.Ve(lambda v, o=self.srow(0, S, S), i=self.srow(32, S, S):
                         v.tensor_copy(o, i), deps=[ss], inc=True)
            self.Ve(lambda v, o=self.srow(0, 0, S), a=self.srow(0, 0, S),
                    b=self.srow(0, S, S):
                    v.scalar_tensor_tensor(o, a, -1.0, b, ALU.mult, ALU.mult),
                    deps=[k4], inc=True, sw=True)
        self.p.barrier()
        uprev = None
        for qc in range(4):
            vdep = self.p.tok("V")
            ta = self.T(lambda t, o=pb[7][:, 0:512], l=self.sb["onesf"][32:33, :],
                        r=a_row(qc * 512, 512): t.matmul(o, l, r, start=True, stop=True),
                        deps=[vdep], inc=True)
            tb = None
            if need_mean:
                tb = self.T(lambda t, o=pb[4][:, 0:512], l=self.sb["onesf"][0:1, :],
                            r=b_row(qc * 512, 512): t.matmul(o, l, r, start=True, stop=True),
                            deps=[vdep], inc=True)
            for c in range(NC):
                src = (self.fm(src_f32, c, qc * 512, 512) if src_f32 is not None
                       else self.fm(src_b, c, qc * 512, 512))
                if kind == "plain":
                    u1 = self.Ve(lambda v, o=self.sb["scr"].bitcast(F32)[:, 3088:3600], a=src, b=pb[7][:, 0:512]:
                                 v.tensor_mul(o, a, b), deps=[ta, uprev], inc=True)
                    uprev = self.Ve(lambda v, o=self.fm(dst_slot, c, qc * 512, 512),
                                    a=self.sb["scr"].bitcast(F32)[:, 3088:3600], b=pb[4][:, 0:512]:
                                    v.tensor_add(o, a, b), deps=[tb, u1], inc=True)
                elif kind == "affine":
                    u1 = self.Ve(lambda v, o=self.sb["scr"].bitcast(F32)[:, 3088:3600], a=src, b=pb[7][:, 0:512]:
                                 v.tensor_mul(o, a, b), deps=[ta, uprev], inc=True)
                    u2 = self.Ve(lambda v, o=self.sb["scr"].bitcast(F32)[:, 3088:3600],
                                 a=self.sb["scr"].bitcast(F32)[:, 3088:3600], b=pb[4][:, 0:512]:
                                 v.tensor_add(o, a, b), deps=[tb, u1], inc=True)
                    uprev = self.Ve(lambda v, o=self.fm(dst_slot, c, qc * 512, 512),
                                    a=self.sb["scr"].bitcast(F32)[:, 3088:3600], wc=self.pprow(w_col + c),
                                    bc=self.pprow(b_col + c):
                                    v.tensor_scalar(o, a, wc, bc, ALU.mult, ALU.add),
                                    deps=[u2], inc=True)
                else:
                    u1 = self.Ve(lambda v, o=self.sb["scr"].bitcast(F32)[:, 3088:3600], a=src, b=pb[7][:, 0:512]:
                                 v.tensor_mul(o, a, b), deps=[ta, uprev], inc=True)
                    uprev = self.Ve(lambda v, o=self.fm(dst_slot, c, qc * 512, 512),
                                    a=self.sb["scr"].bitcast(F32)[:, 3088:3600], wc=self.pprow(w_col + c):
                                    v.tensor_scalar_mul(o, a, wc), deps=[u1], inc=True)
        self.p.barrier()

    def attention(self, inp_slot, mkq, mkk, mkv, mkwo, scale, causal=False,
                  pos=False, bq_col=None, bk_col=None, vbias_row=None,
                  bo_col=None, out_mode="acc", out_slot=None, res_slot=None):
        mk = self
        pb = self.pb
        pool = Pool(self)
        self.sc_free = [None, None, None]
        self.probs_free = [None, None]
        self.pv_free = [None, None]
        self.sum_free = [None, None]
        self.prj_tok = {}
        self.bnd_v_tok = None
        self._pend_epi = None
        for pr in range(HPAIRS):
            sq, tq = pool.load(mkq(pr), NC, 128)
            sk, tk_ = pool.load(mkk(pr), NC, 128)
            sdep = self.p.tok("S")
            for which, slot_off, wsl, wtok, bcol in (
                    (0, SCR_Q, sq, tq, bq_col), (1, SCR_K, sk, tk_, bk_col)):
                for qc in range(4):
                    bi = which * 2 + (qc % 2)
                    bank = pb[bi]
                    dep_prev = self.prj_tok.get(bi)
                    bdep = self.sc_free[bi] if bi < 3 else self.bnd_v_tok
                    for kc in range(NC):
                        self.T(lambda t, o=bank[:, 0:512],
                               l=pool.tile(wsl, kc, 128),
                               r=self.fm(inp_slot, kc, qc * 512, 512), st=(kc == 0), sp=(kc == NC - 1):
                               t.matmul(o, l, r, start=st, stop=sp),
                               deps=[wtok, dep_prev, bdep, self.bnd_v_tok],
                               inc=(kc == NC - 1))
                    ttok = self.p.tok("T")
                    dst = self.sb["scr"][:, slot_off + qc * 512: slot_off + qc * 512 + 512]
                    if pos:
                        self.Ve(lambda v, o=dst, a=bank[:, 0:512],
                                b=self.pos2v[:, qc * 512:qc * 512 + 512]:
                                v.tensor_add(o, a, b), deps=[ttok], inc=True)
                    elif bcol is not None:
                        self.Ve(lambda v, o=dst, a=bank[:, 0:512],
                                bc=self.pprow(bcol + pr):
                                v.tensor_scalar_add(o, a, bc), deps=[ttok], inc=True)
                    else:
                        self.Ve(lambda v, o=dst, a=bank[:, 0:512]:
                                v.tensor_copy(o, a), deps=[ttok], inc=True)
                    self.prj_tok[bi] = self.p.tok("V")
            pool.done(sq, self.p.tok("T"))
            pool.done(sk, self.p.tok("T"))
            sv, tv = pool.load(mkv(pr), NC, 128)
            if pr == 0:
                ov = self.sb["scr"][:, SCR_V:SCR_V + 2080].rearrange(
                    "p (t c) -> p t c", c=65)[:, :, 64:65]
                self.Ve(lambda v, o=ov: v.memset(o, 1.0),
                        deps=[self.p.tok("T")], inc=True)
            cells = self.sb["cells"]
            sq_jobs = [(0, SCR_Q, qc) for qc in range(4)] + [(1, SCR_K, qc) for qc in range(4)]
            sq_buf_tok = [None, None]
            cs_toks = []

            def emit_square(i):
                which, slot_off, qc = sq_jobs[i]
                buf = i % 2
                sqv = self.sb["scr"][:, SCR_PROBS + buf * 512:SCR_PROBS + buf * 512 + 512]
                t = self.Sc(lambda s, o=sqv,
                            i_=self.sb["scr"][:, slot_off + qc * 512:slot_off + qc * 512 + 512]:
                            s.activation(o, i_, AF.Square),
                            deps=[self.prj_tok.get(which * 2 + qc % 2), sdep,
                                  self.probs_free[0], self.probs_free[1],
                                  sq_buf_tok[buf]], inc=True)
                return t, sqv

            sq_emitted = [emit_square(0), emit_square(1)]

            def emit_colsum(i):
                t, sqv = sq_emitted[i]
                which, slot_off, qc = sq_jobs[i]
                p0 = 32 * which
                self.T(lambda tt, o=pb[qc][p0:p0 + 1, 0:512], l=self.sb["onesb"][:, 0:1],
                       r=sqv, tp=(0, p0): tt.matmul(o, l, r, start=True, stop=True,
                                                    tile_position=tp),
                       deps=[t, self.sc_free[qc % 3] if qc < 3 else self.sum_free[1]],
                       inc=True)
                cs_toks.append(self.p.tok("T"))
                sq_buf_tok[i % 2] = cs_toks[-1]
                if i + 2 < len(sq_jobs):
                    sq_emitted.append(emit_square(i + 2))

            for tt in range(16):
                if tt % 2 == 0 and tt // 2 < 8:
                    emit_colsum(tt // 2)
                bank = pb[4 + (tt % 2)]
                vdep = self.pv_free[tt % 2]
                last_plain = (vbias_row is None)
                for kc in range(NC):
                    self.T(lambda t, o=bank[:, 0:128],
                           l=self.fm(inp_slot, kc, tt * 128, 128),
                           r=pool.tile(sv, kc, 128), st=(kc == 0),
                           sp=(last_plain and kc == NC - 1):
                           t.matmul(o, l, r, start=st, stop=sp),
                           deps=[tv, vdep],
                           inc=(last_plain and kc == NC - 1))
                if vbias_row is not None:
                    self.T(lambda t, o=bank[:, 0:128], l=self.sb["onesf"][0:1, :],
                           r=vbias_row(pr): t.matmul(o, l, r, start=False, stop=True),
                           inc=True)
                ttok = self.p.tok("T")
                self.Ve(lambda v, o=self.sb["scr"][:, SCR_V + tt * 130: SCR_V + tt * 130 + 64],
                        a=bank[:, 0:64]: v.tensor_copy(o, a), deps=[ttok], inc=True)
                self.Ve(lambda v, o=self.sb["scr"][:, SCR_V + tt * 130 + 65: SCR_V + tt * 130 + 129],
                        a=bank[:, 64:128]: v.tensor_copy(o, a), inc=True)
                self.pv_free[tt % 2] = self.p.tok("V")
            pool.done(sv, self.p.tok("T"))
            for which in (0, 1):
                p0 = 32 * which
                mtk = None
                for qc in range(4):
                    rt = self.Ve(lambda v, o=cells[p0:p0 + 1, 8 + qc:9 + qc],
                                 i=pb[qc][p0:p0 + 1, 0:512]:
                                 v.reduce_max(o, i, AX.X),
                                 deps=[cs_toks[which * 4 + qc], mtk], inc=True, sw=True)
                    mtk = rt
                m1t = self.Ve(lambda v, o=cells[p0:p0 + 1, 0:1], a=cells[p0:p0 + 1, 8:9],
                              b=cells[p0:p0 + 1, 9:10]: v.tensor_max(o, a, b),
                              deps=[mtk], inc=True, sw=True)
                m2t = self.Ve(lambda v, o=cells[p0:p0 + 1, 12:13], a=cells[p0:p0 + 1, 10:11],
                              b=cells[p0:p0 + 1, 11:12]: v.tensor_max(o, a, b),
                              deps=[m1t], inc=True, sw=True)
                self.Ve(lambda v, o=cells[p0:p0 + 1, 0:1], a=cells[p0:p0 + 1, 0:1],
                        b=cells[p0:p0 + 1, 12:13]: v.tensor_max(o, a, b),
                        deps=[m2t], inc=True, sw=True)
            kc1 = self.Ve(lambda v, o=cells[0:1, 1:2], i=cells[32:33, 0:1]:
                          v.tensor_copy(o, i), deps=[self.p.tok("V")], inc=True, sw=True)
            s1 = self.Ve(lambda v, o=cells[0:1, 4:5], a=cells[0:1, 0:1],
                         b=cells[0:1, 1:2]: v.tensor_add(o, a, b),
                         deps=[kc1], inc=True, sw=True)
            self.Ve(lambda v, o=self.sb["stats"].bitcast(BF16)[0:8, 2560:4608][0:1, 0:1], i=cells[0:1, 4:5]:
                    v.tensor_scalar_mul(o, i, -0.5 * scale), deps=[s1], inc=True, sw=True)
            vb = self.p.tok("V")
            tb = self.T(lambda t, o=pb[6][:, 256:257], l=self.sb["onesb"][0:1, :],
                        r=self.sb["stats"].bitcast(BF16)[0:8, 2560:4608][0:1, 0:1]:
                        t.matmul(o, l, r, start=True, stop=True),
                        deps=[vb, self.sum_free[0]], inc=True)
            self.Ve(lambda v, o=self.sb["negB"][:, 0:1], i=pb[6][:, 256:257]:
                    v.tensor_copy(o, i), deps=[tb, sdep], inc=True)
            self.bnd_v_tok = self.p.tok("V")
            self.probs_free[0] = cs_toks[-1]
            self.probs_free[1] = cs_toks[-1]
            negB_tok = self.p.tok("V")
            if "bnd" in self.dump:
                self.p.barrier()
                self.Q(lambda q, o=self.dump["bnd"][:, pr * 4:pr * 4 + 1],
                       s=self.sb["negB"][:, 0:1]:
                       q.dma_start(out=o, in_=s, allow_slow_non_contiguous=True),
                       inc=True, dma=True)
                self.Q(lambda q, o=self.dump["bnd"][0:1, 32 + pr * 8:32 + pr * 8 + 5],
                       s=self.sb["cells"][0:1, 0:5]:
                       q.dma_start(out=o, in_=s, allow_slow_non_contiguous=True),
                       inc=True, dma=True)
                self.p.barrier()
            for hh, h0 in ((0, 0), (1, 64)):
                for qb in range(4):
                    pvb_i = qb % 2
                    pvbank = pb[4 + pvb_i]
                    par = (4 * hh + qb) % 2
                    sumbank = pb[6] if par == 0 else pb[3]
                    kmax = (4 * qb + 4) if causal else 16
                    kcs = list(range(kmax))
                    strips = [kcs[i:i + 2] for i in range(0, kmax, 2)]
                    nst = len(strips)
                    exp_toks = {}

                    def emit_se(st_i, h0=h0, qb=qb):
                        strip = strips[st_i]
                        buf = st_i % 2
                        pbase = SCR_PROBS + buf * 1024
                        toks = []
                        for j, kc in enumerate(strip):
                            scb = pb[kc % 3]
                            self.T(lambda t, o=scb[:, 0:512],
                                   l=self.sb["scr"][h0:h0 + 64, SCR_K + kc * 128:SCR_K + kc * 128 + 128],
                                   r=self.sb["scr"][h0:h0 + 64, SCR_Q + qb * 512:SCR_Q + qb * 512 + 512]:
                                   t.matmul(o, l, r, start=True, stop=True),
                                   deps=[negB_tok, self.sc_free[kc % 3]], inc=True)
                            sctok = self.p.tok("T")
                            ptile = self.sb["scr"][:, pbase + j * 512: pbase + j * 512 + 512]
                            diag = causal and (kc >= 4 * qb)
                            et = self.Sc(lambda s, o=ptile, i=scb[:, 0:512],
                                         b=self.sb["negB"][:, 0:1], sc=scale:
                                         s.activation(o, i, AF.Exp, bias=b, scale=sc),
                                         deps=[sctok, self.probs_free[buf]],
                                         inc=True)
                            self.sc_free[kc % 3] = self.p.tok("S")
                            if diag:
                                gt = self.G(lambda g, o=ptile, base=512 * qb - 128 * kc:
                                            g.affine_select(o, o, [[1, 512]], ALU.is_ge,
                                                            0.0, base=base,
                                                            channel_multiplier=-1),
                                            deps=[et], inc=True)
                                toks.append(gt)
                            else:
                                toks.append(et)
                        exp_toks[st_i] = toks

                    def emit_cspv(st_i, h0=h0, qb=qb, pvbank=pvbank, sumbank=sumbank,
                                  par=par, pvb_i=pvb_i, nst_=None):
                        strip = strips[st_i]
                        buf = st_i % 2
                        pbase = SCR_PROBS + buf * 1024
                        dep_all = list(exp_toks[st_i])
                        for j, kc in enumerate(strip):
                            ptile = self.sb["scr"][:, pbase + j * 512: pbase + j * 512 + 512]
                            first = (st_i == 0 and j == 0)
                            last = (st_i == nst - 1 and j == len(strip) - 1)
                            self.T(lambda t, o=pvbank[0:65, 0:512],
                                   l=self.sb["scr"][:, SCR_V + kc * 130 + hh * 65: SCR_V + kc * 130 + hh * 65 + 65],
                                   r=ptile, st=first, sp=last:
                                   t.matmul(o, l, r, start=st, stop=sp),
                                   deps=dep_all + [self.pv_free[pvb_i]], inc=True)
                            dep_all = []
                        self.probs_free[buf] = self.p.tok("T")

                    emit_se(0)
                    if self._pend_epi is not None:
                        self._pend_epi()
                        self._pend_epi = None
                    for st_i in range(nst):
                        if st_i + 1 < nst:
                            emit_se(st_i + 1)
                        emit_cspv(st_i)
                    tk = self.p.tok("T")

                    def epilogue(tk=tk, sumbank=sumbank, h0=h0, qb=qb, pr=pr,
                                 par=par, pvbank=pvbank, pvb_i=pvb_i):
                        e1 = self.Ve(lambda v, o=self.srow(0, 0, 512), i=pvbank[64:65, 0:512]:
                                     v.tensor_copy(o, i), deps=[tk], inc=True)
                        rc = self.Ve(lambda v, o=self.srow(0, 0, 512), i=self.srow(0, 0, 512):
                                     v.reciprocal(o, i), deps=[e1], inc=True)
                        self.Ve(lambda v, o=self.sb["stats"].bitcast(BF16)[0:8, 2560:4608][0:1, 64:576], i=self.srow(0, 0, 512):
                                v.tensor_copy(o, i), deps=[rc], inc=True)
                        rv = self.p.tok("V")
                        tbc = self.T(lambda t, o=sumbank[0:64, 0:512],
                                     l=self.sb["onesb"][0:1, 0:64],
                                     r=self.sb["stats"].bitcast(BF16)[0:8, 2560:4608][0:1, 64:576]:
                                     t.matmul(o, l, r, start=True, stop=True),
                                     deps=[rv], inc=True)
                        rb = self.Ve(lambda v, o=self.sb["stats"][0:64, 512:1024],
                                     i=sumbank[0:64, 0:512]:
                                     v.tensor_copy(o, i), deps=[tbc], inc=True)
                        self.sum_free[par] = self.p.tok("V")
                        if h0 == 0:
                            self.Ve(lambda v, o=self.fm("ctx", pr, qb * 512, 512, 0, 64),
                                    a=pvbank[0:64, 0:512],
                                    b=self.sb["stats"][0:64, 512:1024]:
                                    v.tensor_mul(o, a, b), deps=[rb], inc=True)
                        else:
                            sg = self.Ve(lambda v, o=self.sb["stats"][0:64, 2304:2816],
                                         a=pvbank[0:64, 0:512],
                                         b=self.sb["stats"][0:64, 512:1024]:
                                         v.tensor_mul(o, a, b), deps=[rb], inc=True)
                            self.Ve(lambda v, o=self.fm("ctx", pr, qb * 512, 512, 64, 64),
                                    i=self.sb["stats"][0:64, 2304:2816]:
                                    v.tensor_copy(o, i), deps=[sg], inc=True)
                        self.pv_free[pvb_i] = self.p.tok("V")

                    self._pend_epi = epilogue
            if self._pend_epi is not None:
                self._pend_epi()
                self._pend_epi = None
        self.dump_slot("ctx", "ctx")
        self.dump_slot("accpre", "x_acc")
        self.p.barrier()
        pool = Pool(self)
        wo_tok = {}
        for mc in range(NC):
            sw, tw = pool.load(mkwo(mc), NC, 128)
            for qc in range(4):
                bi = qc % 2
                bank = pb[bi]
                vdep = wo_tok.get(bi)
                for kc in range(NC):
                    self.T(lambda t, o=bank[:, 0:512], l=pool.tile(sw, kc, 128),
                           r=self.fm("ctx", kc, qc * 512, 512), st=(kc == 0), sp=(kc == NC - 1):
                           t.matmul(o, l, r, start=st, stop=sp),
                           deps=[tw, vdep], inc=(kc == NC - 1))
                ttok = self.p.tok("T")
                xa = self.fm("x_acc", mc, qc * 512, 512)
                if out_mode == "acc":
                    if bo_col is not None:
                        self.Ve(lambda v, o=xa, a=bank[:, 0:512], bc=self.pprow(bo_col + mc), x=xa:
                                v.scalar_tensor_tensor(o, a, bc, x, ALU.add, ALU.add),
                                deps=[ttok], inc=True)
                    else:
                        self.Ve(lambda v, o=xa, a=bank[:, 0:512], x=xa:
                                v.tensor_add(o, a, x), deps=[ttok], inc=True)
                else:
                    oslot = self.fm(out_slot, mc, qc * 512, 512)
                    r1 = self.Ve(lambda v, o=oslot, a=bank[:, 0:512],
                                 b=self.fm(res_slot, mc, qc * 512, 512):
                                 v.tensor_add(o, a, b), deps=[ttok], inc=True)
                    self.Ve(lambda v, o=xa, a=oslot, x=xa:
                            v.tensor_add(o, a, x), deps=[r1], inc=True)
                wo_tok[bi] = self.p.tok("V")
            pool.done(sw, self.p.tok("T"))
        self.p.barrier()

    def moe_l0(self, h2_slot, sub=99):
        mk = self
        pb = self.pb
        pool = Pool(self)
        self.gel_tok = {0: None, 1: None, 2: None}
        self.ye_tok = {3: None, 4: None, 5: None, 6: None}
        sgw, tgw = pool.load(self.w["L0_ffn_gate_w"].rearrange("(c p) n -> p c n", p=128),
                             NC, 8)
        for qc in range(4):
            vdep = self.p.tok("V")
            for kc in range(NC):
                self.T(lambda t, o=pb[6][0:8, 0:512], l=pool.tile(sgw, kc, 8),
                       r=self.fm(h2_slot, kc, qc * 512, 512), st=(kc == 0), sp=(kc == NC - 1):
                       t.matmul(o, l, r, start=st, stop=sp),
                       deps=[tgw, vdep], inc=(kc == NC - 1))
            tk = self.p.tok("T")
            self.Ve(lambda v, o=self.sb["stats"].bitcast(BF16)[0:8, 2560:4608][0:8, qc * 512:qc * 512 + 512],
                    a=pb[6][0:8, 0:512], bc=self.pprow(80, 0, 8):
                    v.tensor_scalar_add(o, a, bc), deps=[tk], inc=True)
        self.p.barrier()
        if sub < 1:
            return
        for tt in range(16):
            t1 = self.T(lambda t, o=self.pb_bf7[:, 0:8],
                        i=self.sb["stats"].bitcast(BF16)[0:8, 2560:4608][0:8, tt * 128:(tt + 1) * 128],
                        d=self.sb["idbf"][0:8, 0:8]: t.transpose(o, i, d),
                        deps=[self.p.tok("V")], inc=True)
            a1 = self.Ve(lambda v, o=self.sb["stats"][:, 1024:1032], a=self.pb_bf7[:, 0:8],
                         b=self.sb["pp"][:, 83:91]: v.tensor_add(o, a, b), deps=[t1],
                         inc=True)
            a2 = self.Ve(lambda v, o=self.sb["cells"][:, 5:6], i=self.sb["stats"][:, 1024:1032]:
                         v.reduce_max(o, i, AX.X), deps=[a1], inc=True, sw=True)
            self.Ve(lambda v, o=self.sb["mtok"][:, 0:8], i=self.sb["stats"][:, 1024:1032],
                    sc=self.sb["cells"][:, 5:6]:
                    v.tensor_single_scalar(o, i, sc, ALU.is_equal), deps=[a2],
                    inc=True, sw=True)
            v1 = self.p.tok("V")
            t2 = self.T(lambda t, o=self.pb_bf7[0:8, 128:256], i=self.sb["mtok"][:, 0:8],
                        d=self.sb["idbf"][:, :]: t.transpose(o, i, d), deps=[v1], inc=True)
            self.Ve(lambda v, o=self.sb["stats"].bitcast(BF16)[0:8, 2560:4608][0:8, tt * 128:(tt + 1) * 128],
                    i=self.pb_bf7[0:8, 128:256]: v.tensor_copy(o, i), deps=[t2], inc=True)
        self.p.barrier()
        if sub < 2:
            return
        for e in range(min(sub - 1, 8)):
            sel_tok = self.make_sel(e)
            w1r = self.w["L0_ffn_w1"].rearrange("e (c p) n -> e p c n", p=128)
            w2r = self.w["L0_ffn_w2"].rearrange("e (c p) n -> e p c n", p=128)
            for tb in range(2):
                heb_free = self.p.tok("T")
                for colh in range(2):
                    col = tb * 1024 + colh * 512
                    tbc = self.T(lambda t, o=pb[7][:, 0:512],
                                 l=self.sb["stats"].bitcast(BF16)[0:8, 2432:2560][0:8, 0:128],
                                 r=self.sb["stats"].bitcast(BF16)[0:8, 2560:4608][0:8, col:col + 512]:
                                 t.matmul(o, l, r, start=True, stop=True),
                                 deps=[self.p.tok("V"), sel_tok], inc=True)
                    for kc in range(NC):
                        self.Ve(lambda v,
                                o=self.sb["scr"][:, SCR_MH2 + kc * 1024 + colh * 512:
                                                 SCR_MH2 + kc * 1024 + colh * 512 + 512],
                                a=self.fm(h2_slot, kc, col, 512), b=pb[7][:, 0:512]:
                                v.tensor_mul(o, a, b), deps=[tbc],
                                inc=(kc == NC - 1))
                mh2_tok = self.p.tok("V")
                if sub == 2:
                    continue
                w1br = self.wb["L0_ffn_w1"].rearrange("e (c p) n -> e p c n", p=128)
                for hp in range(16):
                    if tb == 0:
                        sw, tw = pool.load(w1r[e, :, :, hp * 256:(hp + 1) * 256], NC, 256,
                                           stash=w1br[e, :, :, hp * 256:(hp + 1) * 256])
                    else:
                        sw, tw = pool.load_b(w1br[e, :, :, hp * 256:(hp + 1) * 256], NC, 256)
                    for hsub in range(2):
                        hid = hp * 2 + hsub
                        for colh in range(2):
                            bi = (2 * hsub + colh) % 3
                            bank = pb[bi]
                            gel_dep = self.gel_tok[bi]
                            for kc in range(NC):
                                self.T(lambda t, o=bank[:, 0:512],
                                       l=pool.tile(sw, kc, 256)[:, hsub * 128:(hsub + 1) * 128],
                                       r=self.sb["scr"][:, SCR_MH2 + kc * 1024 + colh * 512:
                                                        SCR_MH2 + kc * 1024 + colh * 512 + 512],
                                       st=(kc == 0), sp=(kc == NC - 1):
                                       t.matmul(o, l, r, start=st, stop=sp),
                                       deps=[tw, mh2_tok, gel_dep], inc=(kc == NC - 1))
                            tk = self.p.tok("T")
                            hb = self.heb_view(hid, colh)
                            self.Sc(lambda s, o=hb, i=bank[:, 0:512]:
                                    s.activation(o, i, AF.Gelu_apprx_tanh),
                                    deps=[tk, heb_free], inc=True)
                            self.gel_tok[bi] = self.p.tok("S")
                    pool.done(sw, self.p.tok("T"))
                heb_tok = self.p.tok("S")
                if sub == 3:
                    continue
                w2br = self.wb["L0_ffn_w2"].rearrange("e (c p) n -> e p c n", p=128)
                for dout in range(NC):
                    for hh in range(2):
                        if tb == 0:
                            sw, tw = pool.load(
                                w2r[e, :, hh * 16:(hh + 1) * 16, dout * 128:(dout + 1) * 128],
                                16, 128,
                                stash=w2br[e, :, hh * 16:(hh + 1) * 16, dout * 128:(dout + 1) * 128])
                        else:
                            sw, tw = pool.load_b(
                                w2br[e, :, hh * 16:(hh + 1) * 16, dout * 128:(dout + 1) * 128],
                                16, 128)
                        for colh in range(2):
                            bi = 3 + (2 * dout + colh) % 4
                            bank = pb[bi]
                            for hc in range(16):
                                hid = hh * 16 + hc
                                self.T(lambda t, o=bank[:, 0:512],
                                       l=pool.tile(sw, hc, 128),
                                       r=self.heb_view(hid, colh), st=(hc == 0 and hh == 0), sp=(hc == 15 and hh == 1):
                                       t.matmul(o, l, r, start=st, stop=sp),
                                       deps=[tw, heb_tok, self.ye_tok[bi]],
                                       inc=(hc == 15))
                        pool.done(sw, self.p.tok("T"))
                    for colh in range(2):
                        bi = 3 + (2 * dout + colh) % 4
                        bank = pb[bi]
                        tk = self.p.tok("T")
                        xa = self.fm("x_acc", dout, tb * 1024 + colh * 512, 512)
                        self.Ve(lambda v, o=xa, a=bank[:, 0:512], x=xa:
                                v.tensor_add(o, a, x), deps=[tk], inc=True)
                        self.ye_tok[bi] = self.p.tok("V")
        self.p.barrier()

    def heb_view(self, hid, colh):
        slot = "ctx" if hid < 16 else "xn"
        base = (hid % 16) * 1024 + colh * 512
        return self.sb[slot][:, base:base + 512]

    def moe_l1(self, h2_slot):
        mk = self
        pb = self.pb
        pool = Pool(self)
        self.sil_tok = {0: None, 1: None}
        self.mex_tok = {0: None, 1: None}
        self.ye_tok = {4: None, 5: None, 6: None}
        srw, trw = pool.load(self.w["L1_ffn_router_w"].rearrange("(c p) n -> p c n", p=128),
                             NC, 8)
        for qc in range(4):
            vdep = self.p.tok("V")
            for kc in range(NC):
                self.T(lambda t, o=pb[6][0:8, 0:512], l=pool.tile(srw, kc, 8),
                       r=self.fm(h2_slot, kc, qc * 512, 512), st=(kc == 0), sp=(kc == NC - 1):
                       t.matmul(o, l, r, start=st, stop=sp),
                       deps=[trw, vdep], inc=(kc == NC - 1))
            tk = self.p.tok("T")
            self.Ve(lambda v, o=self.sb["stats"].bitcast(BF16)[0:8, 2560:4608][0:8, qc * 512:qc * 512 + 512],
                    a=pb[6][0:8, 0:512]: v.tensor_copy(o, a), deps=[tk], inc=True)
        self.p.barrier()
        cells = self.sb["cells"]
        for tt in range(16):
            t1 = self.T(lambda t, o=self.pb_bf7[:, 0:8],
                        i=self.sb["stats"].bitcast(BF16)[0:8, 2560:4608][0:8, tt * 128:(tt + 1) * 128],
                        d=self.sb["idbf"][0:8, 0:8]: t.transpose(o, i, d),
                        deps=[self.p.tok("V")], inc=True)
            lt = self.sb["stats"][:, 1024:1032]
            c1 = self.Ve(lambda v, o=lt, i=self.pb_bf7[:, 0:8]: v.tensor_copy(o, i),
                         deps=[t1], inc=True)
            c2 = self.Ve(lambda v, o=cells[:, 6:7], i=lt:
                         v.reduce_max(o, i, AX.X, negate=True), deps=[c1], inc=True, sw=True)
            c3 = self.Sc(lambda s, o=self.sb["stats"][:, 1032:1040], i=lt, b=cells[:, 6:7]:
                         s.activation(o, i, AF.Exp, bias=b), deps=[c2], inc=True)
            et = self.sb["stats"][:, 1032:1040]
            c4 = self.Ve(lambda v, o=cells[:, 7:8], i=et: v.reduce_sum(o, i, AX.X),
                         deps=[c3], inc=True)
            c5 = self.Ve(lambda v, o=cells[:, 8:9], i=cells[:, 7:8]: v.reciprocal(o, i),
                         deps=[c4], inc=True, sw=True)
            probs = self.sb["stats"][:, 1040:1048]
            c6 = self.Ve(lambda v, o=probs, i=et, sc=cells[:, 8:9]:
                         v.tensor_single_scalar(o, i, sc, ALU.mult), deps=[c5],
                         inc=True, sw=True)
            ltb = self.sb["stats"][:, 1048:1056]
            c7 = self.Ve(lambda v, o=ltb, a=probs, b=self.sb["pp"][:, 83:91]:
                         v.tensor_add(o, a, b), deps=[c6], inc=True, sw=True)
            c8 = self.Ve(lambda v, o=cells[:, 9:10], i=ltb: v.reduce_max(o, i, AX.X),
                         deps=[c7], inc=True, sw=True)
            m1 = self.sb["stats"][:, 1056:1064]
            c9 = self.Ve(lambda v, o=m1, i=ltb, sc=cells[:, 9:10]:
                         v.tensor_single_scalar(o, i, sc, ALU.is_equal), deps=[c8],
                         inc=True, sw=True)
            ltb2 = self.sb["stats"][:, 1064:1072]
            c10 = self.Ve(lambda v, o=ltb2, a=m1, b=ltb:
                          v.scalar_tensor_tensor(o, a, -4.0, b, ALU.mult, ALU.add),
                          deps=[c9], inc=True, sw=True)
            c11 = self.Ve(lambda v, o=cells[:, 10:11], i=ltb2: v.reduce_max(o, i, AX.X),
                          deps=[c10], inc=True, sw=True)
            m2 = self.sb["stats"][:, 1072:1080]
            c12 = self.Ve(lambda v, o=m2, i=ltb2, sc=cells[:, 10:11]:
                          v.tensor_single_scalar(o, i, sc, ALU.is_equal), deps=[c11],
                          inc=True, sw=True)
            tmp8 = self.sb["stats"][:, 1080:1088]
            c13 = self.Ve(lambda v, o=tmp8, a=probs, b=m1: v.tensor_mul(o, a, b),
                          deps=[c12], inc=True, sw=True)
            c14 = self.Ve(lambda v, o=cells[:, 11:12], i=tmp8: v.reduce_sum(o, i, AX.X),
                          deps=[c13], inc=True, sw=True)
            c15 = self.Ve(lambda v, o=tmp8, a=probs, b=m2: v.tensor_mul(o, a, b),
                          deps=[c14], inc=True, sw=True)
            c16 = self.Ve(lambda v, o=cells[:, 12:13], i=tmp8: v.reduce_sum(o, i, AX.X),
                          deps=[c15], inc=True, sw=True)
            c17 = self.Ve(lambda v, o=cells[:, 13:14], a=cells[:, 11:12], b=cells[:, 12:13]:
                          v.tensor_sub(o, a, b), deps=[c16], inc=True, sw=True)
            c18 = self.Sc(lambda s, o=cells[:, 14:15], i=cells[:, 13:14]:
                          s.activation(o, i, AF.Sigmoid), deps=[c17], inc=True)
            c19 = self.Ve(lambda v, o=cells[:, 15:16], i=cells[:, 14:15]:
                          v.tensor_scalar(o, i, -1.0, 1.0, ALU.mult, ALU.add),
                          deps=[c18], inc=True)
            cm = self.sb["mtok"][:, 0:8]
            c20 = self.Ve(lambda v, o=cm, i=m1, sc=cells[:, 14:15]:
                          v.tensor_single_scalar(o, i, sc, ALU.mult), deps=[c19],
                          inc=True, sw=True)
            self.Ve(lambda v, o=cm, a=m2, sc=cells[:, 15:16], b=cm:
                    v.scalar_tensor_tensor(o, a, sc, b, ALU.mult, ALU.add),
                    deps=[c20], inc=True, sw=True)
            v1 = self.p.tok("V")
            t2 = self.T(lambda t, o=self.pb_bf7[0:8, 128:256], i=cm,
                        d=self.sb["idbf"][:, :]: t.transpose(o, i, d), deps=[v1], inc=True)
            self.Ve(lambda v, o=self.sb["stats"].bitcast(BF16)[0:8, 2560:4608][0:8, tt * 128:(tt + 1) * 128],
                    i=self.pb_bf7[0:8, 128:256]: v.tensor_copy(o, i), deps=[t2], inc=True)
        self.p.barrier()
        egr = self.w["L1_ffn_e_gate"].rearrange("e (c p) n -> e p c n", p=128)
        eur = self.w["L1_ffn_e_up"].rearrange("e (c p) n -> e p c n", p=128)
        edr = self.w["L1_ffn_e_down"].rearrange("e (c p) n -> e p c n", p=128)
        shg = self.w["L1_ffn_sh_gate"].rearrange("(c p) n -> p c n", p=128)
        shu = self.w["L1_ffn_sh_up"].rearrange("(c p) n -> p c n", p=128)
        shd = self.w["L1_ffn_sh_down"].rearrange("(c p) n -> p c n", p=128)
        jobs = [(e, tbh) for e in range(9) for tbh in range(2)]
        sel_toks = {}
        mup_toks = {}

        def build_mup(ji):
            e, tbh = jobs[ji]
            if e >= 8:
                mup_toks[ji] = None
                return
            if e not in sel_toks:
                sel_toks[e] = self.make_sel(e)
            mb = (ji % 2) * 8192
            for colh in range(2):
                col = tbh * 1024 + colh * 512
                tbc = self.T(lambda t, o=pb[7][:, 0:512],
                             l=self.sb["stats"].bitcast(BF16)[0:8, 2432:2560][0:8, 0:128],
                             r=self.sb["stats"].bitcast(BF16)[0:8, 2560:4608][0:8, col:col + 512]:
                             t.matmul(o, l, r, start=True, stop=True),
                             deps=[self.p.tok("V"), sel_toks[e]], inc=True)
                for kc in range(NC):
                    self.Ve(lambda v,
                            o=self.sb["xn"][:, mb + kc * 1024 + colh * 512:
                                            mb + kc * 1024 + colh * 512 + 512],
                            a=self.fm(h2_slot, kc, col, 512), b=pb[7][:, 0:512]:
                            v.tensor_mul(o, a, b), deps=[tbc],
                            inc=(kc == NC - 1))
            mup_toks[ji] = self.p.tok("V")

        build_mup(0)
        for ji, (e, tbh) in enumerate(jobs):
            if True:
                mup_tok = mup_toks[ji]
                mb = (ji % 2) * 8192
                egbr = self.wb["L1_ffn_e_gate"].rearrange("e (c p) n -> e p c n", p=128)
                eubr = self.wb["L1_ffn_e_up"].rearrange("e (c p) n -> e p c n", p=128)
                shgbr = self.wb["L1_ffn_sh_gate"].rearrange("(c p) n -> p c n", p=128)
                shubr = self.wb["L1_ffn_sh_up"].rearrange("(c p) n -> p c n", p=128)
                for ffp in range(8):
                    sl = slice(ffp * 256, (ffp + 1) * 256)
                    if tbh == 0:
                        if e < 8:
                            sg_, tg_ = pool.load(egr[e, :, :, sl], NC, 256,
                                                 stash=egbr[e, :, :, sl])
                            su_, tu_ = pool.load(eur[e, :, :, sl], NC, 256,
                                                 stash=eubr[e, :, :, sl])
                        else:
                            sg_, tg_ = pool.load(shg[:, :, sl], NC, 256,
                                                 stash=shgbr[:, :, sl])
                            su_, tu_ = pool.load(shu[:, :, sl], NC, 256,
                                                 stash=shubr[:, :, sl])
                    else:
                        if e < 8:
                            sg_, tg_ = pool.load_b(egbr[e, :, :, sl], NC, 256)
                            su_, tu_ = pool.load_b(eubr[e, :, :, sl], NC, 256)
                        else:
                            sg_, tg_ = pool.load_b(shgbr[:, :, sl], NC, 256)
                            su_, tu_ = pool.load_b(shubr[:, :, sl], NC, 256)
                    for sub in range(2):
                        ffc = ffp * 2 + sub
                        for colh in range(2):
                            gbank = pb[colh]
                            ubank = pb[2 + colh]
                            sil_dep = self.sil_tok[colh]
                            mex_dep = self.mex_tok[colh]
                            for kc in range(NC):
                                self.T(lambda t, o=gbank[:, 0:512],
                                       l=pool.tile(sg_, kc, 256)[:, sub * 128:(sub + 1) * 128],
                                       r=self.fm(h2_slot, kc, tbh * 1024 + colh * 512, 512),
                                       st=(kc == 0), sp=(kc == NC - 1):
                                       t.matmul(o, l, r, start=st, stop=sp),
                                       deps=[tg_, sil_dep], inc=(kc == NC - 1))
                            gt = self.p.tok("T")
                            for kc in range(NC):
                                if e < 8:
                                    r = self.sb["xn"][:, mb + kc * 1024 + colh * 512:
                                                      mb + kc * 1024 + colh * 512 + 512]
                                else:
                                    r = self.fm(h2_slot, kc, tbh * 1024 + colh * 512, 512)
                                self.T(lambda t, o=ubank[:, 0:512],
                                       l=pool.tile(su_, kc, 256)[:, sub * 128:(sub + 1) * 128],
                                       rr=r, st=(kc == 0), sp=(kc == NC - 1):
                                       t.matmul(o, l, rr, start=st, stop=sp),
                                       deps=[tu_, mup_tok, mex_dep], inc=(kc == NC - 1))
                            ut = self.p.tok("T")
                            sgt = self.sb["scr"][:, SCR_SG + colh * 512:SCR_SG + colh * 512 + 512]
                            self.Sc(lambda s, o=sgt, i=gbank[:, 0:512]:
                                    s.activation(o, i, AF.Silu),
                                    deps=[gt, self.mex_tok[colh]], inc=True)
                            self.sil_tok[colh] = self.p.tok("S")
                            st_ = self.sil_tok[colh]
                            self.Ve(lambda v, o=self.mexp_view(ffc, colh),
                                    a=ubank[:, 0:512], b=sgt:
                                    v.tensor_mul(o, a, b), deps=[ut, st_], inc=True)
                            self.mex_tok[colh] = self.p.tok("V")
                    pool.done(sg_, self.p.tok("T"))
                    pool.done(su_, self.p.tok("T"))
                mexp_tok = self.p.tok("V")
                if ji + 1 < len(jobs):
                    build_mup(ji + 1)
                edbr = self.wb["L1_ffn_e_down"].rearrange("e (c p) n -> e p c n", p=128)
                shdbr = self.wb["L1_ffn_sh_down"].rearrange("(c p) n -> p c n", p=128)
                for dout in range(NC):
                    dl = slice(dout * 128, (dout + 1) * 128)
                    if tbh == 0:
                        if e < 8:
                            sw, tw = pool.load(edr[e, :, :, dl], 16, 128,
                                               stash=edbr[e, :, :, dl])
                        else:
                            sw, tw = pool.load(shd[:, :, dl], 16, 128, stash=shdbr[:, :, dl])
                    else:
                        if e < 8:
                            sw, tw = pool.load_b(edbr[e, :, :, dl], 16, 128)
                        else:
                            sw, tw = pool.load_b(shdbr[:, :, dl], 16, 128)
                    for colh in range(2):
                        bi = 4 + (2 * dout + colh) % 3
                        bank = pb[bi]
                        for ffc in range(16):
                            self.T(lambda t, o=bank[:, 0:512], l=pool.tile(sw, ffc, 128),
                                   r=self.mexp_view(ffc, colh), st=(ffc == 0), sp=(ffc == 15):
                                   t.matmul(o, l, r, start=st, stop=sp),
                                   deps=[tw, mexp_tok, self.ye_tok[bi]], inc=(ffc == 15))
                        tk = self.p.tok("T")
                        xa = self.fm("x_acc", dout, tbh * 1024 + colh * 512, 512)
                        self.Ve(lambda v, o=xa, a=bank[:, 0:512], x=xa:
                                v.tensor_add(o, a, x), deps=[tk], inc=True)
                        self.ye_tok[bi] = self.p.tok("V")
                    pool.done(sw, self.p.tok("T"))
        self.p.barrier()

    def mexp_view(self, ffc, colh):
        return self.sb["ctx"][:, ffc * 1024 + colh * 512: ffc * 1024 + colh * 512 + 512]


    def layer0(self, upto=99):
        w = self.w
        self.layernorm(None, "xn", "plain", src_f32="x_acc")
        self.dump_slot("xn0", "xn")
        if "stats" in self.dump:
            self.p.barrier()
            self.Q(lambda q, o=self.dump["stats"], s=self.sb["stats"][:, :]:
                   q.dma_start(out=o, in_=s), inc=True, dma=True)
            self.p.barrier()
        if upto < 2:
            return
        wq = lambda p: w["L0_attn_Wq"].rearrange("(c p) n -> p c n", p=128)[:, :, p * 128:(p + 1) * 128]
        wk = lambda p: w["L0_attn_Wk"].rearrange("(c p) n -> p c n", p=128)[:, :, p * 128:(p + 1) * 128]
        wv = lambda p: w["L0_attn_Wv"].rearrange("(c p) n -> p c n", p=128)[:, :, p * 128:(p + 1) * 128]
        wo = lambda m: w["L0_attn_Wo"].rearrange("(c p) n -> p c n", p=128)[:, :, m * 128:(m + 1) * 128]
        self.attention("xn", wq, wk, wv, wo, scale=D ** -0.5, causal=True, pos=True,
                       out_mode="acc")
        self.dump_slot("acc_a0", "x_acc")
        if upto < 3:
            return
        self.layernorm("xn", "hA", "affine", w_col=0, b_col=8, eps_col=81)
        self.dump_slot("h1_l0", "hA")
        if upto < 4:
            return
        qkv = w["L0_ffn_w_qkv"].rearrange("(c p) n -> p c n", p=128)
        mq = lambda p: qkv[:, :, p * 128:(p + 1) * 128]
        mkk = lambda p: qkv[:, :, 1024 + p * 128: 1024 + (p + 1) * 128]
        mv = lambda p: qkv[:, :, 2048 + p * 128: 2048 + (p + 1) * 128]
        mwo = lambda m: w["L0_ffn_w_proj"].rearrange("(c p) n -> p c n", p=128)[:, :, m * 128:(m + 1) * 128]
        self.attention("hA", mq, mkk, mv, mwo, scale=0.125, causal=False, pos=False,
                       out_mode="res", out_slot="xn", res_slot="xn")
        self.dump_slot("y1", "xn")
        if upto < 5:
            return
        self.layernorm("xn", "hA", "affine", w_col=16, b_col=24, eps_col=81)
        self.dump_slot("h2_l0", "hA")
        if upto < 6:
            return
        self.moe_l0("hA", sub=(upto - 5 if upto < 99 else 99))
        self.dump_slot("x1", "x_acc")

    def layer1(self, upto=99):
        w = self.w
        self.layernorm(None, "xn", "plain", src_f32="x_acc")
        self.dump_slot("l1_xn", "xn")
        if upto < 2:
            return
        qkv = w["L1_attn_Wqkv"].rearrange("(c p) n -> p c n", p=128)
        mq = lambda p: qkv[:, :, p * 128:(p + 1) * 128]
        mkk = lambda p: qkv[:, :, 1024 + p * 128: 1024 + (p + 1) * 128]
        mv = lambda p: qkv[:, :, 2048 + p * 128: 2048 + (p + 1) * 128]
        mwo = lambda m: w["L1_attn_Wproj"].rearrange("(c p) n -> p c n", p=128)[:, :, m * 128:(m + 1) * 128]
        vb = lambda p: self.srow(0, 3072 + p * 128, 128)
        self.Q(lambda q, o=self.srow(0, 3072, 1024), i=self.w["L1_attn_bqkv"][None, 2048:3072]:
               q.dma_start(out=o, in_=i), inc=True, dma=True)
        self.p.barrier()
        self.attention("xn", mq, mkk, mv, mwo, scale=0.5, causal=False, pos=False,
                       bq_col=48, bk_col=56, vbias_row=vb, bo_col=72, out_mode="acc")
        self.dump_slot("l1_acc", "x_acc")
        if upto < 3:
            return
        self.layernorm("xn", "hA", "rms", w_col=32, eps_col=82)
        self.dump_slot("l1_h", "hA")
        if upto < 4:
            return
        fq = lambda p: w["L1_ffn_wq"].rearrange("(c p) n -> p c n", p=128)[:, :, p * 128:(p + 1) * 128]
        fk = lambda p: w["L1_ffn_wk"].rearrange("(c p) n -> p c n", p=128)[:, :, p * 128:(p + 1) * 128]
        fv = lambda p: w["L1_ffn_wv"].rearrange("(c p) n -> p c n", p=128)[:, :, p * 128:(p + 1) * 128]
        fwo = lambda m: w["L1_ffn_wo"].rearrange("(c p) n -> p c n", p=128)[:, :, m * 128:(m + 1) * 128]
        self.attention("hA", fq, fk, fv, fwo, scale=0.125, causal=False, pos=False,
                       out_mode="res", out_slot="xn", res_slot="xn")
        self.dump_slot("l1_x2", "xn")
        if upto < 5:
            return
        self.layernorm("xn", "hA", "rms", w_col=40, eps_col=82)
        self.dump_slot("l1_h2", "hA")
        if upto < 6:
            return
        self.moe_l1("hA")


def build_nc(layers=(0, 1), dump=(), upto=99):
    nc = bass.Bass("TRN2", target_bir_lowering=False,
                   dynamic_dma_scratch_size=16384)
    prog = Prog()
    w = {}
    shapes = {
        "L0_attn_Wq": [D, D], "L0_attn_Wk": [D, D], "L0_attn_Wv": [D, D],
        "L0_attn_Wo": [D, D], "L0_ffn_w_qkv": [D, 3 * D], "L0_ffn_w_proj": [D, D],
        "L0_ffn_gate_w": [D, 8], "L0_ffn_w1": [8, D, 4 * D], "L0_ffn_w2": [8, 4 * D, D],
        "L1_attn_Wqkv": [D, 3 * D], "L1_attn_bqkv": [3 * D], "L1_attn_Wproj": [D, D],
        "L1_ffn_wq": [D, D], "L1_ffn_wk": [D, D], "L1_ffn_wv": [D, D],
        "L1_ffn_wo": [D, D], "L1_ffn_router_w": [D, 8],
        "L1_ffn_sh_gate": [D, 2048], "L1_ffn_sh_up": [D, 2048],
        "L1_ffn_sh_down": [2048, D],
        "L1_ffn_e_gate": [8, D, 2048], "L1_ffn_e_up": [8, D, 2048],
        "L1_ffn_e_down": [8, 2048, D],
    }
    need = set()
    if 0 in layers:
        need |= {k for k in shapes if k.startswith("L0")}
    if 1 in layers:
        need |= {k for k in shapes if k.startswith("L1")}
    for k in sorted(need):
        w[k] = nc.dram_tensor(k, shapes[k], F32, kind="ExternalInput").ap()
    xin = nc.dram_tensor("xT", [D, S], F32, kind="ExternalInput").ap()
    wb = {}
    for k, shp in (("L0_ffn_w1", [8, D, 4 * D]), ("L0_ffn_w2", [8, 4 * D, D]),
                   ("L1_ffn_e_gate", [8, D, 2048]), ("L1_ffn_e_up", [8, D, 2048]),
                   ("L1_ffn_e_down", [8, 2048, D]), ("L1_ffn_sh_gate", [D, 2048]),
                   ("L1_ffn_sh_up", [D, 2048]), ("L1_ffn_sh_down", [2048, D])):
        if (0 in layers and k.startswith("L0")) or (1 in layers and k.startswith("L1")):
            wb[k] = nc.dram_tensor(k + "_b16", shp, BF16, kind="Internal").ap()
    pos2_in = nc.dram_tensor("pos2c", [128, S], F32, kind="ExternalInput").ap()
    pp_in = nc.dram_tensor("ppc", [128, 96], F32, kind="ExternalInput").ap()
    w["onesrc"] = nc.dram_tensor("onesrc", [1, 128], F32, kind="ExternalInput").ap()
    out = nc.dram_tensor("out", [D, S], F32, kind="ExternalOutput").ap()
    dump_t = {}
    for name in dump:
        if name == "bnd":
            dump_t[name] = nc.dram_tensor("DUMP_bnd", [128, 96], F32,
                                          kind="ExternalOutput").ap()
            continue
        if name == "qkv0":
            dump_t[name] = nc.dram_tensor("DUMP_qkv0", [128, 8192], BF16,
                                          kind="ExternalOutput").ap()
            continue
        if name == "stats":
            dump_t[name] = nc.dram_tensor("DUMP_stats", [128, 4096], F32,
                                          kind="ExternalOutput").ap()
            continue
        dt = F32 if name in ("acc_a0", "x1", "accpre", "l1_acc") else BF16
        dump_t[name] = nc.dram_tensor("DUMP_" + name, [128, 16384], dt,
                                      kind="ExternalOutput").ap()

    import contextlib
    with contextlib.ExitStack() as st:
        sb = {}
        def sbt(name, shape, dtype):
            sb[name] = st.enter_context(nc.sbuf_tensor(name, shape, dtype))
        sbt("x_acc", [128, 16384], F32)
        sbt("xn", [128, 16384], BF16)
        sbt("hA", [128, 16384], BF16)
        sbt("ctx", [128, 16384], BF16)
        sbt("scr", [128, SCR_COLS], BF16)
        sbt("wpan", [128, WPAN_SLOT * WPAN_NSLOT], BF16)
        sbt("stats", [128, 4096], F32)
        sbt("idbf", [128, 128], BF16)
        sbt("onesf", [64, 128], F32)
        sbt("pp", [128, 96], F32)
        sbt("onesb", [128, 128], BF16)
        sbt("negB", [128, 1], F32)
        sbt("cells", [128, 16], F32)
        sbt("mtok", [128, 8], BF16)
        sbt("cb", [128, 8], BF16)
        pb = [st.enter_context(nc.psum_tensor(f"pb{i}", [128, 512], F32))
              for i in range(8)]
        sems = {e: st.enter_context(nc.semaphore(f"sem{e}")) for e in Prog.SEMS}

        mk = MK(nc, prog, sb, pb, w, dump_t)
        mk.wb = wb
        mk.pb_bf7 = pb[7].bitcast(BF16)
        mk.pb_bf6 = pb[6].bitcast(BF16)
        mk.pos2v = sb["hA"].bitcast(F32)[:, 0:S]
        mk.Q(lambda q, o=sb["x_acc"][:, :].rearrange("p (c t) -> p c t", c=NC),
             i=xin.rearrange("(c p) t -> p c t", p=128): q.dma_start(out=o, in_=i),
             inc=True, dma=True)
        mk.Q(lambda q, o=mk.pos2v, i=pos2_in: q.dma_start(out=o, in_=i),
             inc=True, dma=True)
        mk.Q(lambda q, o=sb["pp"][:, :], i=pp_in: q.dma_start(out=o, in_=i),
             inc=True, dma=True)
        mk.G(lambda g: g.memset(sb["onesb"][:, :], 1.0))
        mk.G(lambda g: g.memset(sb["onesf"][:, :], 1.0))
        g1 = mk.G(lambda g: g.memset(sb["idbf"][:, :], 1.0), inc=True)
        mk.p.emit("G", lambda g: g.affine_select(sb["idbf"][:, :], sb["idbf"][:, :],
                                                 [[1, 128]], ALU.is_equal, 0.0,
                                                 base=0, channel_multiplier=-1),
                  deps=[g1], inc=True, self_wait=True)
        prog.barrier()
        if 0 in layers:
            mk.layer0(upto)
        if 1 in layers:
            mk.layer1(upto if layers == (1,) else 99)
        prog.barrier()
        mk.Q(lambda q, o=out.rearrange("(c p) t -> p c t", p=128),
             i=sb["x_acc"][:, :].rearrange("p (c t) -> p c t", c=NC):
             q.dma_start(out=o, in_=i), inc=True, dma=True)
        mk.Q(lambda q: q.nop(), deps=[prog.tok("Q")], self_wait=True)

        with nc.Block() as block:
            def runner(engname):
                def body(eng):
                    ops = prog.ops[engname]
                    sem_map = sems
                    for fn, waits, incn, semname in ops:
                        for (sname, v) in waits:
                            eng.wait_ge(sem_map[sname], v)
                        inst = fn(eng)
                        if incn:
                            inst.then_inc(sem_map[semname], incn)
                return body
            block.tensor(runner("T"))
            block.scalar(runner("S"))
            block.vector(runner("V"))
            block.gpsimd(runner("G"))
            block.sync(runner("Q"))
    return nc



def _host_consts():
    idx = np.arange(S, dtype=np.float64)
    dim = 64
    eps = float(np.finfo(np.float32).eps)
    scale = 1.0 / (10000.0 ** (np.arange(0, dim, 2, dtype=np.float64) /
                               np.float64(dim)) + eps)
    idx32 = np.arange(S, dtype=np.float32)
    sc32 = (1.0 / (10000.0 ** (np.arange(0, dim, 2, dtype=np.float32) / np.float32(dim))
                   + np.float32(eps))).astype(np.float32)
    ang = idx32[:, None] * sc32[None, :]
    pos = np.stack([np.sin(ang), np.cos(ang)], axis=-1).reshape(S, dim).astype(np.float32)
    pos2 = np.concatenate([pos.T, pos.T], axis=0).astype(np.float32)
    return pos2


def _pack_pp(inputs):
    pp = np.zeros((128, 96), np.float32)
    def put(col, vec):
        v = np.asarray(vec, np.float32).reshape(-1)
        n = v.size // 128
        pp[:, col:col + n] = v.reshape(n, 128).T
    put(0, inputs["L0_ffn_ln1_w"]); put(8, inputs["L0_ffn_ln1_b"])
    put(16, inputs["L0_ffn_ln2_w"]); put(24, inputs["L0_ffn_ln2_b"])
    put(32, inputs["L1_ffn_norm1_w"]); put(40, inputs["L1_ffn_norm2_w"])
    put(48, inputs["L1_attn_bqkv"]); put(72, inputs["L1_attn_bproj"])
    pp[0:8, 80] = np.asarray(inputs["L0_ffn_gate_b"], np.float32)
    pp[:, 81] = 1e-5
    pp[:, 82] = 1e-6
    for e in range(8):
        pp[:, 83 + e] = -e * 1e-6
    return pp


_WEIGHT_KEYS = [
    "L0_attn_Wq", "L0_attn_Wk", "L0_attn_Wv", "L0_attn_Wo",
    "L0_ffn_w_qkv", "L0_ffn_w_proj", "L0_ffn_gate_w", "L0_ffn_w1", "L0_ffn_w2",
    "L1_attn_Wqkv", "L1_attn_bqkv", "L1_attn_Wproj",
    "L1_ffn_wq", "L1_ffn_wk", "L1_ffn_wv", "L1_ffn_wo", "L1_ffn_router_w",
    "L1_ffn_sh_gate", "L1_ffn_sh_up", "L1_ffn_sh_down",
    "L1_ffn_e_gate", "L1_ffn_e_up", "L1_ffn_e_down",
]


def kernel(**inputs):
    x = np.asarray(inputs["L0_attn_x"], np.float32)
    pos2 = _host_consts()
    pp = _pack_pp(inputs)
    warr = {k: np.ascontiguousarray(np.asarray(inputs[k], np.float32))
            for k in _WEIGHT_KEYS}
    xTs = [np.ascontiguousarray(x[i].T) for i in range(NCORES)]
    consts = {"pos2c": pos2, "ppc": pp,
              "onesrc": np.ones((1, 128), np.float32)}

    if FUSE:
        nc = build_nc(layers=(0, 1))
        in_maps = []
        for i in range(NCORES):
            m = dict(consts)
            m.update(warr)
            m["xT"] = xTs[i]
            in_maps.append(m)
        res = run_bass_kernel_spmd(nc, in_maps, core_ids=list(range(NCORES)))
        outs = [res.results[i]["out"] for i in range(NCORES)]
    else:
        nc0 = build_nc(layers=(0,))
        in_maps = []
        w0 = {k: v for k, v in warr.items() if k.startswith("L0")}
        for i in range(NCORES):
            m = dict(consts)
            m.update(w0)
            m["xT"] = xTs[i]
            in_maps.append(m)
        res0 = run_bass_kernel_spmd(nc0, in_maps, core_ids=list(range(NCORES)))
        mids = [np.asarray(res0.results[i]["out"]) for i in range(NCORES)]
        nc1 = build_nc(layers=(1,))
        w1 = {k: v for k, v in warr.items() if k.startswith("L1")}
        in_maps = []
        for i in range(NCORES):
            m = dict(consts)
            m.update(w1)
            m["xT"] = mids[i]
            in_maps.append(m)
        res = run_bass_kernel_spmd(nc1, in_maps, core_ids=list(range(NCORES)))
        outs = [res.results[i]["out"] for i in range(NCORES)]

    return np.stack([np.asarray(o).T for o in outs], axis=0).astype(np.float32)
